# version 11 (final)
# baseline (speedup 1.0000x reference)
import contextlib
import math
import numpy as np
import ml_dtypes

import concourse.bass as bass
import concourse.mybir as mybir

F32 = mybir.dt.float32
BF16 = mybir.dt.bfloat16
U8 = mybir.dt.uint8
AF = mybir.ActivationFunctionType
ALU = mybir.AluOpType
AX = mybir.AxisListType
BF = ml_dtypes.bfloat16

ENGS = ("tensor", "vector", "scalar", "gpsimd", "sync")

class Prog:
    def __init__(self, nc):
        self.nc = nc
        self.actions = {e: [] for e in ENGS}
        self.semnames = []
        self.sems = {}
        self.count = {}
        self.watermark = {e: {} for e in ENGS}

    def sem(self, name):
        if name not in self.count:
            self.semnames.append(name)
            self.count[name] = 0
        return name

    def _bump(self, sem, amt):
        self.sem(sem)
        self.count[sem] += amt
        return (sem, self.count[sem])

    def wait(self, eng, tok):
        if tok is None:
            return
        sem, val = tok
        if self.watermark[eng].get(sem, 0) >= val:
            return
        self.watermark[eng][sem] = val
        self.actions[eng].append(("wait", sem, val))

    def op(self, eng, fn, deps=(), sig=None, sig_amt=1):
        for d in deps:
            self.wait(eng, d)
        tok = None
        if sig is not None:
            tok = self._bump(sig, sig_amt)
            self.actions[eng].append(("op_sig", fn, tok[0], sig_amt))
        else:
            self.actions[eng].append(("op", fn))
        return tok

    def dma(self, eng, out, in_, deps=(), sig=None, **kw):
        return self.op(eng, lambda e: e.dma_start(out=out, in_=in_, **kw),
                       deps=deps, sig=sig, sig_amt=16)

    def cc(self, fn, deps=(), sig=None):
        for d in deps:
            self.wait("gpsimd", d)
        tok = self._bump(sig, 1)
        self.actions["gpsimd"].append(("op_sig_def", fn, tok[0]))
        return tok

    def phase_barrier(self):
        snap = [(s, v) for s, v in self.count.items() if v > 0]
        for eng in ENGS:
            for s, v in snap:
                self.wait(eng, (s, v))

    def build(self):
        nc = self.nc
        with contextlib.ExitStack() as st:
            for name in self.semnames:
                self.sems[name] = st.enter_context(nc.semaphore(name))
            with nc.Block() as block:
                for eng in ENGS:
                    acts = self.actions[eng]
                    if not acts:
                        continue
                    def runner(e, acts=acts):
                        for a in acts:
                            if a[0] == "wait":
                                e.wait_ge(self.sems[a[1]], a[2])
                            elif a[0] == "op":
                                a[1](e)
                            elif a[0] == "op_sig_def":
                                a[1](e).then_inc(self.sems[a[2]])
                            else:
                                a[1](e).then_inc(self.sems[a[2]], a[3])
                    getattr(block, eng)(runner)
        return nc


class Arena:
    def __init__(self, nc, nbytes):
        self.nc = nc
        self.nbytes = nbytes
        self.t = None
        self.off = 0
        self.marks = {}

    def alloc(self, pbytes, dtype, free_elems, name=""):
        pbytes = (pbytes + 63) & ~63
        self.region_off = getattr(self, "region_off", {})
        self.region_off[name] = self.off
        assert self.off + pbytes <= self.nbytes, \
            f"arena overflow: {name} off={self.off} + {pbytes} > {self.nbytes}"
        esz = mybir.dt.size(dtype)
        assert free_elems * esz <= pbytes
        ap = self.t[:, self.off:self.off + free_elems * esz].bitcast(dtype)
        self.off += pbytes
        return ap

    def view(self, off, dtype, free_elems):
        esz = mybir.dt.size(dtype)
        return self.t[:, off:off + free_elems * esz].bitcast(dtype)

    def mark(self, tag):
        self.marks[tag] = self.off

    def reset(self, tag):
        self.off = self.marks[tag]

    def reset_zero(self):
        self.off = 0


TOK = 1024
HALO = 512
TOT = TOK + HALO
NT = TOK // 128
NTT = TOT // 128
D = 2048
DC = D // 128
H0, DH0, W0 = 8, 128, 512
WIN = 640
F1 = 8192
FC = F1 // 128
EPS = 1e-5


def ceil(a, b):
    return -(-a // b)


PI = math.pi


class L1Build:

    def __init__(self, with_l1=True, dbg=False, stop_after=None, fused=False):
        nc = bass.Bass("TRN2", target_bir_lowering=False)
        self.nc = nc
        self.p = Prog(nc)
        self.with_l1 = with_l1
        self.dbg = dbg
        self.fused = fused
        self.stop_after = stop_after
        d = {}
        def di(name, shape, dt=F32):
            d[name] = nc.dram_tensor(name, shape, dt, kind="ExternalInput").ap()
        def do(name, shape, dt=F32):
            kind = "Internal" if fused else "ExternalOutput"
            d[name] = nc.dram_tensor(name, shape, dt, kind=kind).ap()
        di("xT", [D, TOT]); di("masks", [5, 128, WIN])
        di("WqT", [D, 1024]); di("WkvT", [D, 2048]); di("WgT", [D, H0])
        di("bg", [H0]); di("qsc", [128]); di("ksc", [128]); di("WoT", [1024, D])
        di("p1T", [FC // 2, 1, 128, DC, 256]); di("b1", [F1])
        di("p2T", [DC, 2, 128, 32, 128]); di("b2", [D])
        do("x1", [D, TOK])
        if with_l1:
            for wn in ("Wq1", "Wk1", "Wv1"):
                di(wn, [DC, 1, 128, DC, 128])
            di("ampw", [DC, 1, 128, DC, 128]); di("ampb", [D])
            di("phw", [DC, 1, 128, DC, 128]); di("phb", [D])
            di("entwT0", [DC, 2, 128, 32, 128]); di("entwT1", [DC, 2, 128, 32, 128])
            di("entpw0", [F1, 256]); di("entpw1", [F1, 256])
            di("entb", [2, F1]); di("entpb", [2, 256])
            do("q1", [D, TOK], BF16); do("kvp", [128, 1056])
            if fused:
                do("weffp", [2 * D, 256], BF16)
            else:
                do("weffp", [2, D, 256], BF16)
            do("beffp", [2, 256]); do("h0", [D, TOK])
        if fused:
            d["kvsum"] = nc.dram_tensor("kvsum", [128, 1056], F32).ap()
            d["weffg"] = nc.dram_tensor("weffg", [8 * 2 * D, 256], BF16).ap()
            d["beffg"] = nc.dram_tensor("beffg", [16, 256], F32).ap()
        if dbg:
            do("d_xn", [128, DC * TOK], BF16)
            do("d_xnh", [128, DC * HALO], BF16)
            do("d_kfm", [128, H0 * TOT], BF16)
            do("d_vtm", [128, NTT * 1024], BF16)
            do("d_qfm", [128, H0 * TOK], BF16)
            do("d_gtm", [128, 64])
            do("d_ofm", [128, H0 * TOK], BF16)
            do("d_xattn", [D, TOK])
        self.d = d
        self.xattn = nc.dram_tensor("xattn", [D, TOK], F32).ap()
        self.out_toks = []

    def proj_std(self, whbm, kc, mc, rhs_fn, evac_fn, wb, pair, deps0=(),
                 dma_split=1, bias_row=None, bias_dep=None, mw=1, tiled=False):
        p = self.p
        ntok = rhs_fn(0).shape[-1]
        nseg = ceil(ntok, 512)
        assert nseg <= 2
        t_w = [None, None]
        t_cons = [None, None]
        t_ev = None
        kc_spl = kc // dma_split
        for mg in range(mc // mw):
            b = mg % 2
            dd = list(deps0) + ([t_cons[b]] if t_cons[b] else [])
            for s in range(dma_split):
                src = (whbm[mg, s] if tiled else
                       whbm[s * kc_spl * 128:(s + 1) * kc_spl * 128,
                            mg * mw * 128:(mg + 1) * mw * 128].rearrange(
                           "(c p) n -> p c n", p=128))
                t_w[b] = p.dma(
                    "gpsimd",
                    wb[b][:, s * kc_spl * 128 * mw:(s + 1) * kc_spl * 128 * mw
                          ].rearrange("p (c n) -> p c n", c=kc_spl),
                    src, deps=dd, sig=f"dgp{b}")
            for j in range(mw):
                m = mg * mw + j
                def mm(t, m=m, b=b, j=j):
                    last = None
                    for k in range(kc):
                        lhs = wb[b][:, (k * mw + j) * 128:(k * mw + j + 1) * 128]
                        for s in range(nseg):
                            n = min(512, ntok - s * 512)
                            last = t.matmul(pair[s if nseg == 2 else m % 2][:, 0:n],
                                            lhs, rhs_fn(k)[:, s * 512:s * 512 + n],
                                            start=(k == 0),
                                            stop=(k == kc - 1 and bias_row is None))
                    if bias_row is not None:
                        for s in range(nseg):
                            n = min(512, ntok - s * 512)
                            last = t.matmul(pair[s if nseg == 2 else m % 2][:, 0:n],
                                            bias_row[0:1, m * 128:(m + 1) * 128],
                                            self.CONB[0:1, 512:512 + n],
                                            start=False, stop=True)
                    return last
                dd = list(deps0) + [t_w[b]] + ([t_ev] if t_ev else [])
                if bias_dep is not None:
                    dd.append(bias_dep)
                t_mm = p.op("tensor", mm, deps=dd, sig="pe")
                t_cons[b] = t_mm
                t_ev = evac_fn(m, t_mm)
        return t_ev

    def build(self, st=None):
        if st is None:
            with contextlib.ExitStack() as st_:
                return self._build_in(st_, finalize=True)
        return self._build_in(st, finalize=False)

    def _build_in(self, st, finalize):
        nc, p, d = self.nc, self.p, self.d
        if True:
            ar = Arena(nc, 212600)
            ar.t = st.enter_context(nc.sbuf_tensor("AR", [128, 212600], U8))
            self.ar = ar
            P = []
            for i in range(8):
                dt = BF16 if i == 3 else F32
                n = 1024 if i == 3 else 512
                P.append(st.enter_context(nc.psum_tensor(f"P{i}", [128, n], dt)))
            self.P = P
            self.CON = ar.alloc(2048, F32, 512, "CON")
            self.CONB = ar.alloc(2048, BF16, 1024, "CONB")
            self.IDENT = ar.alloc(256, BF16, 128, "IDENT")
            self.MASKS = ar.alloc(5 * WIN * 2, BF16, 5 * WIN, "MASKS")
            self.SQ = ar.alloc(6144, F32, 1536, "SQ")
            self.ROWS = ar.alloc(4096, F32, 1024, "ROWS")
            self.STR = [ar.alloc(6144, F32, 1536, f"STR{i}") for i in range(2)]
            self.RQK = ar.alloc(256, F32, 64, "RQK")
            self.GTM = ar.alloc(256, F32, 64, "GTM")
            self.GROW = ar.alloc(2048, BF16, 1024, "GROW")
            self.SMC = ar.alloc(256, F32, 64, "SMC")
            self.XNO = ar.alloc(DC * TOK * 2, BF16, DC * TOK, "XNO")
            ar.mark("base")
            self.t_p3 = None
            self.t_rr_prev = None
            self.t_rqk_read = None
            self.t_tp_prev = None
            self._consts()
            ar.mark("blk")
            self.VTM = ar.alloc(NTT * 1024 * 2, BF16, NTT * 1024, "VTM")
            self.QFM = ar.alloc(H0 * TOK * 2, BF16, H0 * TOK, "QFM")
            self.KFM = ar.alloc(H0 * TOT * 2, BF16, H0 * TOT, "KFM")
            self.OFM = ar.alloc(H0 * TOK * 2, BF16, H0 * TOK, "OFM")
            self.QKT = ar.alloc(1024 * 2, BF16, 1024, "QKT")
            ar.mark("l0a")
            self.XNH = ar.alloc(DC * HALO * 2, BF16, DC * HALO, "XNH")
            self.WBUF = ar.alloc(16 * 1024 * 2, BF16, 16 * 1024, "WBUF")
            self._ln1()
            if self.stop_after == "A":
                ot = [p.dma("sync", d["d_xn"], self.XNO, deps=self.t_xn, sig="dso"),
                      p.dma("sync", d["d_xnh"], self.XNH, deps=self.t_xn, sig="dso")]
                self.out_toks += ot
                for tk in self.out_toks:
                    p.wait("sync", tk)
                p.build()
                return nc
            self._kv()
            self._q_gates()
            if self.dbg:
                ot = [p.dma("sync", d["d_xn"], self.XNO, deps=self.t_xn, sig="dso"),
                      p.dma("sync", d["d_xnh"], self.XNH, deps=self.t_xn, sig="dso"),
                      p.dma("sync", d["d_kfm"], self.KFM, deps=self.t_kv, sig="dso"),
                      p.dma("sync", d["d_vtm"], self.VTM, deps=self.t_kv, sig="dso"),
                      p.dma("sync", d["d_qfm"], self.QFM, deps=[self.t_q], sig="dso"),
                      p.dma("sync", d["d_gtm"], self.GTM[:, 0:64],
                            deps=[self.t_gates], sig="dso")]
                self.out_toks += ot
            if self.stop_after == "C":
                for tk in self.out_toks:
                    p.wait("sync", tk)
                p.build()
                return nc
            ar.reset("l0a")
            self.EXPT = ar.alloc(WIN * 4, F32, WIN, "EXPT")
            self.EXPE = ar.alloc(WIN * 4, F32, WIN, "EXPE")
            self.ATTB = ar.alloc(WIN * 2, BF16, WIN, "ATTB")
            self.ATTT = ar.alloc(WIN * 2, BF16, WIN, "ATTT")
            self.WOB = [ar.alloc(8 * 128 * 2, BF16, 8 * 128, f"WOB{i}") for i in range(2)]
            self.XSA = [ar.alloc(2048, F32, 512, f"XSA{i}") for i in range(4)]
            self._attention()
            if self.dbg:
                self.out_toks += [
                    p.dma("sync", d["d_ofm"], self.OFM,
                          deps=[self.t_av_done[qt] for qt in range(NT)], sig="dso"),
                    p.dma("sync", d["d_xattn"], self.xattn, deps=self.t_wo, sig="dso")]
            ar.reset("blk")
            self.H = ar.alloc(FC * 512 * 2, BF16, FC * 512, "H")
            self.W1B = [ar.alloc(16 * 256 * 2, BF16, 16 * 256, f"W1B{i}")
                        for i in range(2)]
            self.W2B = [ar.alloc(FC * 128 * 2, BF16, FC * 128, f"W2B{i}")
                        for i in range(3)]
            self.XS = [ar.alloc(2048, F32, 512, f"XS{i}") for i in range(4)]
            self.B1R = ar.view(ar.region_off["MASKS"], BF16, F1)
            self.B2R = ar.view(ar.region_off["STR0"], BF16, D)
            self._ffn()
            if self.with_l1:
                ar.reset("blk")
                self.ENTPW = ar.alloc(FC * 256 * 2, BF16, FC * 256, "ENTPW")
                self.WTP = [ar.alloc(FC * 128 * 2, BF16, FC * 128, f"WTP{i}")
                            for i in range(3)]
                self.WEFFST = ar.alloc(16 * 256 * 2, BF16, 16 * 256, "WEFFST")
                self.ENTB = ar.alloc(2 * 64 * 2, BF16, 2 * 64, "ENTBC")
                self.ENTBRAW = ar.alloc(256, BF16, 128, "ENTBRAW")
                self.EPB = ar.alloc(4096, F32, 1024, "EPB")
                self._ln2_weff()
                if self.fused:
                    self._issue_cc_weff()
                ar.reset("blk")
                self.Q1FM = ar.alloc(DC * TOK * 2, BF16, DC * TOK, "Q1FM")
                self.K1FM = ar.alloc(DC * TOK * 2, BF16, DC * TOK, "K1FM")
                self.V1FM = ar.alloc(DC * TOK * 2, BF16, DC * TOK, "V1FM")
                self.WP = [ar.alloc(DC * 128 * 2, BF16, DC * 128, f"WP{i}")
                           for i in range(2)]
                self.K1T = ar.alloc(D * 2, BF16, D, "K1T")
                self.V1T = ar.alloc(32 * 65 * 2, BF16, 32 * 65, "V1T")
                self.KVST = ar.alloc(1056 * 4, F32, 1056, "KVST")
                self._l1_proj()
                if self.fused:
                    self._issue_cc_kv()
                ar.reset("blk")
                self.TH = ar.alloc(DC * TOK * 4, F32, DC * TOK, "TH")
                self.AMP = ar.alloc(DC * TOK * 4, F32, DC * TOK, "AMP")
                self.WPK = [ar.alloc(DC * 128 * 2, BF16, DC * 128, f"WPK{i}")
                            for i in range(2)]
                self.SC = ar.view(ar.region_off["XNO"], F32, 2 * 1024)
                self._amp_ph()
            if not finalize:
                return nc
            for tk in self.out_toks:
                p.wait("sync", tk)
            p.build()
        return nc

    def _consts(self):
        p, d = self.p, self.d
        CON, CONB = self.CON, self.CONB
        t0 = p.op("vector", lambda v: v.memset(CON[:, 0:1], 1.0), sig="ve")
        self.t_ones = p.op("vector", lambda v: v.memset(CON[0:1, 64:192], 1.0), sig="ve")
        self.t_onesb = p.op("vector", lambda v: v.memset(CONB[:, :], 1.0), sig="ve")
        t_ms = p.op("gpsimd", lambda g: g.memset(self.IDENT[:, :], 0.0),
                    sig="ge")
        self.t_ident = p.op("gpsimd", lambda g: g.affine_select(
            out=self.IDENT[:, :], in_=self.IDENT[:, :], compare_op=ALU.not_equal,
            fill=1.0, base=0, pattern=[[-1, 128]], channel_multiplier=1),
            deps=[t_ms], sig="ge")
        self.t_masks = p.dma("gpsimd", self.MASKS.rearrange("p (s j) -> p s j", s=5),
                             d["masks"].rearrange("s p j -> p s j"), sig="dgm")
        tq = p.dma("gpsimd", CON[0:1, 256:384],
                   d["qsc"].rearrange("(one n) -> one n", one=1), sig="dgq")
        tk = p.dma("gpsimd", CON[0:1, 384:512],
                   d["ksc"].rearrange("(one n) -> one n", one=1), sig="dgq")
        tm = p.op("vector", lambda v: v.tensor_mul(CON[0:1, 256:384],
                                                   CON[0:1, 256:384],
                                                   CON[0:1, 384:512]),
                  deps=[tk], sig="ve")
        tm2 = p.op("vector", lambda v: v.tensor_scalar_mul(
            CON[0:1, 256:384], CON[0:1, 256:384], 8.0), deps=[tm], sig="ve")
        tmm = p.op("tensor", lambda t: t.matmul(
            self.P[2][:, 0:1], CON[0:1, 256:384], CON[0:1, 0:1],
            start=True, stop=True), deps=[tm2, t0], sig="pe")
        self.t_qks = p.op("scalar", lambda sc: sc.activation(
            CON[:, 1:2], self.P[2][:, 0:1], AF.Copy), deps=[tmm], sig="se")
        self.t_bg = p.dma("gpsimd", CONB[0:1, 264:272],
                          d["bg"].rearrange("(one n) -> one n", one=1),
                          deps=[self.t_onesb], sig="dgv")

    def _ln(self, src_fn, ngrp, out_fn, sums_p, sqs_p, tag, bc_pair=None,
            deps0=()):
        p, P = self.p, self.P
        ones = self.CON[:, 0:1]
        width = ngrp * 512
        t_dma = [None, None]
        t_pe = [None, None]
        t_sq_prev = None
        for c in range(DC):
            b = c % 2
            t_dma[b] = src_fn(c, b, 0, width,
                              [t_pe[b]] if t_pe[b] else [])
            t_sq = p.op("scalar", lambda s, b=b, width=width: s.activation(
                self.SQ[:, 0:width], self.STR[b][:, 0:width], AF.Square),
                deps=[t_dma[b]] + ([t_sq_prev] if t_sq_prev else []), sig="se")
            def mms(t, c=c, b=b):
                last = None
                for g in range(ngrp):
                    last = t.matmul(sums_p[g][0:1, :], ones,
                                    self.STR[b][:, g * 512:(g + 1) * 512],
                                    start=(c == 0), stop=(c == DC - 1))
                for g in range(ngrp):
                    last = t.matmul(sqs_p[g][0:1, :], ones,
                                    self.SQ[:, g * 512:(g + 1) * 512],
                                    start=(c == 0), stop=(c == DC - 1))
                return last
            t_pe[b] = p.op("tensor", mms,
                           deps=[t_dma[b], t_sq] + list(deps0), sig="pe")
            t_sq_prev = t_pe[b]
        t_stats = t_pe[(DC - 1) % 2]
        t_last = None
        t_bc_prev = None
        for g in range(ngrp):
            SU, SS = sums_p[g], sqs_p[g]
            t0 = p.op("vector", lambda v, SU=SU: v.tensor_scalar_mul(
                self.ROWS[0:1, 0:512], SU[0:1, :], 1.0 / D),
                deps=[t_stats] + ([t_bc_prev] if t_bc_prev else []), sig="ve")
            ta = p.op("vector", lambda v, SS=SS: v.tensor_scalar_mul(
                self.ROWS[0:1, 512:1024], SS[0:1, :], 1.0 / D), sig="ve")
            tb = p.op("vector", lambda v: v.tensor_mul(
                self.SQ[0:1, 0:512], self.ROWS[0:1, 0:512], self.ROWS[0:1, 0:512]),
                deps=[t0], sig="ve")
            tc = p.op("vector", lambda v: v.tensor_sub(
                self.ROWS[0:1, 512:1024], self.ROWS[0:1, 512:1024],
                self.SQ[0:1, 0:512]), deps=[ta, tb], sig="ve")
            t2 = p.op("vector", lambda v: v.tensor_scalar_add(
                self.ROWS[0:1, 512:1024], self.ROWS[0:1, 512:1024], EPS),
                deps=[tc], sig="ve")
            t3 = p.op("scalar", lambda s: s.activation(
                self.SQ[0:1, 0:512], self.ROWS[0:1, 512:1024], AF.Ln),
                deps=[t2], sig="se")
            t4 = p.op("scalar", lambda s: s.activation(
                self.ROWS[0:1, 512:1024], self.SQ[0:1, 0:512], AF.Exp, scale=-0.5),
                deps=[t3], sig="se")
            if hasattr(self, "RD"):
                p.op("vector", lambda v, g=g: v.tensor_copy(
                    self.RD[0:1, g * 1024:(g + 1) * 1024], self.ROWS[0:1, 0:1024]),
                    deps=[t4], sig="ve")
            BA, BB = SU, SS
            def bc(t, BA=BA, BB=BB):
                t.matmul(BA[:, :], self.CON[0:1, 64:192], self.ROWS[0:1, 0:512],
                         start=True, stop=True)
                return t.matmul(BB[:, :], self.CON[0:1, 64:192],
                                self.ROWS[0:1, 512:1024], start=True, stop=True)
            t_bc = p.op("tensor", bc, deps=[t0, t4, self.t_ones], sig="pe")
            t_bc_prev = t_bc
            t_ap = [None, None]
            for c in range(DC):
                b = c % 2
                t_x = src_fn(c, b, g * 512, 512,
                             ([t_ap[b]] if t_ap[b] else []) + [t_bc])
                t_sub = p.op("vector", lambda v, b=b, g=g, BA=BA: v.tensor_sub(
                    self.STR[b][:, 1024:1536], self.STR[b][:, g * 512:g * 512 + 512],
                    BA[:, :]), deps=[t_x, t_bc], sig="ve")
                t_ap[b] = p.op("vector", lambda v, b=b, BB=BB, c=c, g=g: v.tensor_mul(
                    out_fn(c, g), self.STR[b][:, 1024:1536], BB[:, :]),
                    deps=[t_sub], sig="ve")
            t_last = [t_ap[0], t_ap[1], t_bc]
        return t_last

    def _ln1(self):
        p, d, P = self.p, self.d, self.P

        def src(c, b, lo, n, deps):
            return p.dma("sync", self.STR[b][:, lo:lo + n],
                         d["xT"][c * 128:(c + 1) * 128, lo:lo + n],
                         deps=deps, sig=f"dsx{b}")

        def out(c, g):
            if g == 0:
                return self.XNH[:, c * HALO:c * HALO + 512]
            return self.XNO[:, c * TOK + (g - 1) * 512:c * TOK + g * 512]

        toks = self._ln(src, 3, out, [P[2], P[6], P[7]], [P[0], P[1], P[5]], "ln1",
                        bc_pair=(P[2], P[0]), deps0=[self.t_qks])
        self.t_xn = toks[:2]
        self.t_ln1_pe = toks[2]

    def xn_tile(self, c, m):
        if m < 4:
            return self.XNH[:, c * HALO + m * 128:c * HALO + (m + 1) * 128]
        return self.XNO[:, c * TOK + (m - 4) * 128:c * TOK + (m - 3) * 128]

    def _kv(self):
        p, d, P = self.p, self.d, self.P
        t_w = [None] * DC
        t_free = {}
        t_mm_last = None
        KFM3 = self.KFM.rearrange("p (h t) -> p h t", h=8)
        for half in range(2):
            for c in range(DC):
                dep = [t_mm_last] if (half == 1 and c == 0) else []
                t_w[c] = p.dma("gpsimd", self.WBUF[:, c * 1024:(c + 1) * 1024],
                               d["WkvT"][c * 128:(c + 1) * 128,
                                         half * 1024:(half + 1) * 1024],
                               deps=dep, sig="dgk")
            for m in range(NTT):
                u = half * NTT + m
                pa, pb = [(P[0], P[1]), (P[6], P[7]), (P[4], P[5])][u % 3]
                def mmk(t, m=m, pa=pa, pb=pb):
                    last = None
                    for c in range(DC):
                        lhs = self.xn_tile(c, m)
                        t.matmul(pa[:, :], lhs, self.WBUF[:, c * 1024:c * 1024 + 512],
                                 start=(c == 0), stop=(c == DC - 1))
                        last = t.matmul(pb[:, :], lhs,
                                        self.WBUF[:, c * 1024 + 512:(c + 1) * 1024],
                                        start=(c == 0), stop=(c == DC - 1))
                    return last
                deps = [t_w[c] for c in range(DC)] + self.t_xn
                if t_free.get(u - 3) is not None:
                    deps.append(t_free[u - 3])
                t_mm = p.op("tensor", mmk, deps=deps, sig="pe")
                t_mm_last = t_mm
                if half == 1:
                    tv1 = p.op("vector", lambda v, m=m, pa=pa: v.tensor_copy(
                        self.VTM[:, m * 1024:m * 1024 + 512], pa[:, :]), deps=[t_mm],
                        sig="ve")
                    t_free[u] = p.op("vector", lambda v, m=m, pb=pb: v.tensor_copy(
                        self.VTM[:, m * 1024 + 512:(m + 1) * 1024], pb[:, :]),
                        deps=[tv1], sig="ve")
                else:
                    t_s1 = p.op("scalar", lambda s, pa=pa: s.activation(
                        self.SQ[:, 0:512], pa[:, :], AF.Square),
                        deps=[t_mm] + ([self.t_rr_prev] if self.t_rr_prev else []),
                        sig="se")
                    t_s2 = p.op("scalar", lambda s, pb=pb: s.activation(
                        self.SQ[:, 512:1024], pb[:, :], AF.Square), sig="se")
                    t_rr = p.op("vector", lambda v: v.reduce_sum(
                        self.RQK[:, 0:8],
                        self.SQ[:, 0:1024].rearrange("p (h e) -> p h e", h=8),
                        axis=AX.X),
                        deps=[t_s1, t_s2] + ([self.t_rqk_read]
                                             if self.t_rqk_read else []), sig="ve")
                    t_cl = p.op("vector", lambda v: v.tensor_scalar_max(
                        self.RQK[:, 0:8], self.RQK[:, 0:8], 1e-24),
                        deps=[t_rr], sig="ve")
                    self.t_rr_prev = t_rr
                    t_l = p.op("scalar", lambda s: s.activation(
                        self.RQK[:, 8:16], self.RQK[:, 0:8], AF.Ln),
                        deps=[t_cl], sig="se")
                    t_e = p.op("scalar", lambda s: s.activation(
                        self.RQK[:, 0:8], self.RQK[:, 8:16], AF.Exp, scale=-0.5),
                        deps=[t_l], sig="se")
                    tk2 = None
                    for h in range(8):
                        ps = pa if h < 4 else pb
                        dd = [t_e] + ([self.t_tp_prev]
                                      if (h == 0 and self.t_tp_prev) else [])
                        tk2 = p.op("vector", lambda v, h=h, ps=ps: v.tensor_scalar_mul(
                            self.QKT[:, h * 128:(h + 1) * 128],
                            ps[:, (h % 4) * 128:(h % 4 + 1) * 128],
                            self.RQK[:, h:h + 1]), deps=dd,
                            sig="ve" if h == 7 else None)
                    t_free[u] = tk2
                    def tps(t, m=m):
                        last = None
                        for h in range(8):
                            last = t.transpose(P[3][:, h * 128:(h + 1) * 128],
                                               self.QKT[:, h * 128:(h + 1) * 128],
                                               self.IDENT[:, :])
                        return last
                    t_tp = p.op("tensor", tps,
                                deps=[tk2, self.t_ident, self.t_p3], sig="pe")
                    self.t_tp_prev = t_tp
                    self.t_p3 = p.op(
                        "vector", lambda v, m=m: v.tensor_scalar_mul(
                            KFM3[:, :, m * 128:(m + 1) * 128],
                            P[3][:, 0:1024].rearrange("p (h e) -> p h e", h=8),
                            self.CON[:, 1:2]),
                        deps=[t_tp, self.t_qks], sig="ve")
        self.t_kv = [self.t_p3, t_free[NTT * 2 - 1]]

    def _q_gates(self):
        p, d, P = self.p, self.d, self.P
        QFM3 = self.QFM.rearrange("p (h t) -> p h t", h=8)
        t_w = [None] * DC
        for c in range(DC):
            t_w[c] = p.dma("gpsimd", self.WBUF[:, c * 1024:c * 1024 + 1024],
                           d["WqT"][c * 128:(c + 1) * 128, 0:1024],
                           deps=[self.t_kv[1]] if c == 0 else [], sig="dgk")
        t_free = {}
        for m in range(NT):
            pa, pb = [(P[0], P[1]), (P[6], P[7]), (P[4], P[5])][m % 3]
            def mmq(t, m=m, pa=pa, pb=pb):
                last = None
                for c in range(DC):
                    lhs = self.xn_tile(c, m + 4)
                    t.matmul(pa[:, :], lhs, self.WBUF[:, c * 1024:c * 1024 + 512],
                             start=(c == 0), stop=(c == DC - 1))
                    last = t.matmul(pb[:, :], lhs,
                                    self.WBUF[:, c * 1024 + 512:c * 1024 + 1024],
                                    start=(c == 0), stop=(c == DC - 1))
                return last
            deps = [t_w[c] for c in range(DC)]
            if t_free.get(m - 3) is not None:
                deps.append(t_free[m - 3])
            t_mm = p.op("tensor", mmq, deps=deps, sig="pe")
            t_s1 = p.op("scalar", lambda s, pa=pa: s.activation(
                self.SQ[:, 0:512], pa[:, :], AF.Square),
                deps=[t_mm] + ([self.t_rr_prev] if self.t_rr_prev else []), sig="se")
            t_s2 = p.op("scalar", lambda s, pb=pb: s.activation(
                self.SQ[:, 512:1024], pb[:, :], AF.Square), sig="se")
            t_rr = p.op("vector", lambda v: v.reduce_sum(
                self.RQK[:, 0:8],
                self.SQ[:, 0:1024].rearrange("p (h e) -> p h e", h=8),
                axis=AX.X),
                deps=[t_s1, t_s2] + ([self.t_rqk_read]
                                     if self.t_rqk_read else []), sig="ve")
            t_cl = p.op("vector", lambda v: v.tensor_scalar_max(
                self.RQK[:, 0:8], self.RQK[:, 0:8], 1e-24), deps=[t_rr], sig="ve")
            self.t_rr_prev = t_rr
            t_l = p.op("scalar", lambda s: s.activation(
                self.RQK[:, 8:16], self.RQK[:, 0:8], AF.Ln), deps=[t_cl], sig="se")
            t_e = p.op("scalar", lambda s: s.activation(
                self.RQK[:, 0:8], self.RQK[:, 8:16], AF.Exp, scale=-0.5),
                deps=[t_l], sig="se")
            tq = None
            for h in range(8):
                ps = pa if h < 4 else pb
                dd = [t_e] + ([self.t_tp_prev] if (h == 0 and self.t_tp_prev) else [])
                tq = p.op("vector", lambda v, h=h, ps=ps: v.tensor_scalar_mul(
                    self.QKT[:, h * 128:(h + 1) * 128],
                    ps[:, (h % 4) * 128:(h % 4 + 1) * 128],
                    self.RQK[:, h:h + 1]), deps=dd, sig="ve" if h == 7 else None)
            t_free[m] = tq
            self.t_rqk_read = tq
            def tpq(t, m=m):
                last = None
                for h in range(8):
                    last = t.transpose(P[3][:, h * 128:(h + 1) * 128],
                                       self.QKT[:, h * 128:(h + 1) * 128],
                                       self.IDENT[:, :])
                return last
            t_tp = p.op("tensor", tpq, deps=[tq, self.t_p3], sig="pe")
            self.t_tp_prev = t_tp
            self.t_p3 = p.op("scalar", lambda s, m=m: s.activation(
                QFM3[:, :, m * 128:(m + 1) * 128],
                P[3][:, 0:1024].rearrange("p (h e) -> p h e", h=8), AF.Copy),
                deps=[t_tp], sig="se")
        self.t_q = self.t_p3
        WGT = self.CONB[:, 128:256].rearrange("p (c n) -> p c n", c=16)
        t_wg = p.dma("gpsimd", WGT, d["WgT"].rearrange("(c p) n -> p c n", p=128),
                     sig="dgg")
        t_gs = None
        for g in range(2):
            def mmg(t, g=g):
                for k in range(DC):
                    t.matmul(P[2][0:8, :], WGT[:, k, :],
                             self.XNO[:, k * TOK + g * 512:k * TOK + (g + 1) * 512],
                             start=(k == 0), stop=False)
                return t.matmul(P[2][0:8, :], self.CONB[0:1, 264:272],
                                self.CONB[0:1, 512:1024], start=False, stop=True)
            t_g = p.op("tensor", mmg,
                       deps=[t_wg, self.t_bg] + self.t_xn + ([t_gs] if t_gs else []),
                       sig="pe")
            t_gs = p.op("scalar", lambda s, g=g: s.activation(
                self.GROW[0:8, g * 512:(g + 1) * 512], P[2][0:8, :], AF.Sigmoid),
                deps=[t_g], sig="se")
        def tpg(t):
            last = None
            for qt in range(8):
                last = t.transpose(P[3][:, qt * 8:(qt + 1) * 8],
                                   self.GROW[0:8, qt * 128:(qt + 1) * 128],
                                   self.IDENT[0:8, 0:8])
            return last
        t_tpg = p.op("tensor", tpg, deps=[t_gs, self.t_p3], sig="pe")
        self.t_p3 = p.op("scalar", lambda s: s.activation(
            self.GTM[:, 0:64], P[3][:, 0:64], AF.Copy), deps=[t_tpg], sig="se")
        self.t_gates = self.t_p3

    def _attention(self):
        p, d, P = self.p, self.d, self.P
        t_simfree = None
        t_avfree = {}
        t_attb_free = None
        t_attt_free = None
        t_expe_free = None
        self.t_av_done = {}
        t_wo = {"tok": None}
        units = [(qt, h) for qt in range(NT) for h in range(H0)]
        wo_iter = [None] * len(units)
        for i in range(32, 48):
            wo_iter[i] = (0, i - 32)
        self.t_xattn = []
        t_wo_ev = {}
        for u, (qt, h) in enumerate(units):
            slot = qt if qt < 4 else 4
            qs = self.QFM.rearrange("p (hh t) -> p hh t", hh=8)[:, h,
                                                               qt * 128:(qt + 1) * 128]
            ks = self.KFM.rearrange("p (hh t) -> p hh t", hh=8)[:, h,
                                                               qt * 128:qt * 128 + WIN]
            def sim(t, qs=qs, ks=ks):
                t.matmul(P[0][:, :], qs, ks[:, 0:512], start=True, stop=True)
                return t.matmul(P[1][:, 0:128], qs, ks[:, 512:640],
                                start=True, stop=True)
            t_sim = p.op("tensor", sim,
                         deps=[self.t_q, self.t_kv[0], t_simfree], sig="pe")
            t_ma1 = p.op("vector", lambda v, slot=slot: v.tensor_add(
                self.EXPT[:, 0:512], P[0][:, :], self.MASKS.rearrange(
                    "p (s j) -> p s j", s=5)[:, slot, 0:512]),
                deps=[t_sim, self.t_masks, t_expe_free], sig="ve")
            t_simfree = p.op("vector", lambda v, slot=slot: v.tensor_add(
                self.EXPT[:, 512:640], P[1][:, 0:128], self.MASKS.rearrange(
                    "p (s j) -> p s j", s=5)[:, slot, 512:640]), sig="ve")
            t_mx = p.op("vector", lambda v: v.reduce_max(
                self.SMC[:, 0:1], self.EXPT[:, 0:WIN], axis=AX.X, negate=True),
                deps=[t_ma1, t_simfree], sig="ve")
            t_exp = p.op("scalar", lambda s: s.activation(
                self.EXPE[:, 0:WIN], self.EXPT[:, 0:WIN], AF.Exp,
                bias=self.SMC[:, 0:1], accum_out=self.SMC[:, 1:2]),
                deps=[t_mx], sig="se")
            t_rc = p.op("vector", lambda v: v.reciprocal(
                self.SMC[:, 2:3], self.SMC[:, 1:2]), deps=[t_exp], sig="ve")
            t_rg = p.op("vector", lambda v, qt=qt, h=h: v.tensor_mul(
                self.SMC[:, 3:4], self.SMC[:, 2:3],
                self.GTM[:, qt * 8 + h:qt * 8 + h + 1]),
                deps=[self.t_gates, t_rc], sig="ve")
            t_attb = p.op("vector", lambda v: v.tensor_scalar_mul(
                self.ATTB[:, 0:WIN], self.EXPE[:, 0:WIN], self.SMC[:, 3:4]),
                deps=[t_attb_free, t_rg], sig="ve")
            t_expe_free = t_attb
            def tpa(t):
                last = None
                for j in range(5):
                    last = t.transpose(P[3][:, j * 128:(j + 1) * 128],
                                       self.ATTB[:, j * 128:(j + 1) * 128],
                                       self.IDENT[:, :])
                return last
            t_tp = p.op("tensor", tpa, deps=[t_attb, self.t_p3], sig="pe")
            t_attb_free = t_tp
            self.t_p3 = p.op("scalar", lambda s: s.activation(
                self.ATTT[:, 0:WIN], P[3][:, 0:WIN], AF.Copy),
                deps=[t_tp, t_attt_free], sig="se")
            pav = P[2] if u % 2 == 0 else P[6]
            def av(t, qt=qt, h=h, pav=pav):
                last = None
                for j in range(5):
                    last = t.matmul(
                        pav[:, 0:128],
                        self.VTM[:, (qt + j) * 1024 + h * 128:(qt + j) * 1024
                                 + (h + 1) * 128],
                        self.ATTT[:, j * 128:(j + 1) * 128],
                        start=(j == 0), stop=(j == 4))
                return last
            t_av = p.op("tensor", av,
                        deps=[self.t_p3, self.t_kv[1], t_avfree.get(u - 2)],
                        sig="pe")
            t_attt_free = t_av
            t_ave = p.op("scalar", lambda s, qt=qt, h=h, pav=pav: s.activation(
                self.OFM[:, h * 1024 + qt * 128:h * 1024 + (qt + 1) * 128],
                pav[:, 0:128], AF.Copy), deps=[t_av], sig="se")
            t_avfree[u] = t_ave
            self.t_av_done[qt] = t_ave
            if wo_iter[u] is not None:
                self._wo_dc(*wo_iter[u], t_wo_ev)
        for dc in range(16):
            self._wo_dc(1, dc, t_wo_ev)
        self.t_wo_ev = t_wo_ev
        self.t_wo = [t_wo_ev[(1, dc)] for dc in range(16)] + \
                    [t_wo_ev[(0, dc)] for dc in range(16)]

    def _wo_dc(self, half, dc, t_wo_ev):
        p, d, P = self.p, self.d, self.P
        b = dc % 2
        t_wp = p.dma("gpsimd", self.WOB[b].rearrange("p (c n) -> p c n", c=8),
                     d["WoT"][:, dc * 128:(dc + 1) * 128].rearrange(
                         "(c p) n -> p c n", p=128),
                     deps=[t_wo_ev.get((half, dc - 2), None),
                           t_wo_ev.get((half - 1, dc + 14), None)], sig=f"dgp{b}")
        pwo = P[7] if dc % 2 == 0 else P[4]
        def mm(t, half=half, dc=dc, b=b, pwo=pwo):
            last = None
            for k in range(8):
                last = t.matmul(pwo[:, :], self.WOB[b][:, k * 128:(k + 1) * 128],
                                self.OFM[:, k * 1024 + half * 512:k * 1024
                                         + half * 512 + 512],
                                start=(k == 0), stop=(k == 7))
            return last
        deps = [t_wp] + [self.t_av_done[qt] for qt in range(half * 4, half * 4 + 4)]
        ev = t_wo_ev.get((half, dc - 2)) or t_wo_ev.get((half - 1, dc + 14))
        if ev:
            deps.append(ev)
        t_mm = p.op("tensor", mm, deps=deps, sig="pe")
        i = dc % 4
        xsa_war = [t for t in (t_wo_ev.get((half, dc - 4)),
                               t_wo_ev.get((half - 1, dc + 12))) if t]
        t_cp = p.op("scalar", lambda s, pwo=pwo, i=i: s.activation(
            self.XSA[i][:, :], pwo[:, :], AF.Copy), deps=[t_mm] + xsa_war, sig="se")
        t_dma = p.dma("sync", self.xattn[dc * 128:(dc + 1) * 128,
                                         half * 512:half * 512 + 512],
                      self.XSA[i][:, :], deps=[t_cp], sig="dso")
        t_wo_ev[(half, dc)] = t_dma

    def _ffn(self):
        p, d, P = self.p, self.d, self.P
        t_x1 = []
        t_h_free = None
        xs_last = {}
        w2_free = [None, None, None]
        t_b1r = p.dma("gpsimd", self.B1R[0:1, :],
                      d["b1"].rearrange("(one n) -> one n", one=1),
                      deps=self.t_wo, sig="dgb")
        t_b2r = p.dma("gpsimd", self.B2R[0:1, :],
                      d["b2"].rearrange("(one n) -> one n", one=1),
                      deps=self.t_wo, sig="dgb")
        for half in range(2):
            def rhs1(k, half=half):
                return self.XNO[:, k * TOK + half * 512:k * TOK + half * 512 + 512]
            def ev1(m, t_mm, half=half):
                ps = P[4] if m % 2 == 0 else P[5]
                return p.op("scalar", lambda s, m=m, ps=ps: s.activation(
                    self.H[:, m * 512:(m + 1) * 512], ps[:, :], AF.Gelu),
                    deps=[t_mm], sig="se")
            t_f1 = self.proj_std(d["p1T"], DC, FC, rhs1, ev1, self.W1B,
                                 (P[4], P[5]),
                                 deps0=self.t_xn + self.t_wo + ([t_h_free] if t_h_free else []),
                                 bias_row=self.B1R, bias_dep=t_b1r, mw=2,
                                 tiled=True)
            t_g_ev = None
            t_ij = {}
            for dc in range(16):
                b = dc % 3
                tw = [None]
                def wdma(s2, dc=dc, b=b):
                    return p.dma(
                        "gpsimd",
                        self.W2B[b][:, s2 * 32 * 128:(s2 + 1) * 32 * 128].rearrange(
                            "p (c n) -> p c n", c=32),
                        d["p2T"][dc, s2],
                        deps=[w2_free[b]] if w2_free[b] else [],
                        sig=f"dgp{b}")
                t_w2 = [wdma(0), wdma(1)]
                pg = P[6] if dc % 2 == 0 else P[7]
                def mm2(t, dc=dc, b=b, pg=pg, half=half):
                    for k in range(FC):
                        t.matmul(pg[:, :], self.W2B[b][:, k * 128:(k + 1) * 128],
                                 self.H[:, k * 512:(k + 1) * 512],
                                 start=(k == 0), stop=False)
                    return t.matmul(pg[:, :], self.B2R[0:1, dc * 128:(dc + 1) * 128],
                                    self.CONB[0:1, 512:1024], start=False, stop=True)
                deps = t_w2 + [t_f1, t_b2r]
                if t_g_ev:
                    deps.append(t_g_ev)
                t_mm2 = p.op("tensor", mm2, deps=deps, sig="pe")
                t_ij[dc] = t_mm2
                w2_free[b] = t_mm2
                j = dc % 2
                xs_war = [t for t in (xs_last.get(j),) if t]
                t_xa = p.dma("sync", self.XS[j][:, :],
                             self.xattn[dc * 128:(dc + 1) * 128,
                                        half * 512:half * 512 + 512],
                             deps=[self.t_wo_ev[(half, dc)]] + xs_war,
                             sig=f"dsa{j}")
                t_xo = p.dma("sync", self.XS[2 + j][:, :],
                             d["xT"][dc * 128:(dc + 1) * 128,
                                     HALO + half * 512:HALO + half * 512 + 512],
                             deps=xs_war, sig=f"dsb{j}")
                t_r1 = p.op("vector", lambda v, j=j, pg=pg: v.tensor_add(
                    self.XS[j][:, :], self.XS[j][:, :], pg[:, :]),
                    deps=[t_mm2, t_xa], sig="ve")
                t_g_ev = p.op("vector", lambda v, j=j: v.tensor_add(
                    self.XS[j][:, :], self.XS[j][:, :], self.XS[2 + j][:, :]),
                    deps=[t_xo, t_r1], sig="ve")
                t_d = p.dma("sync", d["x1"][dc * 128:(dc + 1) * 128,
                                            half * 512:half * 512 + 512],
                            self.XS[j][:, :], deps=[t_g_ev], sig="dso")
                xs_last[j] = t_d
                t_x1.append(t_d)
            t_h_free = t_ij[15]
            self.t_ffn_p45 = t_f1
        self.t_x1 = t_x1
        self.t_ffn_done = [t_x1[-1], t_h_free]
        self.out_toks += t_x1


    def _ln2_weff(self):
        p, d, P = self.p, self.d, self.P
        t_x1_last = self.t_x1[-1]
        self.t_beff_rd = None
        self.wouts = []
        ENTPW3 = self.ENTPW.rearrange("p (c n) -> p c n", c=FC)
        def src(c, b, lo, n, deps):
            return p.dma("sync", self.STR[b][:, lo:lo + n],
                         d["x1"][c * 128:(c + 1) * 128, lo:lo + n],
                         deps=list(deps) + [t_x1_last], sig=f"dsx{b}")
        def out(c, g):
            return self.XNO[:, c * TOK + g * 512:c * TOK + (g + 1) * 512]
        toks = self._ln(src, 2, out, [P[2], P[6]], [P[0], P[1]], "ln2",
                        bc_pair=(P[2], P[0]))
        self.t_xn1 = toks[:2]
        t_pw_free = None
        t_ev = None
        t_weff = []
        for l in (0, 1):
            t_pw = p.dma("gpsimd", ENTPW3,
                         d[f"entpw{l}"].rearrange("(c p) n -> p c n", p=128),
                         deps=([t_pw_free] if t_pw_free else self.t_ffn_done),
                         sig="dge")
            t_wp = [None, None, None]
            t_cons = [None, None, None]
            for di in range(16):
                b = di % 3
                dd = [t_cons[b]] if t_cons[b] else []
                for sp in range(2):
                    t_wp[b] = p.dma(
                        "gpsimd",
                        self.WTP[b][:, sp * 32 * 128:(sp + 1) * 32 * 128].rearrange(
                            "p (c n) -> p c n", c=32),
                        d[f"entwT{l}"][di, sp], deps=dd, sig=f"dgp{b}")
                def mmw(t, b=b):
                    last = None
                    for k in range(FC):
                        last = t.matmul(P[4][:, 0:256],
                                        self.WTP[b][:, k * 128:(k + 1) * 128],
                                        ENTPW3[:, k, :],
                                        start=(k == 0), stop=(k == FC - 1))
                    return last
                deps = [t_wp[b], t_pw, self.t_ffn_p45]
                if t_ev:
                    deps.append(t_ev)
                t_mm = p.op("tensor", mmw, deps=deps, sig="pe")
                t_cons[b] = t_mm
                t_ev = p.op("vector", lambda v, di=di: v.tensor_copy(
                    self.WEFFST[:, di * 256:(di + 1) * 256], P[4][:, 0:256]),
                    deps=[t_mm], sig="ve")
            wdst = (d["weffp"][l * D:(l + 1) * D, :] if self.fused
                    else d["weffp"][l])
            t_out = p.dma("sync", wdst.rearrange("(c p) n -> p c n", p=128),
                          self.WEFFST.rearrange("p (c n) -> p c n", c=16),
                          deps=[t_ev], sig="dso")
            self.out_toks.append(t_out)
            self.wouts.append(t_out)
            t_weff.append(t_ev)
            t_eb = p.dma("gpsimd", self.ENTBRAW[0:64, 0:128],
                         d["entb"][l].rearrange("(c p) -> c p", p=128),
                         sig=f"dgn{l}")
            t_tpe = p.op("tensor", lambda t: t.transpose(
                P[3][:, 0:64], self.ENTBRAW[0:64, 0:128], self.IDENT[0:64, 0:64]),
                deps=[t_eb, self.t_ident, self.t_p3], sig="pe")
            t_ecol = p.op("vector", lambda v, l=l: v.tensor_copy(
                self.ENTB[:, l * 64:(l + 1) * 64], P[3][:, 0:64]),
                deps=[t_tpe], sig="ve")
            self.t_p3 = t_ecol
            def mmb(t, l=l):
                last = None
                for k in range(FC):
                    last = t.matmul(P[5][0:1, 0:256],
                                    self.ENTB[:, l * 64 + k:l * 64 + k + 1],
                                    ENTPW3[:, k, :],
                                    start=(k == 0), stop=(k == FC - 1))
                return last
            t_bmm = p.op("tensor", mmb,
                         deps=[t_ecol, t_pw, self.t_ffn_p45]
                         + ([self.t_beff_rd] if self.t_beff_rd else []), sig="pe")
            t_pb = p.dma("gpsimd", self.EPB[0:1, 0:256], d["entpb"][l:l + 1, :],
                         sig=f"dgq{l}")
            t_badd = p.op("vector", lambda v: v.tensor_add(
                self.EPB[0:1, 256:512], P[5][0:1, 0:256], self.EPB[0:1, 0:256]),
                deps=[t_bmm, t_pb], sig="ve")
            self.t_beff_rd = t_badd
            t_bo = p.dma("sync", d["beffp"][l:l + 1, :], self.EPB[0:1, 256:512],
                         deps=[t_badd], sig="dso")
            self.out_toks.append(t_bo)
            self.wouts.append(t_bo)
            t_pw_free = p.op("vector", lambda v: v.engine_nop() if False else
                             v.tensor_copy(self.EPB[0:1, 512:514],
                                           self.EPB[0:1, 256:258]),
                             deps=[t_ev, t_badd, t_bo], sig="ve")
        self.t_weff_done = t_weff
        self.t_weff_outs = list(self.wouts)
        self.t_beff_done = [self.t_beff_rd]

    def _l1_proj(self):
        p, d, P = self.p, self.d, self.P
        deps0 = self.t_xn1 + self.t_weff_done + self.t_weff_outs
        outs = {"q": self.Q1FM, "k": self.K1FM, "v": self.V1FM}
        self.t_p3 = None
        for name, w in (("q", "Wq1"), ("k", "Wk1"), ("v", "Wv1")):
            dst = outs[name]
            def ev(m, t_mm, name=name, dst=dst):
                tE = None
                for seg in range(2):
                    ps = P[seg]
                    o = dst[:, m * 1024 + seg * 512:m * 1024 + seg * 512 + 512]
                    if name == "v":
                        tE = p.op("vector", lambda v, o=o, ps=ps: v.tensor_copy(
                            o, ps[:, :]), deps=[t_mm] if seg == 0 else [],
                            sig="ve" if seg == 1 else None)
                    else:
                        tE = p.op("vector", lambda v, o=o, ps=ps: v.tensor_scalar_max(
                            o, ps[:, :], 0.0), deps=[t_mm] if seg == 0 else [],
                            sig="ve" if seg == 1 else None)
                return tE
            def rhs(k):
                return self.XNO[:, k * TOK:(k + 1) * TOK]
            tok = self.proj_std(d[w], DC, DC, rhs, ev, self.WP, (P[0], P[1]),
                                deps0=deps0, tiled=True)
            if name == "q":
                t_q1o = p.dma("sync", d["q1"].rearrange("(c p) n -> p c n", p=128),
                              self.Q1FM.rearrange("p (c n) -> p c n", c=DC),
                              deps=[tok], sig="dso")
                self.out_toks.append(t_q1o)
                self.t_q1o = t_q1o
            deps0 = deps0 + [tok]
        t_kv_proj = deps0[-1]
        V1T3 = self.V1T.rearrange("p (h e) -> p h e", e=65)
        t_ones1 = p.op("vector", lambda v: v.memset(V1T3[:, :, 64:65], 1.0), sig="ve")
        t_acc0 = p.op("vector", lambda v: v.memset(self.KVST[:, :], 0.0), sig="ve")
        t_tpev = None
        t_mm_pair = {}
        t_acc = t_acc0
        for tt in range(NT):
            for cb in range(4):
                def tps(t, tt=tt, cb=cb):
                    last = None
                    for i in range(4):
                        c = cb * 4 + i
                        last = t.transpose(
                            P[3][:, i * 128:(i + 1) * 128],
                            self.K1FM[:, c * 1024 + tt * 128:c * 1024 + (tt + 1) * 128],
                            self.IDENT[:, :])
                    for i in range(4):
                        c = cb * 4 + i
                        last = t.transpose(
                            P[3][:, 512 + i * 128:512 + (i + 1) * 128],
                            self.V1FM[:, c * 1024 + tt * 128:c * 1024 + (tt + 1) * 128],
                            self.IDENT[:, :])
                    return last
                deps = [t_kv_proj, self.t_ident, self.t_p3]
                if t_tpev is not None:
                    deps.append(t_tpev)
                if tt > 0 and cb == 0:
                    deps.append(t_mm_pair[(tt - 1, 15)])
                t_tp = p.op("tensor", tps, deps=deps, sig="pe")
                self.t_p3 = t_tp
                t_k = p.op("vector", lambda v, cb=cb: v.tensor_copy(
                    self.K1T[:, cb * 512:(cb + 1) * 512], P[3][:, 0:512]),
                    deps=[t_tp], sig="ve")
                t_tpev = p.op("vector", lambda v, cb=cb: v.tensor_copy(
                    V1T3[:, cb * 8:(cb + 1) * 8, 0:64],
                    P[3][:, 512:1024].rearrange("p (h e) -> p h e", h=8)),
                    deps=[t_k, t_ones1], sig="ve")
                self.t_p3 = t_tpev
            for hp in range(16):
                ps = P[4] if hp % 2 == 0 else P[5]
                def mkv(t, tt=tt, hp=hp, ps=ps):
                    last = None
                    for odd in range(2):
                        h = 2 * hp + odd
                        last = t.matmul(
                            ps[odd * 64:odd * 64 + 64, 0:65],
                            self.K1T[:, h * 64:(h + 1) * 64],
                            V1T3[:, h, :],
                            start=True, stop=True,
                            tile_position=(0, 64) if odd else None)
                    return last
                deps = [t_tpev, self.t_beff_done[-1]]
                ev = t_mm_pair.get((tt, hp - 2)) or t_mm_pair.get((tt - 1, hp + 14))
                prev_add = getattr(self, "_t_kvadd", {}).get(
                    (tt, hp - 2)) or getattr(self, "_t_kvadd", {}).get(
                    (tt - 1, hp + 14))
                if prev_add:
                    deps.append(prev_add)
                t_mm = p.op("tensor", mkv, deps=deps, sig="pe")
                t_mm_pair[(tt, hp)] = t_mm
                t_add = p.op("vector", lambda v, hp=hp, ps=ps: v.tensor_add(
                    self.KVST[:, hp * 65:(hp + 1) * 65],
                    self.KVST[:, hp * 65:(hp + 1) * 65], ps[:, 0:65]),
                    deps=[t_mm, t_acc], sig="ve")
                t_acc = t_add
                if not hasattr(self, "_t_kvadd"):
                    self._t_kvadd = {}
                self._t_kvadd[(tt, hp)] = t_add
        t_e4 = t_acc
        t_kvo = p.dma("sync", d["kvp"], self.KVST[:, :], deps=[t_e4], sig="dso")
        self.out_toks.append(t_kvo)
        self.t_l1proj_done = [t_kvo, t_e4, self.t_q1o]

    def _issue_cc_weff(self):
        p, d = self.p, self.d
        self.cc_wf = p.cc(lambda g: g.collective_compute(
            "AllGather", ALU.bypass, replica_groups=[list(range(8))],
            ins=[d["weffp"].opt()], outs=[d["weffg"].opt()]),
            deps=[self.wouts[0], self.wouts[2]], sig="ccw")
        self.cc_bf = p.cc(lambda g: g.collective_compute(
            "AllGather", ALU.bypass, replica_groups=[list(range(8))],
            ins=[d["beffp"].opt()], outs=[d["beffg"].opt()]),
            deps=[self.wouts[1], self.wouts[3]], sig="ccb")

    def _issue_cc_kv(self):
        p, d = self.p, self.d
        self.cc_kv = p.cc(lambda g: g.collective_compute(
            "AllReduce", ALU.add, replica_groups=[[0, 1, 2, 3], [4, 5, 6, 7]],
            ins=[d["kvp"].opt()], outs=[d["kvsum"].opt()]),
            deps=[self.t_l1proj_done[0]], sig="cck")

    def _amp_ph(self):
        p, d, P = self.p, self.d, self.P
        APB = self.ar.view(self.ar.region_off["SQ"], BF16, D)
        PHB = self.ar.view(self.ar.region_off["ROWS"], BF16, D)
        t_ab = p.dma("gpsimd", APB[0:1, :],
                     d["ampb"].rearrange("(one n) -> one n", one=1),
                     deps=self.t_xn1, sig="dga")
        t_pb = p.dma("gpsimd", PHB[0:1, :],
                     d["phb"].rearrange("(one n) -> one n", one=1),
                     deps=self.t_xn1, sig="dgh")
        deps0 = self.t_xn1 + self.t_l1proj_done
        def rhs(k):
            return self.XNO[:, k * TOK:(k + 1) * TOK]
        for name, w, brow, bdep, dst, fn in (
                ("amp", "ampw", APB, t_ab, self.AMP, AF.Sigmoid),
                ("ph", "phw", PHB, t_pb, self.TH, AF.Tanh)):
            def ev(m, t_mm, dst=dst, fn=fn):
                tE = None
                for seg in range(2):
                    ps = P[6 + seg]
                    tE = p.op("scalar", lambda s, m=m, seg=seg, ps=ps, dst=dst,
                              fn=fn: s.activation(
                        dst[:, m * 1024 + seg * 512:m * 1024 + seg * 512 + 512],
                        ps[:, :], fn), deps=[t_mm] if seg == 0 else [],
                        sig="se" if seg == 1 else None)
                return tE
            tok = self.proj_std(d[w], DC, DC, rhs, ev, self.WPK, (P[6], P[7]),
                                deps0=deps0, bias_row=brow, bias_dep=bdep,
                                tiled=True)
            deps0 = deps0 + [tok]
        t_ph = deps0[-1]
        t_amp = deps0[-2]
        t_out = []
        t_dve_pr = None
        t_s2_pr = None
        for c in range(DC):
            th = self.TH[:, c * TOK:(c + 1) * TOK]
            t_s1 = p.op("scalar", lambda s, th=th: s.activation(
                self.SC[:, 0:1024], th, AF.Sin, scale=PI),
                deps=[t_ph] + ([t_dve_pr] if t_dve_pr else []), sig="se")
            t_s2 = p.op("scalar", lambda s, th=th: s.activation(
                self.SC[:, 1024:2048], th, AF.Sin, scale=PI / 2), sig="se")
            t_d0 = p.op("vector", lambda v: v.tensor_mul(
                self.SC[:, 1024:2048], self.SC[:, 1024:2048], self.SC[:, 1024:2048]),
                deps=[t_s2], sig="ve")
            t_d1 = p.op("vector", lambda v: v.tensor_scalar(
                self.SC[:, 1024:2048], self.SC[:, 1024:2048], -2.0, 1.0,
                ALU.mult, ALU.add), deps=[t_d0], sig="ve")
            t_d2 = p.op("vector", lambda v: v.tensor_add(
                self.SC[:, 1024:2048], self.SC[:, 1024:2048], self.SC[:, 0:1024]),
                deps=[t_s1, t_d1], sig="ve")
            t_d3 = p.op("vector", lambda v, c=c, th=th: v.tensor_mul(
                th, self.SC[:, 1024:2048],
                self.AMP[:, c * TOK:(c + 1) * TOK]),
                deps=[t_d2, t_amp], sig="ve")
            t_dve_pr = t_d3
            t_o = p.dma("sync", d["h0"][c * 128:(c + 1) * 128, :], th,
                        deps=[t_d3], sig="dso")
            t_out.append(t_o)
        self.out_toks += t_out


def host_masks(first):
    jj, pp = np.meshgrid(np.arange(WIN), np.arange(128))
    band = (jj >= pp) & (jj <= pp + 512)
    out = np.zeros((5, 128, WIN), np.float32)
    for s in range(5):
        ok = band.copy()
        if first and s < 4:
            ok = ok & (jj >= 512 - s * 128)
        out[s] = np.where(ok, 0.0, -1e9).astype(np.float32)
    return out


def tile_proj(WT, kc, mc, mw=1, dma_split=1):
    K, M = WT.shape
    assert K == kc * 128 and M == mc * 128
    kc_spl = kc // dma_split
    A = WT.reshape(dma_split, kc_spl, 128, mc // mw, mw * 128)
    return np.ascontiguousarray(A.transpose(3, 0, 2, 1, 4))


def host_l0_weights(I):
    C = np.ascontiguousarray
    return {
        "WqT": C(np.asarray(I["L0_attn_Wq"]).T),
        "WkvT": C(np.asarray(I["L0_attn_Wkv"]).T),
        "WgT": C(np.asarray(I["L0_attn_Wg"]).T),
        "bg": np.asarray(I["L0_attn_bg"]),
        "qsc": np.asarray(I["L0_attn_q_scale"]),
        "ksc": np.asarray(I["L0_attn_k_scale"]),
        "WoT": C(np.asarray(I["L0_attn_Wo"]).T),
        "p1T": tile_proj(np.asarray(I["L0_ffn_proj1"]).T, DC, FC, mw=2),
        "b1": np.asarray(I["L0_ffn_proj1_bias"]),
        "p2T": tile_proj(np.asarray(I["L0_ffn_proj2"]).T, FC, DC,
                         dma_split=2),
        "b2": np.asarray(I["L0_ffn_proj2_bias"]),
    }


def host_l1_weights(I):
    C = np.ascontiguousarray
    w = {
        "Wq1": tile_proj(np.asarray(I["L1_attn_Wq"]), DC, DC),
        "Wk1": tile_proj(np.asarray(I["L1_attn_Wk"]), DC, DC),
        "Wv1": tile_proj(np.asarray(I["L1_attn_Wv"]), DC, DC),
        "ampw": tile_proj(np.asarray(I["L1_ffn_amp_w"]), DC, DC),
        "ampb": np.asarray(I["L1_ffn_amp_b"]),
        "phw": tile_proj(np.asarray(I["L1_ffn_ph_w"]), DC, DC),
        "phb": np.asarray(I["L1_ffn_ph_b"]),
        "entwT0": tile_proj(np.asarray(I["L1_ffn_ent_w"][0]).T, FC, DC,
                            dma_split=2),
        "entwT1": tile_proj(np.asarray(I["L1_ffn_ent_w"][1]).T, FC, DC,
                            dma_split=2),
        "entb": np.asarray(I["L1_ffn_ent_b"]),
    }
    return w


def host_inputs_l0(I, with_l1=True):
    x = np.asarray(I["L0_attn_x"])
    wl0 = host_l0_weights(I)
    if with_l1:
        wl1 = host_l1_weights(I)
        epw = np.asarray(I["L1_ffn_ent_pw"])
        epb = np.asarray(I["L1_ffn_ent_pb"])
    maps = []
    for c in range(8):
        b, t0 = c // 4, (c % 4) * TOK
        xs = np.zeros((TOT, D), np.float32)
        lo = max(0, t0 - HALO)
        xs[HALO - (t0 - lo):] = x[b, lo:t0 + TOK]
        m = {"xT": np.ascontiguousarray(xs.T),
             "masks": host_masks(t0 == 0)}
        m.update(wl0)
        if with_l1:
            m.update(wl1)
            m["entpw0"] = np.ascontiguousarray(epw[0][:, c * 256:(c + 1) * 256])
            m["entpw1"] = np.ascontiguousarray(epw[1][:, c * 256:(c + 1) * 256])
            m["entpb"] = np.ascontiguousarray(
                np.stack([epb[0, c * 256:(c + 1) * 256],
                          epb[1, c * 256:(c + 1) * 256]]))
        maps.append(m)
    return maps



class L2Build:

    def __init__(self, dbg=False, fused_io=None):
        self.fused = fused_io is not None
        if self.fused:
            nc = fused_io["nc"]
            self.p = fused_io["p"]
            self.cc_kv = fused_io["cc_kv"]
            self.cc_wf = fused_io["cc_wf"]
            self.cc_bf = fused_io["cc_bf"]
        else:
            nc = bass.Bass("TRN2", target_bir_lowering=False)
            self.p = Prog(nc)
        self.nc = nc
        self.dbg = dbg
        d = {}
        def di(name, shape, dt=F32):
            d[name] = nc.dram_tensor(name, shape, dt, kind="ExternalInput").ap()
        def do(name, shape, dt=F32):
            d[name] = nc.dram_tensor(name, shape, dt, kind="ExternalOutput").ap()
        if self.fused:
            for k in ("x1", "h0", "q1", "kvsum", "weffg", "beffg"):
                d[k] = fused_io[k]
        else:
            di("x1", [D, TOK]); di("h0", [D, TOK]); di("q1", [D, TOK], BF16)
            di("kvps", [4, 128, 1056]); di("weff", [2, D, D], BF16)
            di("beff", [2, D])
        di("Wo1", [DC, 1, 128, DC, 128]); di("bo", [D])
        di("pathw", [2, 2, DC, 1, 128, DC, 128])
        di("pathb", [2, 2, D]); di("phase", [2, 2, D])
        di("ln3g", [2, D]); di("ln3b", [2, D]); di("ln4g", [2, D]); di("ln4b", [2, D])
        do("x2", [D, TOK])
        if dbg:
            do("d_o1", [D, TOK], BF16)
            do("d_x2a", [D, TOK])
            do("d_hent0", [D, TOK])
            do("d_hln30", [D, TOK])
            do("d_hpath0", [D, TOK])
            do("d_hln40", [D, TOK])
        self.d = d
        self.out_toks = []

    def build(self):
        nc = self.nc
        with contextlib.ExitStack() as st:
            ar = Arena(nc, 212600)
            ar.t = st.enter_context(nc.sbuf_tensor("AR", [128, 212600], U8))
            self.ar = ar
            self.P = [st.enter_context(nc.psum_tensor(f"P{i}", [128, 512], F32))
                      for i in range(8)]
            return self._build_in(finalize=True)

    def _build_in(self, finalize):
        nc, p, d, ar, P = self.nc, self.p, self.d, self.ar, self.P
        if True:
            self.CON = ar.alloc(1024, F32, 256, "CON")
            self.CONB = ar.alloc(2048, BF16, 1024, "CONB")
            self.IDF = ar.alloc(512, F32, 128, "IDF")
            self.ROWS = ar.alloc(8192, F32, 2048, "ROWS")
            self.SQ2 = ar.alloc(4096, F32, 1024, "SQ2")
            self.TMP = [ar.alloc(2048, F32, 512, f"TMP{i}") for i in range(4)]
            self.GBC = ar.alloc(512, F32, 128, "GBC")
            self.MODC = ar.alloc(256, F32, 64, "MODC")
            self.PBC = ar.alloc(256, F32, 64, "PBC")
            self.X1 = ar.alloc(DC * TOK * 4, F32, DC * TOK, "X1")
            self.GBALL = ar.alloc(1024, F32, 256, "GBALL")
            ar.mark("q")
            self.O1 = ar.alloc(DC * TOK * 2, BF16, DC * TOK, "O1")
            self._consts()
            self._qkv()
            self._wo1()
            if self.dbg:
                self.out_toks.append(p.dma(
                    "sync", d["d_o1"].rearrange("(c p) n -> p c n", p=128),
                    self.O1.rearrange("p (c n) -> p c n", c=DC),
                    deps=[self.t_o1], sig="dso"))
                self.out_toks.append(p.dma(
                    "sync", d["d_x2a"].rearrange("(c p) n -> p c n", p=128),
                    self.X1.rearrange("p (c n) -> p c n", c=DC),
                    deps=[self.t_x2a], sig="dso"))
            ar.reset("q")
            self.H = ar.alloc(DC * TOK * 4, F32, DC * TOK, "H")
            self.HB = ar.alloc(DC * TOK * 2, BF16, DC * TOK, "HB")
            self.WPN = [ar.alloc(DC * 128 * 2, BF16, DC * 128, f"WPN{i}")
                        for i in range(2)]
            self.WPN2 = [ar.alloc(DC * 128 * 2, BF16, DC * 128, f"WPN2{i}")
                         for i in range(2)]
            self.BEFFR = ar.alloc(4096, BF16, 2048, "BEFFR")
            self._ffn1()
            for tk in self.out_toks:
                p.wait("sync", tk)
            p.build()
        return nc

    def _consts(self):
        p, d, P = self.p, self.d, self.P
        t0 = p.op("vector", lambda v: v.memset(self.CON[:, 0:1], 1.0), sig="ve")
        self.t_ones = p.op("vector", lambda v: v.memset(self.CON[0:1, 64:192], 1.0),
                           deps=[t0], sig="ve")
        self.t_onesb = p.op("vector", lambda v: v.memset(self.CONB[:, :], 1.0),
                            sig="ve")
        t_msf = p.op("gpsimd", lambda g: g.memset(self.IDF[:, :], 0.0),
                     sig="ge")
        self.t_idf = p.op("gpsimd", lambda g: g.affine_select(
            out=self.IDF[:, :], in_=self.IDF[:, :], compare_op=ALU.not_equal,
            fill=1.0, base=0, pattern=[[-1, 128]], channel_multiplier=1),
            deps=[t_msf], sig="ge")
        self.t_x1 = p.dma("sync", self.X1.rearrange("p (c n) -> p c n", c=DC),
                          d["x1"].rearrange("(c p) n -> p c n", p=128), sig="dsx")
        self.LNR = self.ar.alloc(8192, F32, 2048, "LNR")
        t_l = []
        for i, nm in enumerate(("ln3g", "ln3b", "ln4g", "ln4b")):
            t_l.append(p.dma("gpsimd", self.LNR[2 * i:2 * i + 2, 0:D], d[nm],
                             sig="dgl"))
        t_ph4 = p.dma("gpsimd", self.LNR[8:12, 0:D],
                      d["phase"].rearrange("l pp n -> (l pp) n"), sig="dgl")
        t_pb4 = p.dma("gpsimd", self.LNR[12:16, 0:D],
                      d["pathb"].rearrange("l pp n -> (l pp) n"), sig="dgl")
        def tpl(t):
            last = None
            for c in range(DC):
                last = t.transpose(P[4][:, c * 16:(c + 1) * 16],
                                   self.LNR[0:16, c * 128:(c + 1) * 128],
                                   self.IDF[0:16, 0:16])
            return last
        t_tp = p.op("tensor", tpl, deps=t_l + [t_ph4, t_pb4, self.t_idf], sig="pe")
        t_gb = p.op("vector", lambda v: v.tensor_copy(
            self.GBALL[:, 0:256], P[4][:, 0:256]), deps=[t_tp], sig="ve")
        PH = self.GBALL.rearrange("p (c r) -> p c r", r=16)
        phs = PH[:, :, 8:12]
        t_rr = t_gb
        for _ in range(2):
            t_a = p.op("vector", lambda v: v.tensor_single_scalar(
                self.SQ2[:, 0:64].rearrange("p (c r) -> p c r", r=4),
                phs, math.pi, ALU.is_gt), deps=[t_rr], sig="ve")
            t_b = p.op("vector", lambda v: v.scalar_tensor_tensor(
                phs, self.SQ2[:, 0:64].rearrange("p (c r) -> p c r", r=4),
                -2 * math.pi, phs, ALU.mult, ALU.add), deps=[t_a], sig="ve")
            t_c = p.op("vector", lambda v: v.tensor_single_scalar(
                self.SQ2[:, 0:64].rearrange("p (c r) -> p c r", r=4),
                phs, -math.pi, ALU.is_lt), deps=[t_b], sig="ve")
            t_rr = p.op("vector", lambda v: v.scalar_tensor_tensor(
                phs, self.SQ2[:, 0:64].rearrange("p (c r) -> p c r", r=4),
                2 * math.pi, phs, ALU.mult, ALU.add), deps=[t_c], sig="ve")
        MOD = self.MODC.rearrange("p (c r) -> p c r", r=4)
        t_s1 = p.op("scalar", lambda s: s.activation(
            MOD, phs, AF.Sin), deps=[t_rr], sig="se")
        t_s2 = p.op("scalar", lambda s: s.activation(
            self.SQ2[:, 64:128].rearrange("p (c r) -> p c r", r=4),
            phs, AF.Sin, scale=0.5), deps=[t_rr], sig="se")
        t_m1 = p.op("vector", lambda v: v.tensor_mul(
            self.SQ2[:, 64:128], self.SQ2[:, 64:128], self.SQ2[:, 64:128]),
            deps=[t_s2], sig="ve")
        t_m2 = p.op("vector", lambda v: v.tensor_scalar(
            self.SQ2[:, 64:128], self.SQ2[:, 64:128], -2.0, 1.0, ALU.mult, ALU.add),
            deps=[t_m1], sig="ve")
        t_mod = p.op("vector", lambda v: v.tensor_add(
            self.MODC[:, 0:64], self.MODC[:, 0:64], self.SQ2[:, 64:128]),
            deps=[t_s1, t_m2], sig="ve")
        PBv = PH[:, :, 12:16].rearrange("p c (l pp) -> p c l pp", l=2, pp=2)
        MODv = MOD.rearrange("p c (l pp) -> p c l pp", l=2, pp=2)
        BPE = self.PBC[:, 0:32].rearrange("p (c l) -> p c l", l=2)
        t_p1 = p.op("vector", lambda v: v.tensor_mul(
            self.SQ2[:, 128:192].rearrange("p (c l pp) -> p c l pp", l=2, pp=2),
            PBv, MODv), deps=[t_mod, t_gb], sig="ve")
        SQv = self.SQ2[:, 128:192].rearrange("p (c l pp) -> p c l pp", l=2, pp=2)
        t_bpe = p.op("vector", lambda v: v.tensor_add(
            BPE, SQv[:, :, :, 0], SQv[:, :, :, 1]), deps=[t_p1], sig="ve")
        self.t_mod = t_mod
        self.t_bpe = t_bpe
        self.t_gb = t_gb
        self.BOR = self.ar.view(self.ar.region_off["LNR"], BF16, D)
        self.t_bo = p.dma("gpsimd", self.BOR[0:1, :],
                          d["bo"].rearrange("(one n) -> one n", one=1),
                          deps=[t_gb], sig="dgb")

    def _qkv(self):
        p, d, P = self.p, self.d, self.P
        self.WPN0 = [self.ar.alloc(DC * 128 * 2, BF16, DC * 128, f"WPN0{i}")
                     for i in range(2)]
        self.Q1 = self.ar.alloc(DC * TOK * 2, BF16, DC * TOK, "Q1")
        self.KVP4 = self.ar.alloc(4 * 1056 * 4, F32, 4 * 1056, "KVP4")
        self.KVE = self.ar.alloc(1056 * 4, F32, 1056, "KVE")
        self.KVB = self.ar.alloc(1056 * 2, BF16, 1056, "KVB")
        t_q = p.dma("sync", self.Q1.rearrange("p (c n) -> p c n", c=DC),
                    d["q1"].rearrange("(c p) n -> p c n", p=128), sig="dsq")
        if self.fused:
            t_s = p.dma("sync", self.KVE[:, :], d["kvsum"],
                        deps=[self.cc_kv], sig="dsk")
        else:
            t_kv = p.dma("sync", self.KVP4.rearrange("p (g n) -> p g n", g=4),
                         d["kvps"].rearrange("g p n -> p g n"), sig="dsk")
            t_s = p.op("vector", lambda v: v.tensor_add(
                self.KVE[:, :], self.KVP4[:, 0:1056], self.KVP4[:, 1056:2112]),
                deps=[t_kv], sig="ve")
            t_s = p.op("vector", lambda v: v.tensor_add(
                self.KVE[:, :], self.KVE[:, :], self.KVP4[:, 2112:3168]),
                deps=[t_s], sig="ve")
            t_s = p.op("vector", lambda v: v.tensor_add(
                self.KVE[:, :], self.KVE[:, :], self.KVP4[:, 3168:4224]),
                deps=[t_s], sig="ve")
        t_b = p.op("vector", lambda v: v.tensor_copy(self.KVB[:, :], self.KVE[:, :]),
                   deps=[t_s], sig="ve")
        t_ev = {}
        for h in range(32):
            hp, odd = h // 2, h % 2
            par = h % 2
            pa, pb = (P[0], P[1]) if par == 0 else (P[2], P[3])
            pc = P[4] if par == 0 else P[5]
            R = self.ROWS[0:1, par * 1024:par * 1024 + 1024]
            TMa = self.TMP[0 + 2 * par]
            TMb = self.TMP[1 + 2 * par]
            lhs = self.KVB[odd * 64:odd * 64 + 64, hp * 65:hp * 65 + 65]
            rhs0 = self.Q1[odd * 64:odd * 64 + 64,
                           (h // 2) * 1024:(h // 2) * 1024 + 512]
            rhs1 = self.Q1[odd * 64:odd * 64 + 64,
                           (h // 2) * 1024 + 512:(h // 2) * 1024 + 1024]
            def mm(t, lhs=lhs, rhs0=rhs0, rhs1=rhs1, pa=pa, pb=pb, odd=odd):
                t.matmul(pa[0:65, :], lhs, rhs0, start=True, stop=True,
                         tile_position=(64, 0) if odd else None)
                return t.matmul(pb[0:65, :], lhs, rhs1, start=True, stop=True,
                                tile_position=(64, 0) if odd else None)
            deps = [t_b, t_q]
            if t_ev.get(h - 2):
                deps.append(t_ev[h - 2])
            t_mm = p.op("tensor", mm, deps=deps, sig="pe")
            t_l0 = p.op("scalar", lambda s, pa=pa, R=R: s.activation(
                R[:, 0:512], pa[64:65, :], AF.Ln), deps=[t_mm], sig="se")
            t_l1 = p.op("scalar", lambda s, pb=pb, R=R: s.activation(
                R[:, 512:1024], pb[64:65, :], AF.Ln), deps=[t_l0], sig="se")
            t_n1 = p.op("scalar", lambda s, R=R: s.activation(
                R[:, 0:1024], R[:, 0:1024], AF.Exp, scale=-1.0),
                deps=[t_l1], sig="se")
            t_r = p.op("vector", lambda v, R=R: v.tensor_scalar_min(
                R[:, 0:1024], R[:, 0:1024], 1e6), deps=[t_n1], sig="ve")
            def bc(t, pc=pc, R=R):
                t.matmul(pc[0:64, :], self.CON[0:1, 64:128], R[:, 0:512],
                         start=True, stop=True)
                return t.matmul(pc[64:128, :], self.CON[0:1, 64:128],
                                R[:, 512:1024], start=True, stop=True,
                                tile_position=(0, 64))
            t_bc = p.op("tensor", bc,
                        deps=[t_r, self.t_ones] + ([t_ev[h - 2]] if t_ev.get(h - 2)
                                                   else []), sig="pe")
            t_c0 = p.op("vector", lambda v, pa=pa, TMa=TMa: v.tensor_copy(
                TMa[:64, :], pa[0:64, :]), deps=[t_mm], sig="ve")
            t_c1 = p.op("scalar", lambda s, pb=pb, TMb=TMb: s.activation(
                TMb[:64, :], pb[0:64, :], AF.Copy), deps=[t_mm], sig="se")
            o0 = self.O1[odd * 64:odd * 64 + 64,
                         (h // 2) * 1024:(h // 2) * 1024 + 512]
            o1 = self.O1[odd * 64:odd * 64 + 64,
                         (h // 2) * 1024 + 512:(h // 2) * 1024 + 1024]
            t_d0 = p.op("vector", lambda v, o0=o0, TMa=TMa, pc=pc: v.tensor_mul(
                o0, TMa[:64, :], pc[0:64, :]), deps=[t_c0, t_bc], sig="ve")
            t_ev[h] = p.op("vector", lambda v, o1=o1, TMb=TMb, pc=pc: v.tensor_mul(
                o1, TMb[:64, :], pc[64:128, :]), deps=[t_c1, t_bc, t_d0], sig="ve")
        self.t_o1 = t_ev[31]

    def _wo1(self):
        p, d, P = self.p, self.d, self.P
        t_free = {}
        t_w = [None, None]
        t_cons = [None, None]
        self.t_x2a = []
        for m in range(DC):
            b = m % 2
            t_w[b] = p.dma("gpsimd",
                           self.WPN0[b].rearrange("p (c n) -> p c n", c=DC),
                           d["Wo1"][m, 0],
                           deps=[t_cons[b]] if t_cons[b] else [], sig=f"dgp{b}")
            pa, pb = (P[6], P[7])
            def mm(t, m=m, b=b):
                last = None
                for seg, ps in ((0, pa), (1, pb)):
                    for k in range(DC):
                        t.matmul(ps[:, :],
                                 self.WPN0[b][:, k * 128:(k + 1) * 128],
                                 self.O1[:, k * 1024 + seg * 512:
                                         k * 1024 + seg * 512 + 512],
                                 start=(k == 0), stop=False)
                    last = t.matmul(ps[:, :], self.BOR[0:1, m * 128:(m + 1) * 128],
                                    self.CONB[0:1, 512:1024], start=False, stop=True)
                return last
            deps = [t_w[b], self.t_o1, self.t_bo, self.t_onesb, self.t_x1]
            if t_free.get(m - 1):
                deps.append(t_free[m - 1])
            t_mm = p.op("tensor", mm, deps=deps, sig="pe")
            t_cons[b] = t_mm
            t_a0 = p.op("vector", lambda v, m=m, pa=pa: v.tensor_add(
                self.X1[:, m * 1024:m * 1024 + 512],
                self.X1[:, m * 1024:m * 1024 + 512], pa[:, :]),
                deps=[t_mm], sig="ve")
            t_free[m] = p.op("vector", lambda v, m=m, pb=pb: v.tensor_add(
                self.X1[:, m * 1024 + 512:(m + 1) * 1024],
                self.X1[:, m * 1024 + 512:(m + 1) * 1024], pb[:, :]),
                deps=[t_a0], sig="ve")
        self.t_x2a = t_free[DC - 1]

    def _ln_seg(self, seg, src, gcol_r, bcol_r):
        p, P = self.p, self.P
        ones = self.CON[:, 0:1]
        GB = self.GBALL.rearrange("p (c r) -> p c r", r=16)
        SU, SS = (P[0], P[2]) if seg == 0 else (P[1], P[3])
        R = self.ROWS[0:1, seg * 1024:seg * 1024 + 1024]
        SQ = self.SQ2[0:1, seg * 512:(seg + 1) * 512]
        t_sq_pr = None
        t_pe = None
        for c in range(DC):
            hsl = self.H[:, c * 1024 + seg * 512:c * 1024 + (seg + 1) * 512]
            t_sq = p.op("scalar", lambda sc, hsl=hsl, seg=seg: sc.activation(
                self.SQ2[:, seg * 512:(seg + 1) * 512], hsl, AF.Square),
                deps=[src] + ([t_sq_pr] if t_sq_pr else []), sig="se")
            def mms(t, c=c, hsl=hsl, SU=SU, SS=SS, seg=seg):
                t.matmul(SU[0:1, :], ones, hsl, start=(c == 0), stop=(c == DC - 1))
                return t.matmul(SS[0:1, :], ones,
                                self.SQ2[:, seg * 512:(seg + 1) * 512],
                                start=(c == 0), stop=(c == DC - 1))
            t_pe = p.op("tensor", mms, deps=[src, t_sq], sig="pe")
            t_sq_pr = t_pe
        t0 = p.op("vector", lambda v: v.tensor_scalar_mul(
            R[:, 0:512], SU[0:1, :], 1.0 / D), deps=[t_pe], sig="ve")
        ta = p.op("vector", lambda v: v.tensor_scalar_mul(
            R[:, 512:1024], SS[0:1, :], 1.0 / D), deps=[t0], sig="ve")
        tb = p.op("vector", lambda v: v.tensor_mul(
            SQ, R[:, 0:512], R[:, 0:512]), deps=[ta], sig="ve")
        tc = p.op("vector", lambda v: v.tensor_sub(
            R[:, 512:1024], R[:, 512:1024], SQ), deps=[tb], sig="ve")
        t2 = p.op("vector", lambda v: v.tensor_scalar_add(
            R[:, 512:1024], R[:, 512:1024], EPS), deps=[tc], sig="ve")
        t3 = p.op("scalar", lambda sc: sc.activation(SQ, R[:, 512:1024], AF.Ln),
                  deps=[t2], sig="se")
        t4 = p.op("scalar", lambda sc: sc.activation(R[:, 512:1024], SQ, AF.Exp,
                                                     scale=-0.5), deps=[t3], sig="se")
        def bc(t, SU=SU, SS=SS):
            t.matmul(SU[:, :], self.CON[0:1, 64:192], R[:, 0:512],
                     start=True, stop=True)
            return t.matmul(SS[:, :], self.CON[0:1, 64:192], R[:, 512:1024],
                            start=True, stop=True)
        t_bc = p.op("tensor", bc, deps=[t0, t4, self.t_ones], sig="pe")
        TMa, TMb = (self.TMP[0], self.TMP[1]) if seg == 0 else \
                   (self.TMP[2], self.TMP[3])
        t_ap = None
        if not hasattr(self, "lnc"):
            self.lnc = {}
        self.lnc[seg] = []
        for c in range(DC):
            hsl = self.H[:, c * 1024 + seg * 512:c * 1024 + (seg + 1) * 512]
            t_u = p.op("vector", lambda v, hsl=hsl, TMa=TMa, SU=SU: v.tensor_sub(
                TMa[:, :], hsl, SU[:, :]),
                deps=[t_bc] + ([t_ap] if t_ap else []), sig="ve")
            t_u2 = p.op("vector", lambda v, TMa=TMa, TMb=TMb, SS=SS: v.tensor_mul(
                TMb[:, :], TMa[:, :], SS[:, :]), deps=[t_u], sig="ve")
            t_ap = p.op("vector", lambda v, hsl=hsl, c=c, TMb=TMb: v.tensor_scalar(
                hsl, TMb[:, :], GB[:, c, gcol_r:gcol_r + 1],
                GB[:, c, bcol_r:bcol_r + 1], ALU.mult, ALU.add),
                deps=[t_u2, self.t_gb], sig="ve")
            self.lnc[seg].append(t_ap)
        return t_ap

    def _hb_seg(self, seg, deps):
        p = self.p
        t_hb = None
        for c in range(DC):
            t_hb = p.op("gpsimd", lambda g, c=c, seg=seg: g.tensor_copy(
                self.HB[:, c * 1024 + seg * 512:c * 1024 + (seg + 1) * 512],
                self.H[:, c * 1024 + seg * 512:c * 1024 + (seg + 1) * 512]),
                deps=deps if c == 0 else [], sig="ge")
        return t_hb

    def _ffn1(self):
        p, d, P = self.p, self.d, self.P
        t_h = p.dma("sync", self.H.rearrange("p (c n) -> p c n", c=DC),
                    d["h0"].rearrange("(c p) n -> p c n", p=128),
                    deps=[self.t_x2a], sig="dsh")

        MOD = self.MODC.rearrange("p (c r) -> p c r", r=4)
        BPE = self.PBC[:, 0:32].rearrange("p (c l) -> p c l", l=2)
        ent_bank = [P[6], P[7]]
        path_bank = [(P[6], P[4]), (P[7], P[5])]
        ring = self.WPN + self.WPN2
        ring_free = [None, None, None, None]
        rstate = {"i": 0}
        def load_panel(hbm_slice, extra, pre_tiled=False):
            i = rstate["i"] % 4
            rstate["i"] += 1
            b = ring[i]
            src = (hbm_slice if pre_tiled else
                   hbm_slice.rearrange("(c p) n -> p c n", p=128))
            tok = p.dma("gpsimd", b.rearrange("p (c n) -> p c n", c=DC),
                        src,
                        deps=[t for t in [ring_free[i]] + extra if t],
                        sig=f"dgr{i}")
            return i, b, tok
        wfree = {}
        t_stage = [[t_h, self.t_x2a], [t_h, self.t_x2a]]
        for l in range(2):
            bdeps = [t for t in ring_free if t] + t_stage[0] + t_stage[1]
            if self.fused:
                tb = None
                for r in range(8):
                    tb = p.dma("gpsimd", self.BEFFR[0:1, r * 256:(r + 1) * 256],
                               d["beffg"][r * 2 + l:r * 2 + l + 1, :],
                               deps=(bdeps + [self.cc_bf]) if r == 0 else [],
                               sig="dgb2")
                self.t_beff = tb
            else:
                self.t_beff = p.dma("gpsimd", self.BEFFR[0:1, 0:D],
                                    d["beff"][l:l + 1, :],
                                    deps=bdeps, sig="dgb2")
            t_ent = [None, None]
            for seg in range(2):
                t_hb = self._hb_seg(seg, t_stage[seg])
                t_ev = None
                for m in range(DC):
                    if self.fused:
                        r0 = (m // 2) * 4096 + l * 2048
                        co = (m % 2) * 128
                        wsl = d["weffg"][r0:r0 + 2048, co:co + 128]
                    else:
                        wsl = d["weff"][l][:, m * 128:(m + 1) * 128]
                    ri, rb, t_w = load_panel(wsl, [t_hb])
                    ps = ent_bank[seg]
                    def mm(t, m=m, rb=rb, l=l, seg=seg, ps=ps):
                        for k in range(DC):
                            t.matmul(ps[:, :],
                                     rb[:, k * 128:(k + 1) * 128],
                                     self.HB[:, k * 1024 + seg * 512:
                                             k * 1024 + seg * 512 + 512],
                                     start=(k == 0), stop=False)
                        return t.matmul(
                            ps[:, :],
                            self.BEFFR[0:1, m * 128:(m + 1) * 128],
                            self.CONB[0:1, 512:1024], start=False, stop=True)
                    deps = [t_w, t_hb, self.t_beff, self.t_onesb]
                    if t_ev:
                        deps.append(t_ev)
                    t_mm = p.op("tensor", mm, deps=deps, sig="pe")
                    ring_free[ri] = t_mm
                    t_ev = p.op("vector", lambda v, m=m, seg=seg, ps=ps: v.tensor_add(
                        self.H[:, m * 1024 + seg * 512:m * 1024 + seg * 512 + 512],
                        self.H[:, m * 1024 + seg * 512:m * 1024 + seg * 512 + 512],
                        ps[:, :]), deps=[t_mm], sig="ve")
                t_ent[seg] = t_ev
            if self.dbg and l == 0:
                tdmp = p.dma("sync", d["d_hent0"].rearrange("(c p) n -> p c n", p=128),
                             self.H.rearrange("p (c n) -> p c n", c=DC),
                             deps=[t_ent[0], t_ent[1]], sig="dso")
                self.out_toks.append(tdmp)
                p.wait("tensor", tdmp)
            t_ln3 = [self._ln_seg(0, t_ent[0], 0 + l, 2 + l),
                     self._ln_seg(1, t_ent[1], 0 + l, 2 + l)]
            if self.dbg and l == 0:
                tdmp = p.dma("sync", d["d_hln30"].rearrange("(c p) n -> p c n", p=128),
                             self.H.rearrange("p (c n) -> p c n", c=DC),
                             deps=[t_ln3[0], t_ln3[1]], sig="dso")
                self.out_toks.append(tdmp)
                p.wait("tensor", tdmp)
            t_path = [None, None]
            for seg in range(2):
                t_hbs = self._hb_seg(seg, [t_ln3[seg]])
                t_ev = None
                for m in range(DC):
                    ra, ba, t_wa = load_panel(
                        d["pathw"][l, 0, m, 0], [t_hbs], pre_tiled=True)
                    rb_, bb, t_wb = load_panel(
                        d["pathw"][l, 1, m, 0], [t_hbs], pre_tiled=True)
                    pa, pb = path_bank[seg]
                    def mm2(t, m=m, ba=ba, bb=bb, seg=seg, pa=pa, pb=pb):
                        for k in range(DC):
                            t.matmul(pa[:, :],
                                     ba[:, k * 128:(k + 1) * 128],
                                     self.HB[:, k * 1024 + seg * 512:
                                             k * 1024 + seg * 512 + 512],
                                     start=(k == 0), stop=(k == DC - 1))
                        last = None
                        for k in range(DC):
                            last = t.matmul(
                                pb[:, :],
                                bb[:, k * 128:(k + 1) * 128],
                                self.HB[:, k * 1024 + seg * 512:
                                        k * 1024 + seg * 512 + 512],
                                start=(k == 0), stop=(k == DC - 1))
                        return last
                    deps = [t_wa, t_wb, t_hbs]
                    if t_ev:
                        deps.append(t_ev)
                    t_mm = p.op("tensor", mm2, deps=deps, sig="pe")
                    ring_free[ra] = t_mm
                    ring_free[rb_] = t_mm
                    hsl = self.H[:, m * 1024 + seg * 512:m * 1024 + (seg + 1) * 512]
                    TMa = self.TMP[0] if seg == 0 else self.TMP[2]
                    t_i0 = p.op("vector", lambda v, hsl=hsl, m=m, l=l, pa=pa,
                                TMa=TMa: v.scalar_tensor_tensor(
                                    TMa[:, :], pa[:, :],
                                    MOD[:, m, 2 * l:2 * l + 1], hsl,
                                    ALU.mult, ALU.add),
                                deps=[t_mm, self.t_mod], sig="ve")
                    t_i1 = p.op("vector", lambda v, m=m, l=l, pb=pb, TMa=TMa,
                                hsl=hsl: v.scalar_tensor_tensor(
                                    hsl, pb[:, :],
                                    MOD[:, m, 2 * l + 1:2 * l + 2], TMa[:, :],
                                    ALU.mult, ALU.add),
                                deps=[t_i0], sig="ve")
                    t_ev = p.op("vector", lambda v, hsl=hsl, m=m, l=l:
                                v.tensor_scalar_add(
                                    hsl, hsl, BPE[:, m, l:l + 1]),
                                deps=[t_i1, self.t_bpe], sig="ve")
                t_path[seg] = t_ev
            if self.dbg and l == 0:
                tdmp = p.dma("sync", d["d_hpath0"].rearrange("(c p) n -> p c n", p=128),
                             self.H.rearrange("p (c n) -> p c n", c=DC),
                             deps=[t_path[0], t_path[1]], sig="dso")
                self.out_toks.append(tdmp)
                p.wait("tensor", tdmp)
            t_stage = [[self._ln_seg(0, t_path[0], 4 + l, 6 + l)],
                       [self._ln_seg(1, t_path[1], 4 + l, 6 + l)]]
            if self.dbg and l == 0:
                tdmp = p.dma("sync", d["d_hln40"].rearrange("(c p) n -> p c n", p=128),
                             self.H.rearrange("p (c n) -> p c n", c=DC),
                             deps=[t_stage[0][0], t_stage[1][0]], sig="dso")
                self.out_toks.append(tdmp)
                p.wait("tensor", tdmp)
        t_last = None
        for c in range(DC):
            t_last = p.op("gpsimd", lambda g, c=c: g.tensor_add(
                self.X1[:, c * 1024:(c + 1) * 1024],
                self.X1[:, c * 1024:(c + 1) * 1024],
                self.H[:, c * 1024:(c + 1) * 1024]),
                deps=[self.lnc[0][c], self.lnc[1][c]] + ([t_last] if t_last else []),
                sig="ge")
            t_o = p.dma("sync", d["x2"][c * 128:(c + 1) * 128, :],
                        self.X1[:, c * 1024:(c + 1) * 1024],
                        deps=[t_last], sig="dso")
            self.out_toks.append(t_o)


from concourse.bass_utils import run_bass_kernel_spmd

_CACHE = {}
USE_FUSED = True


def build_fused():
    b1 = L1Build(with_l1=True, fused=True)
    with contextlib.ExitStack() as st:
        b1._build_in(st, finalize=False)
        p, d = b1.p, b1.d
        fl = b1.ar.view(b1.ar.region_off["SMC"], F32, 4)
        p.op("vector", lambda v: v.memset(fl[0:1, 0:1], 0.0), sig="ve")
        p.op("scalar", lambda s: s.activation(fl[0:1, 1:2], fl[0:1, 0:1],
                                              AF.Copy), sig="se")
        p.op("gpsimd", lambda g: g.memset(fl[0:1, 2:3], 0.0), sig="ge")
        p.phase_barrier()
        b1.ar.reset_zero()
        Pl = list(b1.P)
        Pl[3] = b1.P[3][:, 0:1024].bitcast(F32)
        b2 = L2Build(fused_io={
            "nc": b1.nc, "p": p,
            "x1": d["x1"], "h0": d["h0"], "q1": d["q1"],
            "kvsum": d["kvsum"], "weffg": d["weffg"], "beffg": d["beffg"],
            "cc_kv": b1.cc_kv, "cc_wf": b1.cc_wf, "cc_bf": b1.cc_bf})
        b2.ar = b1.ar
        b2.P = Pl
        return b2._build_in(finalize=True)


def _builders():
    if "b" not in _CACHE:
        b1 = L1Build(with_l1=True)
        nc1 = b1.build()
        b2 = L2Build()
        nc2 = b2.build()
        _CACHE["b"] = (nc1, nc2)
    return _CACHE["b"]


def _builder_fused():
    if "f" not in _CACHE:
        _CACHE["f"] = build_fused()
    return _CACHE["f"]


def host_inputs_fused(I):
    C = np.ascontiguousarray
    _ = I["L0_ffn_gate_w"]
    maps = host_inputs_l0(I, with_l1=True)
    pw = np.asarray(I["L1_ffn_path_w"])
    pw_t = np.stack([np.stack([tile_proj(pw[l, j], DC, DC)
                               for j in range(2)]) for l in range(2)])
    extra = {
        "Wo1": tile_proj(np.asarray(I["L1_attn_Wo"]), DC, DC),
        "bo": np.asarray(I["L1_attn_bo"]),
        "pathw": pw_t,
        "pathb": np.asarray(I["L1_ffn_path_b"]),
        "phase": np.asarray(I["L1_ffn_phase"]),
        "ln3g": np.asarray(I["L1_ffn_ln3_g"]),
        "ln3b": np.asarray(I["L1_ffn_ln3_b"]),
        "ln4g": np.asarray(I["L1_ffn_ln4_g"]),
        "ln4b": np.asarray(I["L1_ffn_ln4_b"]),
    }
    for m in maps:
        m.update(extra)
    return maps


def kernel(**inputs):
    C = np.ascontiguousarray
    if USE_FUSED:
        nc = _builder_fused()
        maps = host_inputs_fused(inputs)
        res = run_bass_kernel_spmd(nc, maps, core_ids=list(range(8)))
        out = np.empty((2, 4096, D), np.float32)
        for c in range(8):
            b, t0 = c // 4, (c % 4) * TOK
            out[b, t0:t0 + TOK] = res.results[c]["x2"].T
        return out
    nc1, nc2 = _builders()
    maps1 = host_inputs_l0(inputs, with_l1=True)
    res1 = run_bass_kernel_spmd(nc1, maps1, core_ids=list(range(8)))
    r1 = res1.results
    weff = np.concatenate([r1[c]["weffp"] for c in range(8)], axis=2)
    beff = np.concatenate([r1[c]["beffp"] for c in range(8)], axis=1)
    pw = np.asarray(inputs["L1_ffn_path_w"])
    pw_t = np.stack([np.stack([tile_proj(pw[l, j], DC, DC)
                               for j in range(2)]) for l in range(2)])
    maps2 = []
    for c in range(8):
        g = (c // 4) * 4
        maps2.append({
            "x1": r1[c]["x1"], "h0": r1[c]["h0"], "q1": r1[c]["q1"],
            "kvps": np.stack([r1[g + i]["kvp"] for i in range(4)]),
            "weff": weff, "beff": beff,
            "Wo1": tile_proj(C(np.asarray(inputs["L1_attn_Wo"])), DC, DC),
            "bo": np.asarray(inputs["L1_attn_bo"]),
            "pathw": pw_t,
            "pathb": np.asarray(inputs["L1_ffn_path_b"]),
            "phase": np.asarray(inputs["L1_ffn_phase"]),
            "ln3g": np.asarray(inputs["L1_ffn_ln3_g"]),
            "ln3b": np.asarray(inputs["L1_ffn_ln3_b"]),
            "ln4g": np.asarray(inputs["L1_ffn_ln4_g"]),
            "ln4b": np.asarray(inputs["L1_ffn_ln4_b"]),
        })
    res2 = run_bass_kernel_spmd(nc2, maps2, core_ids=list(range(8)))
    out = np.empty((2, 4096, D), np.float32)
    for c in range(8):
        b, t0 = c // 4, (c % 4) * TOK
        out[b, t0:t0 + TOK] = res2.results[c]["x2"].T
    return out


# version 12
# speedup vs baseline: 1.0059x; 1.0059x over previous
import contextlib
import math
import numpy as np
import ml_dtypes

import concourse.bass as bass
import concourse.mybir as mybir

F32 = mybir.dt.float32
BF16 = mybir.dt.bfloat16
U8 = mybir.dt.uint8
AF = mybir.ActivationFunctionType
ALU = mybir.AluOpType
AX = mybir.AxisListType
BF = ml_dtypes.bfloat16

ENGS = ("tensor", "vector", "scalar", "gpsimd", "sync")

class Prog:
    def __init__(self, nc):
        self.nc = nc
        self.actions = {e: [] for e in ENGS}
        self.semnames = []
        self.sems = {}
        self.count = {}
        self.watermark = {e: {} for e in ENGS}

    def sem(self, name):
        if name not in self.count:
            self.semnames.append(name)
            self.count[name] = 0
        return name

    def _bump(self, sem, amt):
        self.sem(sem)
        self.count[sem] += amt
        return (sem, self.count[sem])

    def wait(self, eng, tok):
        if tok is None:
            return
        sem, val = tok
        if self.watermark[eng].get(sem, 0) >= val:
            return
        self.watermark[eng][sem] = val
        self.actions[eng].append(("wait", sem, val))

    def op(self, eng, fn, deps=(), sig=None, sig_amt=1):
        for d in deps:
            self.wait(eng, d)
        tok = None
        if sig is not None:
            tok = self._bump(sig, sig_amt)
            self.actions[eng].append(("op_sig", fn, tok[0], sig_amt))
        else:
            self.actions[eng].append(("op", fn))
        return tok

    def dma(self, eng, out, in_, deps=(), sig=None, **kw):
        return self.op(eng, lambda e: e.dma_start(out=out, in_=in_, **kw),
                       deps=deps, sig=sig, sig_amt=16)

    def cc(self, fn, deps=(), sig=None):
        for d in deps:
            self.wait("gpsimd", d)
        tok = self._bump(sig, 1)
        self.actions["gpsimd"].append(("op_sig_def", fn, tok[0]))
        return tok

    def phase_barrier(self):
        snap = [(s, v) for s, v in self.count.items() if v > 0]
        for eng in ENGS:
            for s, v in snap:
                self.wait(eng, (s, v))

    def build(self):
        nc = self.nc
        with contextlib.ExitStack() as st:
            for name in self.semnames:
                self.sems[name] = st.enter_context(nc.semaphore(name))
            with nc.Block() as block:
                for eng in ENGS:
                    acts = self.actions[eng]
                    if not acts:
                        continue
                    def runner(e, acts=acts):
                        for a in acts:
                            if a[0] == "wait":
                                e.wait_ge(self.sems[a[1]], a[2])
                            elif a[0] == "op":
                                a[1](e)
                            elif a[0] == "op_sig_def":
                                a[1](e).then_inc(self.sems[a[2]])
                            else:
                                a[1](e).then_inc(self.sems[a[2]], a[3])
                    getattr(block, eng)(runner)
        return nc


class Arena:
    def __init__(self, nc, nbytes):
        self.nc = nc
        self.nbytes = nbytes
        self.t = None
        self.off = 0
        self.marks = {}

    def alloc(self, pbytes, dtype, free_elems, name=""):
        pbytes = (pbytes + 63) & ~63
        self.region_off = getattr(self, "region_off", {})
        self.region_off[name] = self.off
        assert self.off + pbytes <= self.nbytes, \
            f"arena overflow: {name} off={self.off} + {pbytes} > {self.nbytes}"
        esz = mybir.dt.size(dtype)
        assert free_elems * esz <= pbytes
        ap = self.t[:, self.off:self.off + free_elems * esz].bitcast(dtype)
        self.off += pbytes
        return ap

    def view(self, off, dtype, free_elems):
        esz = mybir.dt.size(dtype)
        return self.t[:, off:off + free_elems * esz].bitcast(dtype)

    def mark(self, tag):
        self.marks[tag] = self.off

    def reset(self, tag):
        self.off = self.marks[tag]

    def reset_zero(self):
        self.off = 0


TOK = 1024
HALO = 512
TOT = TOK + HALO
NT = TOK // 128
NTT = TOT // 128
D = 2048
DC = D // 128
H0, DH0, W0 = 8, 128, 512
WIN = 640
F1 = 8192
FC = F1 // 128
EPS = 1e-5


def ceil(a, b):
    return -(-a // b)


PI = math.pi


class L1Build:

    def __init__(self, with_l1=True, dbg=False, stop_after=None, fused=False):
        nc = bass.Bass("TRN2", target_bir_lowering=False)
        self.nc = nc
        self.p = Prog(nc)
        self.with_l1 = with_l1
        self.dbg = dbg
        self.fused = fused
        self.stop_after = stop_after
        d = {}
        def di(name, shape, dt=F32):
            d[name] = nc.dram_tensor(name, shape, dt, kind="ExternalInput").ap()
        def do(name, shape, dt=F32):
            kind = "Internal" if fused else "ExternalOutput"
            d[name] = nc.dram_tensor(name, shape, dt, kind=kind).ap()
        di("xT", [D, TOT]); di("masks", [5, 128, WIN])
        di("WqT", [D, 1024]); di("WkvT", [D, 2048]); di("WgT", [D, H0])
        di("bg", [H0]); di("qsc", [128]); di("ksc", [128]); di("WoT", [1024, D])
        di("p1T", [FC // 2, 1, 128, DC, 256]); di("b1", [F1])
        di("p2T", [DC, 2, 128, 32, 128]); di("b2", [D])
        do("x1", [D, TOK])
        if with_l1:
            for wn in ("Wq1", "Wk1", "Wv1"):
                di(wn, [DC, 1, 128, DC, 128])
            di("ampw", [DC, 1, 128, DC, 128]); di("ampb", [D])
            di("phw", [DC, 1, 128, DC, 128]); di("phb", [D])
            di("entwT0", [DC, 2, 128, 32, 128]); di("entwT1", [DC, 2, 128, 32, 128])
            di("entpw0", [F1, 256]); di("entpw1", [F1, 256])
            di("entb", [2, F1]); di("entpb", [2, 256])
            do("q1", [D, TOK], BF16); do("kvp", [128, 1056])
            if fused:
                do("weffp", [2 * D, 256], BF16)
            else:
                do("weffp", [2, D, 256], BF16)
            do("beffp", [2, 256]); do("h0", [D, TOK])
        if fused:
            d["kvsum"] = nc.dram_tensor("kvsum", [128, 1056], F32).ap()
            d["weffg"] = nc.dram_tensor("weffg", [8 * 2 * D, 256], BF16).ap()
            d["beffg"] = nc.dram_tensor("beffg", [16, 256], F32).ap()
        if dbg:
            do("d_xn", [128, DC * TOK], BF16)
            do("d_xnh", [128, DC * HALO], BF16)
            do("d_kfm", [128, H0 * TOT], BF16)
            do("d_vtm", [128, NTT * 1024], BF16)
            do("d_qfm", [128, H0 * TOK], BF16)
            do("d_gtm", [128, 64])
            do("d_ofm", [128, H0 * TOK], BF16)
            do("d_xattn", [D, TOK])
        self.d = d
        self.xattn = nc.dram_tensor("xattn", [D, TOK], F32).ap()
        self.out_toks = []

    def proj_std(self, whbm, kc, mc, rhs_fn, evac_fn, wb, pair, deps0=(),
                 dma_split=1, bias_row=None, bias_dep=None, mw=1, tiled=False):
        p = self.p
        ntok = rhs_fn(0).shape[-1]
        nseg = ceil(ntok, 512)
        assert nseg <= 2
        t_w = [None, None]
        t_cons = [None, None]
        t_ev = None
        kc_spl = kc // dma_split
        for mg in range(mc // mw):
            b = mg % 2
            dd = list(deps0) + ([t_cons[b]] if t_cons[b] else [])
            for s in range(dma_split):
                src = (whbm[mg, s] if tiled else
                       whbm[s * kc_spl * 128:(s + 1) * kc_spl * 128,
                            mg * mw * 128:(mg + 1) * mw * 128].rearrange(
                           "(c p) n -> p c n", p=128))
                t_w[b] = p.dma(
                    "gpsimd",
                    wb[b][:, s * kc_spl * 128 * mw:(s + 1) * kc_spl * 128 * mw
                          ].rearrange("p (c n) -> p c n", c=kc_spl),
                    src, deps=dd, sig=f"dgp{b}")
            for j in range(mw):
                m = mg * mw + j
                def mm(t, m=m, b=b, j=j):
                    last = None
                    for k in range(kc):
                        lhs = wb[b][:, (k * mw + j) * 128:(k * mw + j + 1) * 128]
                        for s in range(nseg):
                            n = min(512, ntok - s * 512)
                            last = t.matmul(pair[s if nseg == 2 else m % 2][:, 0:n],
                                            lhs, rhs_fn(k)[:, s * 512:s * 512 + n],
                                            start=(k == 0),
                                            stop=(k == kc - 1 and bias_row is None))
                    if bias_row is not None:
                        for s in range(nseg):
                            n = min(512, ntok - s * 512)
                            last = t.matmul(pair[s if nseg == 2 else m % 2][:, 0:n],
                                            bias_row[0:1, m * 128:(m + 1) * 128],
                                            self.CONB[0:1, 512:512 + n],
                                            start=False, stop=True)
                    return last
                dd = list(deps0) + [t_w[b]] + ([t_ev] if t_ev else [])
                if bias_dep is not None:
                    dd.append(bias_dep)
                t_mm = p.op("tensor", mm, deps=dd, sig="pe")
                t_cons[b] = t_mm
                t_ev = evac_fn(m, t_mm)
        return t_ev

    def build(self, st=None):
        if st is None:
            with contextlib.ExitStack() as st_:
                return self._build_in(st_, finalize=True)
        return self._build_in(st, finalize=False)

    def _build_in(self, st, finalize):
        nc, p, d = self.nc, self.p, self.d
        if True:
            ar = Arena(nc, 212600)
            ar.t = st.enter_context(nc.sbuf_tensor("AR", [128, 212600], U8))
            self.ar = ar
            P = []
            for i in range(8):
                dt = BF16 if i == 3 else F32
                n = 1024 if i == 3 else 512
                P.append(st.enter_context(nc.psum_tensor(f"P{i}", [128, n], dt)))
            self.P = P
            self.CON = ar.alloc(2048, F32, 512, "CON")
            self.CONB = ar.alloc(2048, BF16, 1024, "CONB")
            self.IDENT = ar.alloc(256, BF16, 128, "IDENT")
            self.MASKS = ar.alloc(5 * WIN * 2, BF16, 5 * WIN, "MASKS")
            self.SQ = ar.alloc(6144, F32, 1536, "SQ")
            self.ROWS = ar.alloc(4096, F32, 1024, "ROWS")
            self.STR = [ar.alloc(6144, F32, 1536, f"STR{i}") for i in range(2)]
            self.RQK = ar.alloc(256, F32, 64, "RQK")
            self.GTM = ar.alloc(256, F32, 64, "GTM")
            self.GROW = ar.alloc(2048, BF16, 1024, "GROW")
            self.SMC = ar.alloc(256, F32, 64, "SMC")
            self.XNO = ar.alloc(DC * TOK * 2, BF16, DC * TOK, "XNO")
            ar.mark("base")
            self.t_p3 = None
            self.t_rr_prev = None
            self.t_rqk_read = None
            self.t_tp_prev = None
            self._consts()
            ar.mark("blk")
            self.VTM = ar.alloc(NTT * 1024 * 2, BF16, NTT * 1024, "VTM")
            self.QFM = ar.alloc(H0 * TOK * 2, BF16, H0 * TOK, "QFM")
            self.KFM = ar.alloc(H0 * TOT * 2, BF16, H0 * TOT, "KFM")
            self.OFM = ar.alloc(H0 * TOK * 2, BF16, H0 * TOK, "OFM")
            self.QKT = ar.alloc(1024 * 2, BF16, 1024, "QKT")
            ar.mark("l0a")
            self.XNH = ar.alloc(DC * HALO * 2, BF16, DC * HALO, "XNH")
            self.WBUF = ar.alloc(16 * 1024 * 2, BF16, 16 * 1024, "WBUF")
            self._ln1()
            if self.stop_after == "A":
                ot = [p.dma("sync", d["d_xn"], self.XNO, deps=self.t_xn, sig="dso"),
                      p.dma("sync", d["d_xnh"], self.XNH, deps=self.t_xn, sig="dso")]
                self.out_toks += ot
                for tk in self.out_toks:
                    p.wait("sync", tk)
                p.build()
                return nc
            self._kv()
            self._q_gates()
            if self.dbg:
                ot = [p.dma("sync", d["d_xn"], self.XNO, deps=self.t_xn, sig="dso"),
                      p.dma("sync", d["d_xnh"], self.XNH, deps=self.t_xn, sig="dso"),
                      p.dma("sync", d["d_kfm"], self.KFM, deps=self.t_kv, sig="dso"),
                      p.dma("sync", d["d_vtm"], self.VTM, deps=self.t_kv, sig="dso"),
                      p.dma("sync", d["d_qfm"], self.QFM, deps=[self.t_q], sig="dso"),
                      p.dma("sync", d["d_gtm"], self.GTM[:, 0:64],
                            deps=[self.t_gates], sig="dso")]
                self.out_toks += ot
            if self.stop_after == "C":
                for tk in self.out_toks:
                    p.wait("sync", tk)
                p.build()
                return nc
            ar.reset("l0a")
            self.EXPT = ar.alloc(WIN * 4, F32, WIN, "EXPT")
            self.EXPE = ar.alloc(WIN * 4, F32, WIN, "EXPE")
            self.ATTB = ar.alloc(WIN * 2, BF16, WIN, "ATTB")
            self.ATTT = ar.alloc(WIN * 2, BF16, WIN, "ATTT")
            self.WOB = [ar.alloc(8 * 128 * 2, BF16, 8 * 128, f"WOB{i}") for i in range(2)]
            self.XSA = [ar.alloc(2048, F32, 512, f"XSA{i}") for i in range(4)]
            self._attention()
            if self.dbg:
                self.out_toks += [
                    p.dma("sync", d["d_ofm"], self.OFM,
                          deps=[self.t_av_done[qt] for qt in range(NT)], sig="dso"),
                    p.dma("sync", d["d_xattn"], self.xattn, deps=self.t_wo, sig="dso")]
            ar.reset("blk")
            self.H = ar.alloc(FC * 512 * 2, BF16, FC * 512, "H")
            self.W1B = [ar.alloc(16 * 256 * 2, BF16, 16 * 256, f"W1B{i}")
                        for i in range(2)]
            self.W2B = [ar.alloc(FC * 128 * 2, BF16, FC * 128, f"W2B{i}")
                        for i in range(3)]
            self.XS = [ar.alloc(2048, F32, 512, f"XS{i}") for i in range(4)]
            self.B1R = ar.view(ar.region_off["MASKS"], BF16, F1)
            self.B2R = ar.view(ar.region_off["STR0"], BF16, D)
            self._ffn()
            if self.with_l1:
                ar.reset("blk")
                self.ENTPW = ar.alloc(FC * 256 * 2, BF16, FC * 256, "ENTPW")
                self.WTP = [ar.alloc(FC * 128 * 2, BF16, FC * 128, f"WTP{i}")
                            for i in range(3)]
                self.WEFFST = ar.alloc(16 * 256 * 2, BF16, 16 * 256, "WEFFST")
                self.ENTB = ar.alloc(2 * 64 * 2, BF16, 2 * 64, "ENTBC")
                self.ENTBRAW = ar.alloc(256, BF16, 128, "ENTBRAW")
                self.EPB = ar.alloc(4096, F32, 1024, "EPB")
                self._ln2_weff()
                if self.fused:
                    self._issue_cc_weff()
                ar.reset("blk")
                self.Q1FM = ar.alloc(DC * TOK * 2, BF16, DC * TOK, "Q1FM")
                self.K1FM = ar.alloc(DC * TOK * 2, BF16, DC * TOK, "K1FM")
                self.V1FM = ar.alloc(DC * TOK * 2, BF16, DC * TOK, "V1FM")
                self.WP = [ar.alloc(DC * 128 * 2, BF16, DC * 128, f"WP{i}")
                           for i in range(2)]
                self.K1T = ar.alloc(D * 2, BF16, D, "K1T")
                self.V1T = ar.alloc(32 * 65 * 2, BF16, 32 * 65, "V1T")
                self.KVST = ar.alloc(1056 * 4, F32, 1056, "KVST")
                self._l1_proj()
                if self.fused:
                    self._issue_cc_kv()
                ar.reset("blk")
                self.TH = ar.alloc(DC * TOK * 4, F32, DC * TOK, "TH")
                self.AMP = ar.alloc(DC * TOK * 4, F32, DC * TOK, "AMP")
                self.WPK = [ar.alloc(DC * 128 * 2, BF16, DC * 128, f"WPK{i}")
                            for i in range(2)]
                self.SC = ar.view(ar.region_off["XNO"], F32, 2 * 1024)
                self._amp_ph()
            if not finalize:
                return nc
            for tk in self.out_toks:
                p.wait("sync", tk)
            p.build()
        return nc

    def _consts(self):
        p, d = self.p, self.d
        CON, CONB = self.CON, self.CONB
        t0 = p.op("vector", lambda v: v.memset(CON[:, 0:1], 1.0), sig="ve")
        self.t_ones = p.op("vector", lambda v: v.memset(CON[0:1, 64:192], 1.0), sig="ve")
        self.t_onesb = p.op("vector", lambda v: v.memset(CONB[:, :], 1.0), sig="ve")
        t_ms = p.op("gpsimd", lambda g: g.memset(self.IDENT[:, :], 0.0),
                    sig="ge")
        self.t_ident = p.op("gpsimd", lambda g: g.affine_select(
            out=self.IDENT[:, :], in_=self.IDENT[:, :], compare_op=ALU.not_equal,
            fill=1.0, base=0, pattern=[[-1, 128]], channel_multiplier=1),
            deps=[t_ms], sig="ge")
        self.t_masks = p.dma("gpsimd", self.MASKS.rearrange("p (s j) -> p s j", s=5),
                             d["masks"].rearrange("s p j -> p s j"), sig="dgm")
        tq = p.dma("gpsimd", CON[0:1, 256:384],
                   d["qsc"].rearrange("(one n) -> one n", one=1), sig="dgq")
        tk = p.dma("gpsimd", CON[0:1, 384:512],
                   d["ksc"].rearrange("(one n) -> one n", one=1), sig="dgq")
        tm = p.op("vector", lambda v: v.tensor_mul(CON[0:1, 256:384],
                                                   CON[0:1, 256:384],
                                                   CON[0:1, 384:512]),
                  deps=[tk], sig="ve")
        tm2 = p.op("vector", lambda v: v.tensor_scalar_mul(
            CON[0:1, 256:384], CON[0:1, 256:384], 8.0), deps=[tm], sig="ve")
        tmm = p.op("tensor", lambda t: t.matmul(
            self.P[2][:, 0:1], CON[0:1, 256:384], CON[0:1, 0:1],
            start=True, stop=True), deps=[tm2, t0], sig="pe")
        self.t_qks = p.op("scalar", lambda sc: sc.activation(
            CON[:, 1:2], self.P[2][:, 0:1], AF.Copy), deps=[tmm], sig="se")
        self.t_bg = p.dma("gpsimd", CONB[0:1, 264:272],
                          d["bg"].rearrange("(one n) -> one n", one=1),
                          deps=[self.t_onesb], sig="dgv")

    def _ln(self, src_fn, ngrp, out_fn, sums_p, sqs_p, tag, bc_pair=None,
            deps0=()):
        p, P = self.p, self.P
        ones = self.CON[:, 0:1]
        width = ngrp * 512
        t_dma = [None, None]
        t_pe = [None, None]
        t_sq_prev = None
        for c in range(DC):
            b = c % 2
            t_dma[b] = src_fn(c, b, 0, width,
                              [t_pe[b]] if t_pe[b] else [])
            t_sq = p.op("scalar", lambda s, b=b, width=width: s.activation(
                self.SQ[:, 0:width], self.STR[b][:, 0:width], AF.Square),
                deps=[t_dma[b]] + ([t_sq_prev] if t_sq_prev else []), sig="se")
            def mms(t, c=c, b=b):
                last = None
                for g in range(ngrp):
                    last = t.matmul(sums_p[g][0:1, :], ones,
                                    self.STR[b][:, g * 512:(g + 1) * 512],
                                    start=(c == 0), stop=(c == DC - 1))
                for g in range(ngrp):
                    last = t.matmul(sqs_p[g][0:1, :], ones,
                                    self.SQ[:, g * 512:(g + 1) * 512],
                                    start=(c == 0), stop=(c == DC - 1))
                return last
            t_pe[b] = p.op("tensor", mms,
                           deps=[t_dma[b], t_sq] + list(deps0), sig="pe")
            t_sq_prev = t_pe[b]
        t_stats = t_pe[(DC - 1) % 2]
        t_last = None
        t_bc_prev = None
        for g in range(ngrp):
            SU, SS = sums_p[g], sqs_p[g]
            t0 = p.op("vector", lambda v, SU=SU: v.tensor_scalar_mul(
                self.ROWS[0:1, 0:512], SU[0:1, :], 1.0 / D),
                deps=[t_stats] + ([t_bc_prev] if t_bc_prev else []), sig="ve")
            ta = p.op("vector", lambda v, SS=SS: v.tensor_scalar_mul(
                self.ROWS[0:1, 512:1024], SS[0:1, :], 1.0 / D), sig="ve")
            tb = p.op("vector", lambda v: v.tensor_mul(
                self.SQ[0:1, 0:512], self.ROWS[0:1, 0:512], self.ROWS[0:1, 0:512]),
                deps=[t0], sig="ve")
            tc = p.op("vector", lambda v: v.tensor_sub(
                self.ROWS[0:1, 512:1024], self.ROWS[0:1, 512:1024],
                self.SQ[0:1, 0:512]), deps=[ta, tb], sig="ve")
            t2 = p.op("vector", lambda v: v.tensor_scalar_add(
                self.ROWS[0:1, 512:1024], self.ROWS[0:1, 512:1024], EPS),
                deps=[tc], sig="ve")
            t3 = p.op("scalar", lambda s: s.activation(
                self.SQ[0:1, 0:512], self.ROWS[0:1, 512:1024], AF.Ln),
                deps=[t2], sig="se")
            t4 = p.op("scalar", lambda s: s.activation(
                self.ROWS[0:1, 512:1024], self.SQ[0:1, 0:512], AF.Exp, scale=-0.5),
                deps=[t3], sig="se")
            if hasattr(self, "RD"):
                p.op("vector", lambda v, g=g: v.tensor_copy(
                    self.RD[0:1, g * 1024:(g + 1) * 1024], self.ROWS[0:1, 0:1024]),
                    deps=[t4], sig="ve")
            BA, BB = SU, SS
            def bc(t, BA=BA, BB=BB):
                t.matmul(BA[:, :], self.CON[0:1, 64:192], self.ROWS[0:1, 0:512],
                         start=True, stop=True)
                return t.matmul(BB[:, :], self.CON[0:1, 64:192],
                                self.ROWS[0:1, 512:1024], start=True, stop=True)
            t_bc = p.op("tensor", bc, deps=[t0, t4, self.t_ones], sig="pe")
            t_bc_prev = t_bc
            t_ap = [None, None]
            for c in range(DC):
                b = c % 2
                t_x = src_fn(c, b, g * 512, 512,
                             ([t_ap[b]] if t_ap[b] else []) + [t_bc])
                t_sub = p.op("vector", lambda v, b=b, g=g, BA=BA: v.tensor_sub(
                    self.STR[b][:, 1024:1536], self.STR[b][:, g * 512:g * 512 + 512],
                    BA[:, :]), deps=[t_x, t_bc], sig="ve")
                t_ap[b] = p.op("vector", lambda v, b=b, BB=BB, c=c, g=g: v.tensor_mul(
                    out_fn(c, g), self.STR[b][:, 1024:1536], BB[:, :]),
                    deps=[t_sub], sig="ve")
            t_last = [t_ap[0], t_ap[1], t_bc]
        return t_last

    def _ln1(self):
        p, d, P = self.p, self.d, self.P

        def src(c, b, lo, n, deps):
            return p.dma("sync", self.STR[b][:, lo:lo + n],
                         d["xT"][c * 128:(c + 1) * 128, lo:lo + n],
                         deps=deps, sig=f"dsx{b}")

        def out(c, g):
            if g == 0:
                return self.XNH[:, c * HALO:c * HALO + 512]
            return self.XNO[:, c * TOK + (g - 1) * 512:c * TOK + g * 512]

        toks = self._ln(src, 3, out, [P[2], P[6], P[7]], [P[0], P[1], P[5]], "ln1",
                        bc_pair=(P[2], P[0]), deps0=[self.t_qks])
        self.t_xn = toks[:2]
        self.t_ln1_pe = toks[2]

    def xn_tile(self, c, m):
        if m < 4:
            return self.XNH[:, c * HALO + m * 128:c * HALO + (m + 1) * 128]
        return self.XNO[:, c * TOK + (m - 4) * 128:c * TOK + (m - 3) * 128]

    def _kv(self):
        p, d, P = self.p, self.d, self.P
        t_w = [None] * DC
        t_free = {}
        t_mm_last = None
        KFM3 = self.KFM.rearrange("p (h t) -> p h t", h=8)
        for half in range(2):
            for c in range(DC):
                dep = [t_mm_last] if (half == 1 and c == 0) else []
                t_w[c] = p.dma("gpsimd", self.WBUF[:, c * 1024:(c + 1) * 1024],
                               d["WkvT"][c * 128:(c + 1) * 128,
                                         half * 1024:(half + 1) * 1024],
                               deps=dep, sig="dgk")
            for m in range(NTT):
                u = half * NTT + m
                pa, pb = [(P[0], P[1]), (P[6], P[7]), (P[4], P[5])][u % 3]
                def mmk(t, m=m, pa=pa, pb=pb):
                    last = None
                    for c in range(DC):
                        lhs = self.xn_tile(c, m)
                        t.matmul(pa[:, :], lhs, self.WBUF[:, c * 1024:c * 1024 + 512],
                                 start=(c == 0), stop=(c == DC - 1))
                        last = t.matmul(pb[:, :], lhs,
                                        self.WBUF[:, c * 1024 + 512:(c + 1) * 1024],
                                        start=(c == 0), stop=(c == DC - 1))
                    return last
                deps = [t_w[DC - 1]] + self.t_xn
                if t_free.get(u - 3) is not None:
                    deps.append(t_free[u - 3])
                t_mm = p.op("tensor", mmk, deps=deps, sig="pe")
                t_mm_last = t_mm
                if half == 1:
                    tv1 = p.op("vector", lambda v, m=m, pa=pa: v.tensor_copy(
                        self.VTM[:, m * 1024:m * 1024 + 512], pa[:, :]), deps=[t_mm],
                        sig="ve")
                    t_free[u] = p.op("vector", lambda v, m=m, pb=pb: v.tensor_copy(
                        self.VTM[:, m * 1024 + 512:(m + 1) * 1024], pb[:, :]),
                        deps=[tv1], sig="ve")
                else:
                    t_s1 = p.op("scalar", lambda s, pa=pa: s.activation(
                        self.SQ[:, 0:512], pa[:, :], AF.Square),
                        deps=[t_mm] + ([self.t_rr_prev] if self.t_rr_prev else []),
                        sig="se")
                    t_s2 = p.op("scalar", lambda s, pb=pb: s.activation(
                        self.SQ[:, 512:1024], pb[:, :], AF.Square), sig="se")
                    t_rr = p.op("vector", lambda v: v.reduce_sum(
                        self.RQK[:, 0:8],
                        self.SQ[:, 0:1024].rearrange("p (h e) -> p h e", h=8),
                        axis=AX.X),
                        deps=[t_s1, t_s2] + ([self.t_rqk_read]
                                             if self.t_rqk_read else []), sig="ve")
                    t_cl = p.op("vector", lambda v: v.tensor_scalar_max(
                        self.RQK[:, 0:8], self.RQK[:, 0:8], 1e-24),
                        deps=[t_rr], sig="ve")
                    self.t_rr_prev = t_rr
                    t_l = p.op("scalar", lambda s: s.activation(
                        self.RQK[:, 8:16], self.RQK[:, 0:8], AF.Ln),
                        deps=[t_cl], sig="se")
                    t_e = p.op("scalar", lambda s: s.activation(
                        self.RQK[:, 0:8], self.RQK[:, 8:16], AF.Exp, scale=-0.5),
                        deps=[t_l], sig="se")
                    tk2 = None
                    for h in range(8):
                        ps = pa if h < 4 else pb
                        dd = [t_e] + ([self.t_tp_prev]
                                      if (h == 0 and self.t_tp_prev) else [])
                        tk2 = p.op("vector", lambda v, h=h, ps=ps: v.tensor_scalar_mul(
                            self.QKT[:, h * 128:(h + 1) * 128],
                            ps[:, (h % 4) * 128:(h % 4 + 1) * 128],
                            self.RQK[:, h:h + 1]), deps=dd,
                            sig="ve" if h == 7 else None)
                    t_free[u] = tk2
                    def tps(t, m=m):
                        last = None
                        for h in range(8):
                            last = t.transpose(P[3][:, h * 128:(h + 1) * 128],
                                               self.QKT[:, h * 128:(h + 1) * 128],
                                               self.IDENT[:, :])
                        return last
                    t_tp = p.op("tensor", tps,
                                deps=[tk2, self.t_ident, self.t_p3], sig="pe")
                    self.t_tp_prev = t_tp
                    self.t_p3 = p.op(
                        "vector", lambda v, m=m: v.tensor_scalar_mul(
                            KFM3[:, :, m * 128:(m + 1) * 128],
                            P[3][:, 0:1024].rearrange("p (h e) -> p h e", h=8),
                            self.CON[:, 1:2]),
                        deps=[t_tp, self.t_qks], sig="ve")
        self.t_kv = [self.t_p3, t_free[NTT * 2 - 1]]

    def _q_gates(self):
        p, d, P = self.p, self.d, self.P
        QFM3 = self.QFM.rearrange("p (h t) -> p h t", h=8)
        t_w = [None] * DC
        for c in range(DC):
            t_w[c] = p.dma("gpsimd", self.WBUF[:, c * 1024:c * 1024 + 1024],
                           d["WqT"][c * 128:(c + 1) * 128, 0:1024],
                           deps=[self.t_kv[1]] if c == 0 else [], sig="dgk")
        t_free = {}
        for m in range(NT):
            pa, pb = [(P[0], P[1]), (P[6], P[7]), (P[4], P[5])][m % 3]
            def mmq(t, m=m, pa=pa, pb=pb):
                last = None
                for c in range(DC):
                    lhs = self.xn_tile(c, m + 4)
                    t.matmul(pa[:, :], lhs, self.WBUF[:, c * 1024:c * 1024 + 512],
                             start=(c == 0), stop=(c == DC - 1))
                    last = t.matmul(pb[:, :], lhs,
                                    self.WBUF[:, c * 1024 + 512:c * 1024 + 1024],
                                    start=(c == 0), stop=(c == DC - 1))
                return last
            deps = [t_w[DC - 1]]
            if t_free.get(m - 3) is not None:
                deps.append(t_free[m - 3])
            t_mm = p.op("tensor", mmq, deps=deps, sig="pe")
            t_s1 = p.op("scalar", lambda s, pa=pa: s.activation(
                self.SQ[:, 0:512], pa[:, :], AF.Square),
                deps=[t_mm] + ([self.t_rr_prev] if self.t_rr_prev else []), sig="se")
            t_s2 = p.op("scalar", lambda s, pb=pb: s.activation(
                self.SQ[:, 512:1024], pb[:, :], AF.Square), sig="se")
            t_rr = p.op("vector", lambda v: v.reduce_sum(
                self.RQK[:, 0:8],
                self.SQ[:, 0:1024].rearrange("p (h e) -> p h e", h=8),
                axis=AX.X),
                deps=[t_s1, t_s2] + ([self.t_rqk_read]
                                     if self.t_rqk_read else []), sig="ve")
            t_cl = p.op("vector", lambda v: v.tensor_scalar_max(
                self.RQK[:, 0:8], self.RQK[:, 0:8], 1e-24), deps=[t_rr], sig="ve")
            self.t_rr_prev = t_rr
            t_l = p.op("scalar", lambda s: s.activation(
                self.RQK[:, 8:16], self.RQK[:, 0:8], AF.Ln), deps=[t_cl], sig="se")
            t_e = p.op("scalar", lambda s: s.activation(
                self.RQK[:, 0:8], self.RQK[:, 8:16], AF.Exp, scale=-0.5),
                deps=[t_l], sig="se")
            tq = None
            for h in range(8):
                ps = pa if h < 4 else pb
                dd = [t_e] + ([self.t_tp_prev] if (h == 0 and self.t_tp_prev) else [])
                tq = p.op("vector", lambda v, h=h, ps=ps: v.tensor_scalar_mul(
                    self.QKT[:, h * 128:(h + 1) * 128],
                    ps[:, (h % 4) * 128:(h % 4 + 1) * 128],
                    self.RQK[:, h:h + 1]), deps=dd, sig="ve" if h == 7 else None)
            t_free[m] = tq
            self.t_rqk_read = tq
            def tpq(t, m=m):
                last = None
                for h in range(8):
                    last = t.transpose(P[3][:, h * 128:(h + 1) * 128],
                                       self.QKT[:, h * 128:(h + 1) * 128],
                                       self.IDENT[:, :])
                return last
            t_tp = p.op("tensor", tpq, deps=[tq, self.t_p3], sig="pe")
            self.t_tp_prev = t_tp
            self.t_p3 = p.op("scalar", lambda s, m=m: s.activation(
                QFM3[:, :, m * 128:(m + 1) * 128],
                P[3][:, 0:1024].rearrange("p (h e) -> p h e", h=8), AF.Copy),
                deps=[t_tp], sig="se")
        self.t_q = self.t_p3
        WGT = self.CONB[:, 128:256].rearrange("p (c n) -> p c n", c=16)
        t_wg = p.dma("gpsimd", WGT, d["WgT"].rearrange("(c p) n -> p c n", p=128),
                     sig="dgg")
        t_gs = None
        for g in range(2):
            def mmg(t, g=g):
                for k in range(DC):
                    t.matmul(P[2][0:8, :], WGT[:, k, :],
                             self.XNO[:, k * TOK + g * 512:k * TOK + (g + 1) * 512],
                             start=(k == 0), stop=False)
                return t.matmul(P[2][0:8, :], self.CONB[0:1, 264:272],
                                self.CONB[0:1, 512:1024], start=False, stop=True)
            t_g = p.op("tensor", mmg,
                       deps=[t_wg, self.t_bg] + self.t_xn + ([t_gs] if t_gs else []),
                       sig="pe")
            t_gs = p.op("scalar", lambda s, g=g: s.activation(
                self.GROW[0:8, g * 512:(g + 1) * 512], P[2][0:8, :], AF.Sigmoid),
                deps=[t_g], sig="se")
        def tpg(t):
            last = None
            for qt in range(8):
                last = t.transpose(P[3][:, qt * 8:(qt + 1) * 8],
                                   self.GROW[0:8, qt * 128:(qt + 1) * 128],
                                   self.IDENT[0:8, 0:8])
            return last
        t_tpg = p.op("tensor", tpg, deps=[t_gs, self.t_p3], sig="pe")
        self.t_p3 = p.op("scalar", lambda s: s.activation(
            self.GTM[:, 0:64], P[3][:, 0:64], AF.Copy), deps=[t_tpg], sig="se")
        self.t_gates = self.t_p3

    def _attention(self):
        p, d, P = self.p, self.d, self.P
        t_simfree = None
        t_avfree = {}
        t_attb_free = None
        t_attt_free = None
        t_expe_free = None
        self.t_av_done = {}
        t_wo = {"tok": None}
        units = [(qt, h) for qt in range(NT) for h in range(H0)]
        wo_iter = [None] * len(units)
        for i in range(32, 48):
            wo_iter[i] = (0, i - 32)
        self.t_xattn = []
        t_wo_ev = {}
        for u, (qt, h) in enumerate(units):
            slot = qt if qt < 4 else 4
            qs = self.QFM.rearrange("p (hh t) -> p hh t", hh=8)[:, h,
                                                               qt * 128:(qt + 1) * 128]
            ks = self.KFM.rearrange("p (hh t) -> p hh t", hh=8)[:, h,
                                                               qt * 128:qt * 128 + WIN]
            def sim(t, qs=qs, ks=ks):
                t.matmul(P[0][:, :], qs, ks[:, 0:512], start=True, stop=True)
                return t.matmul(P[1][:, 0:128], qs, ks[:, 512:640],
                                start=True, stop=True)
            t_sim = p.op("tensor", sim,
                         deps=[self.t_q, self.t_kv[0], t_simfree], sig="pe")
            t_ma1 = p.op("vector", lambda v, slot=slot: v.tensor_add(
                self.EXPT[:, 0:512], P[0][:, :], self.MASKS.rearrange(
                    "p (s j) -> p s j", s=5)[:, slot, 0:512]),
                deps=[t_sim, self.t_masks, t_expe_free], sig="ve")
            t_simfree = p.op("vector", lambda v, slot=slot: v.tensor_add(
                self.EXPT[:, 512:640], P[1][:, 0:128], self.MASKS.rearrange(
                    "p (s j) -> p s j", s=5)[:, slot, 512:640]), sig="ve")
            t_mx = p.op("vector", lambda v: v.reduce_max(
                self.SMC[:, 0:1], self.EXPT[:, 0:WIN], axis=AX.X, negate=True),
                deps=[t_ma1, t_simfree], sig="ve")
            t_exp = p.op("scalar", lambda s: s.activation(
                self.EXPE[:, 0:WIN], self.EXPT[:, 0:WIN], AF.Exp,
                bias=self.SMC[:, 0:1], accum_out=self.SMC[:, 1:2]),
                deps=[t_mx], sig="se")
            t_rc = p.op("vector", lambda v: v.reciprocal(
                self.SMC[:, 2:3], self.SMC[:, 1:2]), deps=[t_exp], sig="ve")
            t_rg = p.op("vector", lambda v, qt=qt, h=h: v.tensor_mul(
                self.SMC[:, 3:4], self.SMC[:, 2:3],
                self.GTM[:, qt * 8 + h:qt * 8 + h + 1]),
                deps=[self.t_gates, t_rc], sig="ve")
            t_attb = p.op("vector", lambda v: v.tensor_scalar_mul(
                self.ATTB[:, 0:WIN], self.EXPE[:, 0:WIN], self.SMC[:, 3:4]),
                deps=[t_attb_free, t_rg], sig="ve")
            t_expe_free = t_attb
            def tpa(t):
                last = None
                for j in range(5):
                    last = t.transpose(P[3][:, j * 128:(j + 1) * 128],
                                       self.ATTB[:, j * 128:(j + 1) * 128],
                                       self.IDENT[:, :])
                return last
            t_tp = p.op("tensor", tpa, deps=[t_attb, self.t_p3], sig="pe")
            t_attb_free = t_tp
            self.t_p3 = p.op("scalar", lambda s: s.activation(
                self.ATTT[:, 0:WIN], P[3][:, 0:WIN], AF.Copy),
                deps=[t_tp, t_attt_free], sig="se")
            pav = P[2] if u % 2 == 0 else P[6]
            def av(t, qt=qt, h=h, pav=pav):
                last = None
                for j in range(5):
                    last = t.matmul(
                        pav[:, 0:128],
                        self.VTM[:, (qt + j) * 1024 + h * 128:(qt + j) * 1024
                                 + (h + 1) * 128],
                        self.ATTT[:, j * 128:(j + 1) * 128],
                        start=(j == 0), stop=(j == 4))
                return last
            t_av = p.op("tensor", av,
                        deps=[self.t_p3, self.t_kv[1], t_avfree.get(u - 2)],
                        sig="pe")
            t_attt_free = t_av
            t_ave = p.op("scalar", lambda s, qt=qt, h=h, pav=pav: s.activation(
                self.OFM[:, h * 1024 + qt * 128:h * 1024 + (qt + 1) * 128],
                pav[:, 0:128], AF.Copy), deps=[t_av], sig="se")
            t_avfree[u] = t_ave
            self.t_av_done[qt] = t_ave
            if wo_iter[u] is not None:
                self._wo_dc(*wo_iter[u], t_wo_ev)
        for dc in range(16):
            self._wo_dc(1, dc, t_wo_ev)
        self.t_wo_ev = t_wo_ev
        self.t_wo = [t_wo_ev[(1, dc)] for dc in range(16)] + \
                    [t_wo_ev[(0, dc)] for dc in range(16)]

    def _wo_dc(self, half, dc, t_wo_ev):
        p, d, P = self.p, self.d, self.P
        b = dc % 2
        t_wp = p.dma("gpsimd", self.WOB[b].rearrange("p (c n) -> p c n", c=8),
                     d["WoT"][:, dc * 128:(dc + 1) * 128].rearrange(
                         "(c p) n -> p c n", p=128),
                     deps=[t_wo_ev.get((half, dc - 2), None),
                           t_wo_ev.get((half - 1, dc + 14), None)], sig=f"dgp{b}")
        pwo = P[7] if dc % 2 == 0 else P[4]
        def mm(t, half=half, dc=dc, b=b, pwo=pwo):
            last = None
            for k in range(8):
                last = t.matmul(pwo[:, :], self.WOB[b][:, k * 128:(k + 1) * 128],
                                self.OFM[:, k * 1024 + half * 512:k * 1024
                                         + half * 512 + 512],
                                start=(k == 0), stop=(k == 7))
            return last
        deps = [t_wp] + [self.t_av_done[qt] for qt in range(half * 4, half * 4 + 4)]
        ev = t_wo_ev.get((half, dc - 2)) or t_wo_ev.get((half - 1, dc + 14))
        if ev:
            deps.append(ev)
        t_mm = p.op("tensor", mm, deps=deps, sig="pe")
        i = dc % 4
        xsa_war = [t for t in (t_wo_ev.get((half, dc - 4)),
                               t_wo_ev.get((half - 1, dc + 12))) if t]
        t_cp = p.op("scalar", lambda s, pwo=pwo, i=i: s.activation(
            self.XSA[i][:, :], pwo[:, :], AF.Copy), deps=[t_mm] + xsa_war, sig="se")
        t_dma = p.dma("sync", self.xattn[dc * 128:(dc + 1) * 128,
                                         half * 512:half * 512 + 512],
                      self.XSA[i][:, :], deps=[t_cp], sig="dso")
        t_wo_ev[(half, dc)] = t_dma

    def _ffn(self):
        p, d, P = self.p, self.d, self.P
        t_x1 = []
        t_h_free = None
        xs_last = {}
        w2_free = [None, None, None]
        t_b1r = p.dma("gpsimd", self.B1R[0:1, :],
                      d["b1"].rearrange("(one n) -> one n", one=1),
                      deps=self.t_wo, sig="dgb")
        t_b2r = p.dma("gpsimd", self.B2R[0:1, :],
                      d["b2"].rearrange("(one n) -> one n", one=1),
                      deps=self.t_wo, sig="dgb")
        for half in range(2):
            def rhs1(k, half=half):
                return self.XNO[:, k * TOK + half * 512:k * TOK + half * 512 + 512]
            def ev1(m, t_mm, half=half):
                ps = P[4] if m % 2 == 0 else P[5]
                return p.op("scalar", lambda s, m=m, ps=ps: s.activation(
                    self.H[:, m * 512:(m + 1) * 512], ps[:, :], AF.Gelu),
                    deps=[t_mm], sig="se")
            t_f1 = self.proj_std(d["p1T"], DC, FC, rhs1, ev1, self.W1B,
                                 (P[4], P[5]),
                                 deps0=self.t_xn + self.t_wo + ([t_h_free] if t_h_free else []),
                                 bias_row=self.B1R, bias_dep=t_b1r, mw=2,
                                 tiled=True)
            t_g_ev = None
            t_ij = {}
            for dc in range(16):
                b = dc % 3
                tw = [None]
                def wdma(s2, dc=dc, b=b):
                    return p.dma(
                        "gpsimd",
                        self.W2B[b][:, s2 * 32 * 128:(s2 + 1) * 32 * 128].rearrange(
                            "p (c n) -> p c n", c=32),
                        d["p2T"][dc, s2],
                        deps=[w2_free[b]] if w2_free[b] else [],
                        sig=f"dgp{b}")
                t_w2 = [wdma(0), wdma(1)]
                pg = P[6] if dc % 2 == 0 else P[7]
                def mm2(t, dc=dc, b=b, pg=pg, half=half):
                    for k in range(FC):
                        t.matmul(pg[:, :], self.W2B[b][:, k * 128:(k + 1) * 128],
                                 self.H[:, k * 512:(k + 1) * 512],
                                 start=(k == 0), stop=False)
                    return t.matmul(pg[:, :], self.B2R[0:1, dc * 128:(dc + 1) * 128],
                                    self.CONB[0:1, 512:1024], start=False, stop=True)
                deps = t_w2 + [t_f1, t_b2r]
                if t_g_ev:
                    deps.append(t_g_ev)
                t_mm2 = p.op("tensor", mm2, deps=deps, sig="pe")
                t_ij[dc] = t_mm2
                w2_free[b] = t_mm2
                j = dc % 2
                xs_war = [t for t in (xs_last.get(j),) if t]
                t_xa = p.dma("sync", self.XS[j][:, :],
                             self.xattn[dc * 128:(dc + 1) * 128,
                                        half * 512:half * 512 + 512],
                             deps=[self.t_wo_ev[(half, dc)]] + xs_war,
                             sig=f"dsa{j}")
                t_xo = p.dma("sync", self.XS[2 + j][:, :],
                             d["xT"][dc * 128:(dc + 1) * 128,
                                     HALO + half * 512:HALO + half * 512 + 512],
                             deps=xs_war, sig=f"dsb{j}")
                t_r1 = p.op("vector", lambda v, j=j, pg=pg: v.tensor_add(
                    self.XS[j][:, :], self.XS[j][:, :], pg[:, :]),
                    deps=[t_mm2, t_xa], sig="ve")
                t_g_ev = p.op("vector", lambda v, j=j: v.tensor_add(
                    self.XS[j][:, :], self.XS[j][:, :], self.XS[2 + j][:, :]),
                    deps=[t_xo, t_r1], sig="ve")
                t_d = p.dma("sync", d["x1"][dc * 128:(dc + 1) * 128,
                                            half * 512:half * 512 + 512],
                            self.XS[j][:, :], deps=[t_g_ev], sig="dso")
                xs_last[j] = t_d
                t_x1.append(t_d)
            t_h_free = t_ij[15]
            self.t_ffn_p45 = t_f1
        self.t_x1 = t_x1
        self.t_ffn_done = [t_x1[-1], t_h_free]
        self.out_toks += t_x1


    def _ln2_weff(self):
        p, d, P = self.p, self.d, self.P
        t_x1_last = self.t_x1[-1]
        self.t_beff_rd = None
        self.wouts = []
        ENTPW3 = self.ENTPW.rearrange("p (c n) -> p c n", c=FC)
        def src(c, b, lo, n, deps):
            return p.dma("sync", self.STR[b][:, lo:lo + n],
                         d["x1"][c * 128:(c + 1) * 128, lo:lo + n],
                         deps=list(deps) + [t_x1_last], sig=f"dsx{b}")
        def out(c, g):
            return self.XNO[:, c * TOK + g * 512:c * TOK + (g + 1) * 512]
        toks = self._ln(src, 2, out, [P[2], P[6]], [P[0], P[1]], "ln2",
                        bc_pair=(P[2], P[0]))
        self.t_xn1 = toks[:2]
        t_pw_free = None
        t_ev = None
        t_weff = []
        for l in (0, 1):
            t_pw = p.dma("gpsimd", ENTPW3,
                         d[f"entpw{l}"].rearrange("(c p) n -> p c n", p=128),
                         deps=([t_pw_free] if t_pw_free else self.t_ffn_done),
                         sig="dge")
            t_wp = [None, None, None]
            t_cons = [None, None, None]
            for di in range(16):
                b = di % 3
                dd = [t_cons[b]] if t_cons[b] else []
                for sp in range(2):
                    t_wp[b] = p.dma(
                        "gpsimd",
                        self.WTP[b][:, sp * 32 * 128:(sp + 1) * 32 * 128].rearrange(
                            "p (c n) -> p c n", c=32),
                        d[f"entwT{l}"][di, sp], deps=dd, sig=f"dgp{b}")
                def mmw(t, b=b):
                    last = None
                    for k in range(FC):
                        last = t.matmul(P[4][:, 0:256],
                                        self.WTP[b][:, k * 128:(k + 1) * 128],
                                        ENTPW3[:, k, :],
                                        start=(k == 0), stop=(k == FC - 1))
                    return last
                deps = [t_wp[b], t_pw, self.t_ffn_p45]
                if t_ev:
                    deps.append(t_ev)
                t_mm = p.op("tensor", mmw, deps=deps, sig="pe")
                t_cons[b] = t_mm
                t_ev = p.op("vector", lambda v, di=di: v.tensor_copy(
                    self.WEFFST[:, di * 256:(di + 1) * 256], P[4][:, 0:256]),
                    deps=[t_mm], sig="ve")
            wdst = (d["weffp"][l * D:(l + 1) * D, :] if self.fused
                    else d["weffp"][l])
            t_out = p.dma("sync", wdst.rearrange("(c p) n -> p c n", p=128),
                          self.WEFFST.rearrange("p (c n) -> p c n", c=16),
                          deps=[t_ev], sig="dso")
            self.out_toks.append(t_out)
            self.wouts.append(t_out)
            t_weff.append(t_ev)
            t_eb = p.dma("gpsimd", self.ENTBRAW[0:64, 0:128],
                         d["entb"][l].rearrange("(c p) -> c p", p=128),
                         sig=f"dgn{l}")
            t_tpe = p.op("tensor", lambda t: t.transpose(
                P[3][:, 0:64], self.ENTBRAW[0:64, 0:128], self.IDENT[0:64, 0:64]),
                deps=[t_eb, self.t_ident, self.t_p3], sig="pe")
            t_ecol = p.op("vector", lambda v, l=l: v.tensor_copy(
                self.ENTB[:, l * 64:(l + 1) * 64], P[3][:, 0:64]),
                deps=[t_tpe], sig="ve")
            self.t_p3 = t_ecol
            def mmb(t, l=l):
                last = None
                for k in range(FC):
                    last = t.matmul(P[5][0:1, 0:256],
                                    self.ENTB[:, l * 64 + k:l * 64 + k + 1],
                                    ENTPW3[:, k, :],
                                    start=(k == 0), stop=(k == FC - 1))
                return last
            t_bmm = p.op("tensor", mmb,
                         deps=[t_ecol, t_pw, self.t_ffn_p45]
                         + ([self.t_beff_rd] if self.t_beff_rd else []), sig="pe")
            t_pb = p.dma("gpsimd", self.EPB[0:1, 0:256], d["entpb"][l:l + 1, :],
                         sig=f"dgq{l}")
            t_badd = p.op("vector", lambda v: v.tensor_add(
                self.EPB[0:1, 256:512], P[5][0:1, 0:256], self.EPB[0:1, 0:256]),
                deps=[t_bmm, t_pb], sig="ve")
            self.t_beff_rd = t_badd
            t_bo = p.dma("sync", d["beffp"][l:l + 1, :], self.EPB[0:1, 256:512],
                         deps=[t_badd], sig="dso")
            self.out_toks.append(t_bo)
            self.wouts.append(t_bo)
            t_pw_free = p.op("vector", lambda v: v.engine_nop() if False else
                             v.tensor_copy(self.EPB[0:1, 512:514],
                                           self.EPB[0:1, 256:258]),
                             deps=[t_ev, t_badd, t_bo], sig="ve")
        self.t_weff_done = t_weff
        self.t_weff_outs = list(self.wouts)
        self.t_beff_done = [self.t_beff_rd]

    def _l1_proj(self):
        p, d, P = self.p, self.d, self.P
        deps0 = self.t_xn1 + self.t_weff_done + self.t_weff_outs
        outs = {"q": self.Q1FM, "k": self.K1FM, "v": self.V1FM}
        self.t_p3 = None
        for name, w in (("q", "Wq1"), ("k", "Wk1"), ("v", "Wv1")):
            dst = outs[name]
            def ev(m, t_mm, name=name, dst=dst):
                tE = None
                for seg in range(2):
                    ps = P[seg]
                    o = dst[:, m * 1024 + seg * 512:m * 1024 + seg * 512 + 512]
                    if name == "v":
                        tE = p.op("vector", lambda v, o=o, ps=ps: v.tensor_copy(
                            o, ps[:, :]), deps=[t_mm] if seg == 0 else [],
                            sig="ve" if seg == 1 else None)
                    else:
                        tE = p.op("vector", lambda v, o=o, ps=ps: v.tensor_scalar_max(
                            o, ps[:, :], 0.0), deps=[t_mm] if seg == 0 else [],
                            sig="ve" if seg == 1 else None)
                return tE
            def rhs(k):
                return self.XNO[:, k * TOK:(k + 1) * TOK]
            tok = self.proj_std(d[w], DC, DC, rhs, ev, self.WP, (P[0], P[1]),
                                deps0=deps0, tiled=True)
            if name == "q":
                t_q1o = p.dma("sync", d["q1"].rearrange("(c p) n -> p c n", p=128),
                              self.Q1FM.rearrange("p (c n) -> p c n", c=DC),
                              deps=[tok], sig="dso")
                self.out_toks.append(t_q1o)
                self.t_q1o = t_q1o
            deps0 = deps0 + [tok]
        t_kv_proj = deps0[-1]
        V1T3 = self.V1T.rearrange("p (h e) -> p h e", e=65)
        t_ones1 = p.op("vector", lambda v: v.memset(V1T3[:, :, 64:65], 1.0), sig="ve")
        t_acc0 = p.op("vector", lambda v: v.memset(self.KVST[:, :], 0.0), sig="ve")
        t_tpev = None
        t_mm_pair = {}
        t_acc = t_acc0
        for tt in range(NT):
            for cb in range(4):
                def tps(t, tt=tt, cb=cb):
                    last = None
                    for i in range(4):
                        c = cb * 4 + i
                        last = t.transpose(
                            P[3][:, i * 128:(i + 1) * 128],
                            self.K1FM[:, c * 1024 + tt * 128:c * 1024 + (tt + 1) * 128],
                            self.IDENT[:, :])
                    for i in range(4):
                        c = cb * 4 + i
                        last = t.transpose(
                            P[3][:, 512 + i * 128:512 + (i + 1) * 128],
                            self.V1FM[:, c * 1024 + tt * 128:c * 1024 + (tt + 1) * 128],
                            self.IDENT[:, :])
                    return last
                deps = [t_kv_proj, self.t_ident, self.t_p3]
                if t_tpev is not None:
                    deps.append(t_tpev)
                if tt > 0 and cb == 0:
                    deps.append(t_mm_pair[(tt - 1, 15)])
                t_tp = p.op("tensor", tps, deps=deps, sig="pe")
                self.t_p3 = t_tp
                t_k = p.op("vector", lambda v, cb=cb: v.tensor_copy(
                    self.K1T[:, cb * 512:(cb + 1) * 512], P[3][:, 0:512]),
                    deps=[t_tp], sig="ve")
                t_tpev = p.op("vector", lambda v, cb=cb: v.tensor_copy(
                    V1T3[:, cb * 8:(cb + 1) * 8, 0:64],
                    P[3][:, 512:1024].rearrange("p (h e) -> p h e", h=8)),
                    deps=[t_k, t_ones1], sig="ve")
                self.t_p3 = t_tpev
            for hp in range(16):
                ps = P[4] if hp % 2 == 0 else P[5]
                def mkv(t, tt=tt, hp=hp, ps=ps):
                    last = None
                    for odd in range(2):
                        h = 2 * hp + odd
                        last = t.matmul(
                            ps[odd * 64:odd * 64 + 64, 0:65],
                            self.K1T[:, h * 64:(h + 1) * 64],
                            V1T3[:, h, :],
                            start=True, stop=True,
                            tile_position=(0, 64) if odd else None)
                    return last
                deps = [t_tpev, self.t_beff_done[-1]]
                ev = t_mm_pair.get((tt, hp - 2)) or t_mm_pair.get((tt - 1, hp + 14))
                prev_add = getattr(self, "_t_kvadd", {}).get(
                    (tt, hp - 2)) or getattr(self, "_t_kvadd", {}).get(
                    (tt - 1, hp + 14))
                if prev_add:
                    deps.append(prev_add)
                t_mm = p.op("tensor", mkv, deps=deps, sig="pe")
                t_mm_pair[(tt, hp)] = t_mm
                t_add = p.op("vector", lambda v, hp=hp, ps=ps: v.tensor_add(
                    self.KVST[:, hp * 65:(hp + 1) * 65],
                    self.KVST[:, hp * 65:(hp + 1) * 65], ps[:, 0:65]),
                    deps=[t_mm, t_acc], sig="ve")
                t_acc = t_add
                if not hasattr(self, "_t_kvadd"):
                    self._t_kvadd = {}
                self._t_kvadd[(tt, hp)] = t_add
        t_e4 = t_acc
        t_kvo = p.dma("sync", d["kvp"], self.KVST[:, :], deps=[t_e4], sig="dso")
        self.out_toks.append(t_kvo)
        self.t_l1proj_done = [t_kvo, t_e4, self.t_q1o]

    def _issue_cc_weff(self):
        p, d = self.p, self.d
        self.cc_wf = p.cc(lambda g: g.collective_compute(
            "AllGather", ALU.bypass, replica_groups=[list(range(8))],
            ins=[d["weffp"].opt()], outs=[d["weffg"].opt()]),
            deps=[self.wouts[0], self.wouts[2]], sig="ccw")
        self.cc_bf = p.cc(lambda g: g.collective_compute(
            "AllGather", ALU.bypass, replica_groups=[list(range(8))],
            ins=[d["beffp"].opt()], outs=[d["beffg"].opt()]),
            deps=[self.wouts[1], self.wouts[3]], sig="ccb")

    def _issue_cc_kv(self):
        p, d = self.p, self.d
        self.cc_kv = p.cc(lambda g: g.collective_compute(
            "AllReduce", ALU.add, replica_groups=[[0, 1, 2, 3], [4, 5, 6, 7]],
            ins=[d["kvp"].opt()], outs=[d["kvsum"].opt()]),
            deps=[self.t_l1proj_done[0]], sig="cck")

    def _amp_ph(self):
        p, d, P = self.p, self.d, self.P
        APB = self.ar.view(self.ar.region_off["SQ"], BF16, D)
        PHB = self.ar.view(self.ar.region_off["ROWS"], BF16, D)
        t_ab = p.dma("gpsimd", APB[0:1, :],
                     d["ampb"].rearrange("(one n) -> one n", one=1),
                     deps=self.t_xn1, sig="dga")
        t_pb = p.dma("gpsimd", PHB[0:1, :],
                     d["phb"].rearrange("(one n) -> one n", one=1),
                     deps=self.t_xn1, sig="dgh")
        deps0 = self.t_xn1 + self.t_l1proj_done
        def rhs(k):
            return self.XNO[:, k * TOK:(k + 1) * TOK]
        for name, w, brow, bdep, dst, fn in (
                ("amp", "ampw", APB, t_ab, self.AMP, AF.Sigmoid),
                ("ph", "phw", PHB, t_pb, self.TH, AF.Tanh)):
            def ev(m, t_mm, dst=dst, fn=fn):
                tE = None
                for seg in range(2):
                    ps = P[6 + seg]
                    tE = p.op("scalar", lambda s, m=m, seg=seg, ps=ps, dst=dst,
                              fn=fn: s.activation(
                        dst[:, m * 1024 + seg * 512:m * 1024 + seg * 512 + 512],
                        ps[:, :], fn), deps=[t_mm] if seg == 0 else [],
                        sig="se" if seg == 1 else None)
                return tE
            tok = self.proj_std(d[w], DC, DC, rhs, ev, self.WPK, (P[6], P[7]),
                                deps0=deps0, bias_row=brow, bias_dep=bdep,
                                tiled=True)
            deps0 = deps0 + [tok]
        t_ph = deps0[-1]
        t_amp = deps0[-2]
        t_out = []
        t_dve_pr = None
        t_s2_pr = None
        for c in range(DC):
            th = self.TH[:, c * TOK:(c + 1) * TOK]
            t_s1 = p.op("scalar", lambda s, th=th: s.activation(
                self.SC[:, 0:1024], th, AF.Sin, scale=PI),
                deps=[t_ph] + ([t_dve_pr] if t_dve_pr else []), sig="se")
            t_s2 = p.op("scalar", lambda s, th=th: s.activation(
                self.SC[:, 1024:2048], th, AF.Sin, scale=PI / 2), sig="se")
            t_d0 = p.op("vector", lambda v: v.tensor_mul(
                self.SC[:, 1024:2048], self.SC[:, 1024:2048], self.SC[:, 1024:2048]),
                deps=[t_s2], sig="ve")
            t_d1 = p.op("vector", lambda v: v.tensor_scalar(
                self.SC[:, 1024:2048], self.SC[:, 1024:2048], -2.0, 1.0,
                ALU.mult, ALU.add), deps=[t_d0], sig="ve")
            t_d2 = p.op("vector", lambda v: v.tensor_add(
                self.SC[:, 1024:2048], self.SC[:, 1024:2048], self.SC[:, 0:1024]),
                deps=[t_s1, t_d1], sig="ve")
            t_d3 = p.op("vector", lambda v, c=c, th=th: v.tensor_mul(
                th, self.SC[:, 1024:2048],
                self.AMP[:, c * TOK:(c + 1) * TOK]),
                deps=[t_d2, t_amp], sig="ve")
            t_dve_pr = t_d3
            t_o = p.dma("sync", d["h0"][c * 128:(c + 1) * 128, :], th,
                        deps=[t_d3], sig="dso")
            t_out.append(t_o)
        self.out_toks += t_out


def host_masks(first):
    jj, pp = np.meshgrid(np.arange(WIN), np.arange(128))
    band = (jj >= pp) & (jj <= pp + 512)
    out = np.zeros((5, 128, WIN), np.float32)
    for s in range(5):
        ok = band.copy()
        if first and s < 4:
            ok = ok & (jj >= 512 - s * 128)
        out[s] = np.where(ok, 0.0, -1e9).astype(np.float32)
    return out


def tile_proj(WT, kc, mc, mw=1, dma_split=1):
    K, M = WT.shape
    assert K == kc * 128 and M == mc * 128
    kc_spl = kc // dma_split
    A = WT.reshape(dma_split, kc_spl, 128, mc // mw, mw * 128)
    return np.ascontiguousarray(A.transpose(3, 0, 2, 1, 4))


def host_l0_weights(I):
    C = np.ascontiguousarray
    return {
        "WqT": C(np.asarray(I["L0_attn_Wq"]).T),
        "WkvT": C(np.asarray(I["L0_attn_Wkv"]).T),
        "WgT": C(np.asarray(I["L0_attn_Wg"]).T),
        "bg": np.asarray(I["L0_attn_bg"]),
        "qsc": np.asarray(I["L0_attn_q_scale"]),
        "ksc": np.asarray(I["L0_attn_k_scale"]),
        "WoT": C(np.asarray(I["L0_attn_Wo"]).T),
        "p1T": tile_proj(np.asarray(I["L0_ffn_proj1"]).T, DC, FC, mw=2),
        "b1": np.asarray(I["L0_ffn_proj1_bias"]),
        "p2T": tile_proj(np.asarray(I["L0_ffn_proj2"]).T, FC, DC,
                         dma_split=2),
        "b2": np.asarray(I["L0_ffn_proj2_bias"]),
    }


def host_l1_weights(I):
    C = np.ascontiguousarray
    w = {
        "Wq1": tile_proj(np.asarray(I["L1_attn_Wq"]), DC, DC),
        "Wk1": tile_proj(np.asarray(I["L1_attn_Wk"]), DC, DC),
        "Wv1": tile_proj(np.asarray(I["L1_attn_Wv"]), DC, DC),
        "ampw": tile_proj(np.asarray(I["L1_ffn_amp_w"]), DC, DC),
        "ampb": np.asarray(I["L1_ffn_amp_b"]),
        "phw": tile_proj(np.asarray(I["L1_ffn_ph_w"]), DC, DC),
        "phb": np.asarray(I["L1_ffn_ph_b"]),
        "entwT0": tile_proj(np.asarray(I["L1_ffn_ent_w"][0]).T, FC, DC,
                            dma_split=2),
        "entwT1": tile_proj(np.asarray(I["L1_ffn_ent_w"][1]).T, FC, DC,
                            dma_split=2),
        "entb": np.asarray(I["L1_ffn_ent_b"]),
    }
    return w


def host_inputs_l0(I, with_l1=True):
    x = np.asarray(I["L0_attn_x"])
    wl0 = host_l0_weights(I)
    if with_l1:
        wl1 = host_l1_weights(I)
        epw = np.asarray(I["L1_ffn_ent_pw"])
        epb = np.asarray(I["L1_ffn_ent_pb"])
    maps = []
    for c in range(8):
        b, t0 = c // 4, (c % 4) * TOK
        xs = np.zeros((TOT, D), np.float32)
        lo = max(0, t0 - HALO)
        xs[HALO - (t0 - lo):] = x[b, lo:t0 + TOK]
        m = {"xT": np.ascontiguousarray(xs.T),
             "masks": host_masks(t0 == 0)}
        m.update(wl0)
        if with_l1:
            m.update(wl1)
            m["entpw0"] = np.ascontiguousarray(epw[0][:, c * 256:(c + 1) * 256])
            m["entpw1"] = np.ascontiguousarray(epw[1][:, c * 256:(c + 1) * 256])
            m["entpb"] = np.ascontiguousarray(
                np.stack([epb[0, c * 256:(c + 1) * 256],
                          epb[1, c * 256:(c + 1) * 256]]))
        maps.append(m)
    return maps



class L2Build:

    def __init__(self, dbg=False, fused_io=None):
        self.fused = fused_io is not None
        if self.fused:
            nc = fused_io["nc"]
            self.p = fused_io["p"]
            self.cc_kv = fused_io["cc_kv"]
            self.cc_wf = fused_io["cc_wf"]
            self.cc_bf = fused_io["cc_bf"]
        else:
            nc = bass.Bass("TRN2", target_bir_lowering=False)
            self.p = Prog(nc)
        self.nc = nc
        self.dbg = dbg
        d = {}
        def di(name, shape, dt=F32):
            d[name] = nc.dram_tensor(name, shape, dt, kind="ExternalInput").ap()
        def do(name, shape, dt=F32):
            d[name] = nc.dram_tensor(name, shape, dt, kind="ExternalOutput").ap()
        if self.fused:
            for k in ("x1", "h0", "q1", "kvsum", "weffg", "beffg"):
                d[k] = fused_io[k]
        else:
            di("x1", [D, TOK]); di("h0", [D, TOK]); di("q1", [D, TOK], BF16)
            di("kvps", [4, 128, 1056]); di("weff", [2, D, D], BF16)
            di("beff", [2, D])
        di("Wo1", [DC, 1, 128, DC, 128]); di("bo", [D])
        di("pathw", [2, 2, DC, 1, 128, DC, 128])
        di("pathb", [2, 2, D]); di("phase", [2, 2, D])
        di("ln3g", [2, D]); di("ln3b", [2, D]); di("ln4g", [2, D]); di("ln4b", [2, D])
        do("x2", [D, TOK])
        if dbg:
            do("d_o1", [D, TOK], BF16)
            do("d_x2a", [D, TOK])
            do("d_hent0", [D, TOK])
            do("d_hln30", [D, TOK])
            do("d_hpath0", [D, TOK])
            do("d_hln40", [D, TOK])
        self.d = d
        self.out_toks = []

    def build(self):
        nc = self.nc
        with contextlib.ExitStack() as st:
            ar = Arena(nc, 212600)
            ar.t = st.enter_context(nc.sbuf_tensor("AR", [128, 212600], U8))
            self.ar = ar
            self.P = [st.enter_context(nc.psum_tensor(f"P{i}", [128, 512], F32))
                      for i in range(8)]
            return self._build_in(finalize=True)

    def _build_in(self, finalize):
        nc, p, d, ar, P = self.nc, self.p, self.d, self.ar, self.P
        if True:
            self.CON = ar.alloc(1024, F32, 256, "CON")
            self.CONB = ar.alloc(2048, BF16, 1024, "CONB")
            self.IDF = ar.alloc(512, F32, 128, "IDF")
            self.ROWS = ar.alloc(8192, F32, 2048, "ROWS")
            self.SQ2 = ar.alloc(4096, F32, 1024, "SQ2")
            self.TMP = [ar.alloc(2048, F32, 512, f"TMP{i}") for i in range(4)]
            self.GBC = ar.alloc(512, F32, 128, "GBC")
            self.MODC = ar.alloc(256, F32, 64, "MODC")
            self.PBC = ar.alloc(256, F32, 64, "PBC")
            self.X1 = ar.alloc(DC * TOK * 4, F32, DC * TOK, "X1")
            self.GBALL = ar.alloc(1024, F32, 256, "GBALL")
            ar.mark("q")
            self.O1 = ar.alloc(DC * TOK * 2, BF16, DC * TOK, "O1")
            self._consts()
            self._qkv()
            self._wo1()
            if self.dbg:
                self.out_toks.append(p.dma(
                    "sync", d["d_o1"].rearrange("(c p) n -> p c n", p=128),
                    self.O1.rearrange("p (c n) -> p c n", c=DC),
                    deps=[self.t_o1], sig="dso"))
                self.out_toks.append(p.dma(
                    "sync", d["d_x2a"].rearrange("(c p) n -> p c n", p=128),
                    self.X1.rearrange("p (c n) -> p c n", c=DC),
                    deps=[self.t_x2a], sig="dso"))
            ar.reset("q")
            self.H = ar.alloc(DC * TOK * 4, F32, DC * TOK, "H")
            self.HB = ar.alloc(DC * TOK * 2, BF16, DC * TOK, "HB")
            self.WPN = [ar.alloc(DC * 128 * 2, BF16, DC * 128, f"WPN{i}")
                        for i in range(2)]
            self.WPN2 = [ar.alloc(DC * 128 * 2, BF16, DC * 128, f"WPN2{i}")
                         for i in range(2)]
            self.BEFFR = ar.alloc(4096, BF16, 2048, "BEFFR")
            self._ffn1()
            for tk in self.out_toks:
                p.wait("sync", tk)
            p.build()
        return nc

    def _consts(self):
        p, d, P = self.p, self.d, self.P
        t0 = p.op("vector", lambda v: v.memset(self.CON[:, 0:1], 1.0), sig="ve")
        self.t_ones = p.op("vector", lambda v: v.memset(self.CON[0:1, 64:192], 1.0),
                           deps=[t0], sig="ve")
        self.t_onesb = p.op("vector", lambda v: v.memset(self.CONB[:, :], 1.0),
                            sig="ve")
        t_msf = p.op("gpsimd", lambda g: g.memset(self.IDF[:, :], 0.0),
                     sig="ge")
        self.t_idf = p.op("gpsimd", lambda g: g.affine_select(
            out=self.IDF[:, :], in_=self.IDF[:, :], compare_op=ALU.not_equal,
            fill=1.0, base=0, pattern=[[-1, 128]], channel_multiplier=1),
            deps=[t_msf], sig="ge")
        self.t_x1 = p.dma("sync", self.X1.rearrange("p (c n) -> p c n", c=DC),
                          d["x1"].rearrange("(c p) n -> p c n", p=128), sig="dsx")
        self.LNR = self.ar.alloc(8192, F32, 2048, "LNR")
        t_l = []
        for i, nm in enumerate(("ln3g", "ln3b", "ln4g", "ln4b")):
            t_l.append(p.dma("gpsimd", self.LNR[2 * i:2 * i + 2, 0:D], d[nm],
                             sig="dgl"))
        t_ph4 = p.dma("gpsimd", self.LNR[8:12, 0:D],
                      d["phase"].rearrange("l pp n -> (l pp) n"), sig="dgl")
        t_pb4 = p.dma("gpsimd", self.LNR[12:16, 0:D],
                      d["pathb"].rearrange("l pp n -> (l pp) n"), sig="dgl")
        def tpl(t):
            last = None
            for c in range(DC):
                last = t.transpose(P[4][:, c * 16:(c + 1) * 16],
                                   self.LNR[0:16, c * 128:(c + 1) * 128],
                                   self.IDF[0:16, 0:16])
            return last
        t_tp = p.op("tensor", tpl, deps=t_l + [t_ph4, t_pb4, self.t_idf], sig="pe")
        t_gb = p.op("vector", lambda v: v.tensor_copy(
            self.GBALL[:, 0:256], P[4][:, 0:256]), deps=[t_tp], sig="ve")
        PH = self.GBALL.rearrange("p (c r) -> p c r", r=16)
        phs = PH[:, :, 8:12]
        t_rr = t_gb
        for _ in range(2):
            t_a = p.op("vector", lambda v: v.tensor_single_scalar(
                self.SQ2[:, 0:64].rearrange("p (c r) -> p c r", r=4),
                phs, math.pi, ALU.is_gt), deps=[t_rr], sig="ve")
            t_b = p.op("vector", lambda v: v.scalar_tensor_tensor(
                phs, self.SQ2[:, 0:64].rearrange("p (c r) -> p c r", r=4),
                -2 * math.pi, phs, ALU.mult, ALU.add), deps=[t_a], sig="ve")
            t_c = p.op("vector", lambda v: v.tensor_single_scalar(
                self.SQ2[:, 0:64].rearrange("p (c r) -> p c r", r=4),
                phs, -math.pi, ALU.is_lt), deps=[t_b], sig="ve")
            t_rr = p.op("vector", lambda v: v.scalar_tensor_tensor(
                phs, self.SQ2[:, 0:64].rearrange("p (c r) -> p c r", r=4),
                2 * math.pi, phs, ALU.mult, ALU.add), deps=[t_c], sig="ve")
        MOD = self.MODC.rearrange("p (c r) -> p c r", r=4)
        t_s1 = p.op("scalar", lambda s: s.activation(
            MOD, phs, AF.Sin), deps=[t_rr], sig="se")
        t_s2 = p.op("scalar", lambda s: s.activation(
            self.SQ2[:, 64:128].rearrange("p (c r) -> p c r", r=4),
            phs, AF.Sin, scale=0.5), deps=[t_rr], sig="se")
        t_m1 = p.op("vector", lambda v: v.tensor_mul(
            self.SQ2[:, 64:128], self.SQ2[:, 64:128], self.SQ2[:, 64:128]),
            deps=[t_s2], sig="ve")
        t_m2 = p.op("vector", lambda v: v.tensor_scalar(
            self.SQ2[:, 64:128], self.SQ2[:, 64:128], -2.0, 1.0, ALU.mult, ALU.add),
            deps=[t_m1], sig="ve")
        t_mod = p.op("vector", lambda v: v.tensor_add(
            self.MODC[:, 0:64], self.MODC[:, 0:64], self.SQ2[:, 64:128]),
            deps=[t_s1, t_m2], sig="ve")
        PBv = PH[:, :, 12:16].rearrange("p c (l pp) -> p c l pp", l=2, pp=2)
        MODv = MOD.rearrange("p c (l pp) -> p c l pp", l=2, pp=2)
        BPE = self.PBC[:, 0:32].rearrange("p (c l) -> p c l", l=2)
        t_p1 = p.op("vector", lambda v: v.tensor_mul(
            self.SQ2[:, 128:192].rearrange("p (c l pp) -> p c l pp", l=2, pp=2),
            PBv, MODv), deps=[t_mod, t_gb], sig="ve")
        SQv = self.SQ2[:, 128:192].rearrange("p (c l pp) -> p c l pp", l=2, pp=2)
        t_bpe = p.op("vector", lambda v: v.tensor_add(
            BPE, SQv[:, :, :, 0], SQv[:, :, :, 1]), deps=[t_p1], sig="ve")
        self.t_mod = t_mod
        self.t_bpe = t_bpe
        self.t_gb = t_gb
        self.BOR = self.ar.view(self.ar.region_off["LNR"], BF16, D)
        self.t_bo = p.dma("gpsimd", self.BOR[0:1, :],
                          d["bo"].rearrange("(one n) -> one n", one=1),
                          deps=[t_gb], sig="dgb")

    def _qkv(self):
        p, d, P = self.p, self.d, self.P
        self.WPN0 = [self.ar.alloc(DC * 128 * 2, BF16, DC * 128, f"WPN0{i}")
                     for i in range(2)]
        self.Q1 = self.ar.alloc(DC * TOK * 2, BF16, DC * TOK, "Q1")
        self.KVP4 = self.ar.alloc(4 * 1056 * 4, F32, 4 * 1056, "KVP4")
        self.KVE = self.ar.alloc(1056 * 4, F32, 1056, "KVE")
        self.KVB = self.ar.alloc(1056 * 2, BF16, 1056, "KVB")
        t_q = p.dma("sync", self.Q1.rearrange("p (c n) -> p c n", c=DC),
                    d["q1"].rearrange("(c p) n -> p c n", p=128), sig="dsq")
        if self.fused:
            t_s = p.dma("sync", self.KVE[:, :], d["kvsum"],
                        deps=[self.cc_kv], sig="dsk")
        else:
            t_kv = p.dma("sync", self.KVP4.rearrange("p (g n) -> p g n", g=4),
                         d["kvps"].rearrange("g p n -> p g n"), sig="dsk")
            t_s = p.op("vector", lambda v: v.tensor_add(
                self.KVE[:, :], self.KVP4[:, 0:1056], self.KVP4[:, 1056:2112]),
                deps=[t_kv], sig="ve")
            t_s = p.op("vector", lambda v: v.tensor_add(
                self.KVE[:, :], self.KVE[:, :], self.KVP4[:, 2112:3168]),
                deps=[t_s], sig="ve")
            t_s = p.op("vector", lambda v: v.tensor_add(
                self.KVE[:, :], self.KVE[:, :], self.KVP4[:, 3168:4224]),
                deps=[t_s], sig="ve")
        t_b = p.op("vector", lambda v: v.tensor_copy(self.KVB[:, :], self.KVE[:, :]),
                   deps=[t_s], sig="ve")
        t_ev = {}
        for h in range(32):
            hp, odd = h // 2, h % 2
            par = h % 2
            pa, pb = (P[0], P[1]) if par == 0 else (P[2], P[3])
            pc = P[4] if par == 0 else P[5]
            R = self.ROWS[0:1, par * 1024:par * 1024 + 1024]
            TMa = self.TMP[0 + 2 * par]
            TMb = self.TMP[1 + 2 * par]
            lhs = self.KVB[odd * 64:odd * 64 + 64, hp * 65:hp * 65 + 65]
            rhs0 = self.Q1[odd * 64:odd * 64 + 64,
                           (h // 2) * 1024:(h // 2) * 1024 + 512]
            rhs1 = self.Q1[odd * 64:odd * 64 + 64,
                           (h // 2) * 1024 + 512:(h // 2) * 1024 + 1024]
            def mm(t, lhs=lhs, rhs0=rhs0, rhs1=rhs1, pa=pa, pb=pb, odd=odd):
                t.matmul(pa[0:65, :], lhs, rhs0, start=True, stop=True,
                         tile_position=(64, 0) if odd else None)
                return t.matmul(pb[0:65, :], lhs, rhs1, start=True, stop=True,
                                tile_position=(64, 0) if odd else None)
            deps = [t_b, t_q]
            if t_ev.get(h - 2):
                deps.append(t_ev[h - 2])
            t_mm = p.op("tensor", mm, deps=deps, sig="pe")
            t_l0 = p.op("scalar", lambda s, pa=pa, R=R: s.activation(
                R[:, 0:512], pa[64:65, :], AF.Ln), deps=[t_mm], sig="se")
            t_l1 = p.op("scalar", lambda s, pb=pb, R=R: s.activation(
                R[:, 512:1024], pb[64:65, :], AF.Ln), deps=[t_l0], sig="se")
            t_n1 = p.op("scalar", lambda s, R=R: s.activation(
                R[:, 0:1024], R[:, 0:1024], AF.Exp, scale=-1.0),
                deps=[t_l1], sig="se")
            t_r = p.op("vector", lambda v, R=R: v.tensor_scalar_min(
                R[:, 0:1024], R[:, 0:1024], 1e6), deps=[t_n1], sig="ve")
            def bc(t, pc=pc, R=R):
                t.matmul(pc[0:64, :], self.CON[0:1, 64:128], R[:, 0:512],
                         start=True, stop=True)
                return t.matmul(pc[64:128, :], self.CON[0:1, 64:128],
                                R[:, 512:1024], start=True, stop=True,
                                tile_position=(0, 64))
            t_bc = p.op("tensor", bc,
                        deps=[t_r, self.t_ones] + ([t_ev[h - 2]] if t_ev.get(h - 2)
                                                   else []), sig="pe")
            t_c0 = p.op("vector", lambda v, pa=pa, TMa=TMa: v.tensor_copy(
                TMa[:64, :], pa[0:64, :]), deps=[t_mm], sig="ve")
            t_c1 = p.op("scalar", lambda s, pb=pb, TMb=TMb: s.activation(
                TMb[:64, :], pb[0:64, :], AF.Copy), deps=[t_mm], sig="se")
            o0 = self.O1[odd * 64:odd * 64 + 64,
                         (h // 2) * 1024:(h // 2) * 1024 + 512]
            o1 = self.O1[odd * 64:odd * 64 + 64,
                         (h // 2) * 1024 + 512:(h // 2) * 1024 + 1024]
            t_d0 = p.op("vector", lambda v, o0=o0, TMa=TMa, pc=pc: v.tensor_mul(
                o0, TMa[:64, :], pc[0:64, :]), deps=[t_c0, t_bc], sig="ve")
            t_ev[h] = p.op("vector", lambda v, o1=o1, TMb=TMb, pc=pc: v.tensor_mul(
                o1, TMb[:64, :], pc[64:128, :]), deps=[t_c1, t_bc, t_d0], sig="ve")
        self.t_o1 = t_ev[31]

    def _wo1(self):
        p, d, P = self.p, self.d, self.P
        t_free = {}
        t_w = [None, None]
        t_cons = [None, None]
        self.t_x2a = []
        for m in range(DC):
            b = m % 2
            t_w[b] = p.dma("gpsimd",
                           self.WPN0[b].rearrange("p (c n) -> p c n", c=DC),
                           d["Wo1"][m, 0],
                           deps=[t_cons[b]] if t_cons[b] else [], sig=f"dgp{b}")
            pa, pb = (P[6], P[7])
            def mm(t, m=m, b=b):
                last = None
                for seg, ps in ((0, pa), (1, pb)):
                    for k in range(DC):
                        t.matmul(ps[:, :],
                                 self.WPN0[b][:, k * 128:(k + 1) * 128],
                                 self.O1[:, k * 1024 + seg * 512:
                                         k * 1024 + seg * 512 + 512],
                                 start=(k == 0), stop=False)
                    last = t.matmul(ps[:, :], self.BOR[0:1, m * 128:(m + 1) * 128],
                                    self.CONB[0:1, 512:1024], start=False, stop=True)
                return last
            deps = [t_w[b], self.t_o1, self.t_bo, self.t_onesb, self.t_x1]
            if t_free.get(m - 1):
                deps.append(t_free[m - 1])
            t_mm = p.op("tensor", mm, deps=deps, sig="pe")
            t_cons[b] = t_mm
            t_a0 = p.op("vector", lambda v, m=m, pa=pa: v.tensor_add(
                self.X1[:, m * 1024:m * 1024 + 512],
                self.X1[:, m * 1024:m * 1024 + 512], pa[:, :]),
                deps=[t_mm], sig="ve")
            t_free[m] = p.op("vector", lambda v, m=m, pb=pb: v.tensor_add(
                self.X1[:, m * 1024 + 512:(m + 1) * 1024],
                self.X1[:, m * 1024 + 512:(m + 1) * 1024], pb[:, :]),
                deps=[t_a0], sig="ve")
        self.t_x2a = t_free[DC - 1]

    def _ln_seg(self, seg, src, gcol_r, bcol_r):
        p, P = self.p, self.P
        ones = self.CON[:, 0:1]
        GB = self.GBALL.rearrange("p (c r) -> p c r", r=16)
        SU, SS = (P[0], P[2]) if seg == 0 else (P[1], P[3])
        R = self.ROWS[0:1, seg * 1024:seg * 1024 + 1024]
        SQ = self.SQ2[0:1, seg * 512:(seg + 1) * 512]
        t_sq_pr = None
        t_pe = None
        for c in range(DC):
            hsl = self.H[:, c * 1024 + seg * 512:c * 1024 + (seg + 1) * 512]
            t_sq = p.op("scalar", lambda sc, hsl=hsl, seg=seg: sc.activation(
                self.SQ2[:, seg * 512:(seg + 1) * 512], hsl, AF.Square),
                deps=[src] + ([t_sq_pr] if t_sq_pr else []), sig="se")
            def mms(t, c=c, hsl=hsl, SU=SU, SS=SS, seg=seg):
                t.matmul(SU[0:1, :], ones, hsl, start=(c == 0), stop=(c == DC - 1))
                return t.matmul(SS[0:1, :], ones,
                                self.SQ2[:, seg * 512:(seg + 1) * 512],
                                start=(c == 0), stop=(c == DC - 1))
            t_pe = p.op("tensor", mms, deps=[src, t_sq], sig="pe")
            t_sq_pr = t_pe
        t0 = p.op("vector", lambda v: v.tensor_scalar_mul(
            R[:, 0:512], SU[0:1, :], 1.0 / D), deps=[t_pe], sig="ve")
        ta = p.op("vector", lambda v: v.tensor_scalar_mul(
            R[:, 512:1024], SS[0:1, :], 1.0 / D), deps=[t0], sig="ve")
        tb = p.op("vector", lambda v: v.tensor_mul(
            SQ, R[:, 0:512], R[:, 0:512]), deps=[ta], sig="ve")
        tc = p.op("vector", lambda v: v.tensor_sub(
            R[:, 512:1024], R[:, 512:1024], SQ), deps=[tb], sig="ve")
        t2 = p.op("vector", lambda v: v.tensor_scalar_add(
            R[:, 512:1024], R[:, 512:1024], EPS), deps=[tc], sig="ve")
        t3 = p.op("scalar", lambda sc: sc.activation(SQ, R[:, 512:1024], AF.Ln),
                  deps=[t2], sig="se")
        t4 = p.op("scalar", lambda sc: sc.activation(R[:, 512:1024], SQ, AF.Exp,
                                                     scale=-0.5), deps=[t3], sig="se")
        def bc(t, SU=SU, SS=SS):
            t.matmul(SU[:, :], self.CON[0:1, 64:192], R[:, 0:512],
                     start=True, stop=True)
            return t.matmul(SS[:, :], self.CON[0:1, 64:192], R[:, 512:1024],
                            start=True, stop=True)
        t_bc = p.op("tensor", bc, deps=[t0, t4, self.t_ones], sig="pe")
        TMa, TMb = (self.TMP[0], self.TMP[1]) if seg == 0 else \
                   (self.TMP[2], self.TMP[3])
        t_ap = None
        if not hasattr(self, "lnc"):
            self.lnc = {}
        self.lnc[seg] = []
        for c in range(DC):
            hsl = self.H[:, c * 1024 + seg * 512:c * 1024 + (seg + 1) * 512]
            t_u = p.op("vector", lambda v, hsl=hsl, TMa=TMa, SU=SU: v.tensor_sub(
                TMa[:, :], hsl, SU[:, :]),
                deps=[t_bc] + ([t_ap] if t_ap else []), sig="ve")
            t_u2 = p.op("vector", lambda v, TMa=TMa, TMb=TMb, SS=SS: v.tensor_mul(
                TMb[:, :], TMa[:, :], SS[:, :]), deps=[t_u], sig="ve")
            t_ap = p.op("vector", lambda v, hsl=hsl, c=c, TMb=TMb: v.tensor_scalar(
                hsl, TMb[:, :], GB[:, c, gcol_r:gcol_r + 1],
                GB[:, c, bcol_r:bcol_r + 1], ALU.mult, ALU.add),
                deps=[t_u2, self.t_gb], sig="ve")
            self.lnc[seg].append(t_ap)
        return t_ap

    def _hb_seg(self, seg, deps):
        p = self.p
        t_hb = None
        for c in range(DC):
            t_hb = p.op("gpsimd", lambda g, c=c, seg=seg: g.tensor_copy(
                self.HB[:, c * 1024 + seg * 512:c * 1024 + (seg + 1) * 512],
                self.H[:, c * 1024 + seg * 512:c * 1024 + (seg + 1) * 512]),
                deps=deps if c == 0 else [], sig="ge")
        return t_hb

    def _ffn1(self):
        p, d, P = self.p, self.d, self.P
        t_h = p.dma("sync", self.H.rearrange("p (c n) -> p c n", c=DC),
                    d["h0"].rearrange("(c p) n -> p c n", p=128),
                    deps=[self.t_x2a], sig="dsh")

        MOD = self.MODC.rearrange("p (c r) -> p c r", r=4)
        BPE = self.PBC[:, 0:32].rearrange("p (c l) -> p c l", l=2)
        ent_bank = [P[6], P[7]]
        path_bank = [(P[6], P[4]), (P[7], P[5])]
        ring = self.WPN + self.WPN2
        ring_free = [None, None, None, None]
        rstate = {"i": 0}
        def load_panel(hbm_slice, extra, pre_tiled=False):
            i = rstate["i"] % 4
            rstate["i"] += 1
            b = ring[i]
            src = (hbm_slice if pre_tiled else
                   hbm_slice.rearrange("(c p) n -> p c n", p=128))
            tok = p.dma("gpsimd", b.rearrange("p (c n) -> p c n", c=DC),
                        src,
                        deps=[t for t in [ring_free[i]] + extra if t],
                        sig=f"dgr{i}")
            return i, b, tok
        wfree = {}
        t_stage = [[t_h, self.t_x2a], [t_h, self.t_x2a]]
        for l in range(2):
            bdeps = [t for t in ring_free if t] + t_stage[0] + t_stage[1]
            if self.fused:
                tb = None
                for r in range(8):
                    tb = p.dma("gpsimd", self.BEFFR[0:1, r * 256:(r + 1) * 256],
                               d["beffg"][r * 2 + l:r * 2 + l + 1, :],
                               deps=(bdeps + [self.cc_bf]) if r == 0 else [],
                               sig="dgb2")
                self.t_beff = tb
            else:
                self.t_beff = p.dma("gpsimd", self.BEFFR[0:1, 0:D],
                                    d["beff"][l:l + 1, :],
                                    deps=bdeps, sig="dgb2")
            t_ent = [None, None]
            for seg in range(2):
                t_hb = self._hb_seg(seg, t_stage[seg])
                t_ev = None
                for m in range(DC):
                    if self.fused:
                        r0 = (m // 2) * 4096 + l * 2048
                        co = (m % 2) * 128
                        wsl = d["weffg"][r0:r0 + 2048, co:co + 128]
                    else:
                        wsl = d["weff"][l][:, m * 128:(m + 1) * 128]
                    ri, rb, t_w = load_panel(wsl, [t_hb])
                    ps = ent_bank[seg]
                    def mm(t, m=m, rb=rb, l=l, seg=seg, ps=ps):
                        for k in range(DC):
                            t.matmul(ps[:, :],
                                     rb[:, k * 128:(k + 1) * 128],
                                     self.HB[:, k * 1024 + seg * 512:
                                             k * 1024 + seg * 512 + 512],
                                     start=(k == 0), stop=False)
                        return t.matmul(
                            ps[:, :],
                            self.BEFFR[0:1, m * 128:(m + 1) * 128],
                            self.CONB[0:1, 512:1024], start=False, stop=True)
                    deps = [t_w, t_hb, self.t_beff, self.t_onesb]
                    if t_ev:
                        deps.append(t_ev)
                    t_mm = p.op("tensor", mm, deps=deps, sig="pe")
                    ring_free[ri] = t_mm
                    t_ev = p.op("vector", lambda v, m=m, seg=seg, ps=ps: v.tensor_add(
                        self.H[:, m * 1024 + seg * 512:m * 1024 + seg * 512 + 512],
                        self.H[:, m * 1024 + seg * 512:m * 1024 + seg * 512 + 512],
                        ps[:, :]), deps=[t_mm], sig="ve")
                t_ent[seg] = t_ev
            if self.dbg and l == 0:
                tdmp = p.dma("sync", d["d_hent0"].rearrange("(c p) n -> p c n", p=128),
                             self.H.rearrange("p (c n) -> p c n", c=DC),
                             deps=[t_ent[0], t_ent[1]], sig="dso")
                self.out_toks.append(tdmp)
                p.wait("tensor", tdmp)
            t_ln3 = [self._ln_seg(0, t_ent[0], 0 + l, 2 + l),
                     self._ln_seg(1, t_ent[1], 0 + l, 2 + l)]
            if self.dbg and l == 0:
                tdmp = p.dma("sync", d["d_hln30"].rearrange("(c p) n -> p c n", p=128),
                             self.H.rearrange("p (c n) -> p c n", c=DC),
                             deps=[t_ln3[0], t_ln3[1]], sig="dso")
                self.out_toks.append(tdmp)
                p.wait("tensor", tdmp)
            t_path = [None, None]
            for seg in range(2):
                t_hbs = self._hb_seg(seg, [t_ln3[seg]])
                t_ev = None
                for m in range(DC):
                    ra, ba, t_wa = load_panel(
                        d["pathw"][l, 0, m, 0], [t_hbs], pre_tiled=True)
                    rb_, bb, t_wb = load_panel(
                        d["pathw"][l, 1, m, 0], [t_hbs], pre_tiled=True)
                    pa, pb = path_bank[seg]
                    def mm2(t, m=m, ba=ba, bb=bb, seg=seg, pa=pa, pb=pb):
                        for k in range(DC):
                            t.matmul(pa[:, :],
                                     ba[:, k * 128:(k + 1) * 128],
                                     self.HB[:, k * 1024 + seg * 512:
                                             k * 1024 + seg * 512 + 512],
                                     start=(k == 0), stop=(k == DC - 1))
                        last = None
                        for k in range(DC):
                            last = t.matmul(
                                pb[:, :],
                                bb[:, k * 128:(k + 1) * 128],
                                self.HB[:, k * 1024 + seg * 512:
                                        k * 1024 + seg * 512 + 512],
                                start=(k == 0), stop=(k == DC - 1))
                        return last
                    deps = [t_wa, t_wb, t_hbs]
                    if t_ev:
                        deps.append(t_ev)
                    t_mm = p.op("tensor", mm2, deps=deps, sig="pe")
                    ring_free[ra] = t_mm
                    ring_free[rb_] = t_mm
                    hsl = self.H[:, m * 1024 + seg * 512:m * 1024 + (seg + 1) * 512]
                    TMa = self.TMP[0] if seg == 0 else self.TMP[2]
                    t_i0 = p.op("vector", lambda v, hsl=hsl, m=m, l=l, pa=pa,
                                TMa=TMa: v.scalar_tensor_tensor(
                                    TMa[:, :], pa[:, :],
                                    MOD[:, m, 2 * l:2 * l + 1], hsl,
                                    ALU.mult, ALU.add),
                                deps=[t_mm, self.t_mod], sig="ve")
                    t_i1 = p.op("vector", lambda v, m=m, l=l, pb=pb, TMa=TMa,
                                hsl=hsl: v.scalar_tensor_tensor(
                                    hsl, pb[:, :],
                                    MOD[:, m, 2 * l + 1:2 * l + 2], TMa[:, :],
                                    ALU.mult, ALU.add),
                                deps=[t_i0], sig="ve")
                    t_ev = p.op("vector", lambda v, hsl=hsl, m=m, l=l:
                                v.tensor_scalar_add(
                                    hsl, hsl, BPE[:, m, l:l + 1]),
                                deps=[t_i1, self.t_bpe], sig="ve")
                t_path[seg] = t_ev
            if self.dbg and l == 0:
                tdmp = p.dma("sync", d["d_hpath0"].rearrange("(c p) n -> p c n", p=128),
                             self.H.rearrange("p (c n) -> p c n", c=DC),
                             deps=[t_path[0], t_path[1]], sig="dso")
                self.out_toks.append(tdmp)
                p.wait("tensor", tdmp)
            t_stage = [[self._ln_seg(0, t_path[0], 4 + l, 6 + l)],
                       [self._ln_seg(1, t_path[1], 4 + l, 6 + l)]]
            if self.dbg and l == 0:
                tdmp = p.dma("sync", d["d_hln40"].rearrange("(c p) n -> p c n", p=128),
                             self.H.rearrange("p (c n) -> p c n", c=DC),
                             deps=[t_stage[0][0], t_stage[1][0]], sig="dso")
                self.out_toks.append(tdmp)
                p.wait("tensor", tdmp)
        t_last = None
        for c in range(DC):
            t_last = p.op("gpsimd", lambda g, c=c: g.tensor_add(
                self.X1[:, c * 1024:(c + 1) * 1024],
                self.X1[:, c * 1024:(c + 1) * 1024],
                self.H[:, c * 1024:(c + 1) * 1024]),
                deps=[self.lnc[0][c], self.lnc[1][c]] + ([t_last] if t_last else []),
                sig="ge")
            t_o = p.dma("sync", d["x2"][c * 128:(c + 1) * 128, :],
                        self.X1[:, c * 1024:(c + 1) * 1024],
                        deps=[t_last], sig="dso")
            self.out_toks.append(t_o)


from concourse.bass_utils import run_bass_kernel_spmd

_CACHE = {}
USE_FUSED = True


def build_fused():
    b1 = L1Build(with_l1=True, fused=True)
    with contextlib.ExitStack() as st:
        b1._build_in(st, finalize=False)
        p, d = b1.p, b1.d
        fl = b1.ar.view(b1.ar.region_off["SMC"], F32, 4)
        p.op("vector", lambda v: v.memset(fl[0:1, 0:1], 0.0), sig="ve")
        p.op("scalar", lambda s: s.activation(fl[0:1, 1:2], fl[0:1, 0:1],
                                              AF.Copy), sig="se")
        p.op("gpsimd", lambda g: g.memset(fl[0:1, 2:3], 0.0), sig="ge")
        p.phase_barrier()
        b1.ar.reset_zero()
        Pl = list(b1.P)
        Pl[3] = b1.P[3][:, 0:1024].bitcast(F32)
        b2 = L2Build(fused_io={
            "nc": b1.nc, "p": p,
            "x1": d["x1"], "h0": d["h0"], "q1": d["q1"],
            "kvsum": d["kvsum"], "weffg": d["weffg"], "beffg": d["beffg"],
            "cc_kv": b1.cc_kv, "cc_wf": b1.cc_wf, "cc_bf": b1.cc_bf})
        b2.ar = b1.ar
        b2.P = Pl
        return b2._build_in(finalize=True)


def _builders():
    if "b" not in _CACHE:
        b1 = L1Build(with_l1=True)
        nc1 = b1.build()
        b2 = L2Build()
        nc2 = b2.build()
        _CACHE["b"] = (nc1, nc2)
    return _CACHE["b"]


def _builder_fused():
    if "f" not in _CACHE:
        _CACHE["f"] = build_fused()
    return _CACHE["f"]


def host_inputs_fused(I):
    C = np.ascontiguousarray
    _ = I["L0_ffn_gate_w"]
    maps = host_inputs_l0(I, with_l1=True)
    pw = np.asarray(I["L1_ffn_path_w"])
    pw_t = np.stack([np.stack([tile_proj(pw[l, j], DC, DC)
                               for j in range(2)]) for l in range(2)])
    extra = {
        "Wo1": tile_proj(np.asarray(I["L1_attn_Wo"]), DC, DC),
        "bo": np.asarray(I["L1_attn_bo"]),
        "pathw": pw_t,
        "pathb": np.asarray(I["L1_ffn_path_b"]),
        "phase": np.asarray(I["L1_ffn_phase"]),
        "ln3g": np.asarray(I["L1_ffn_ln3_g"]),
        "ln3b": np.asarray(I["L1_ffn_ln3_b"]),
        "ln4g": np.asarray(I["L1_ffn_ln4_g"]),
        "ln4b": np.asarray(I["L1_ffn_ln4_b"]),
    }
    for m in maps:
        m.update(extra)
    return maps


def kernel(**inputs):
    C = np.ascontiguousarray
    if USE_FUSED:
        nc = _builder_fused()
        maps = host_inputs_fused(inputs)
        res = run_bass_kernel_spmd(nc, maps, core_ids=list(range(8)))
        out = np.empty((2, 4096, D), np.float32)
        for c in range(8):
            b, t0 = c // 4, (c % 4) * TOK
            out[b, t0:t0 + TOK] = res.results[c]["x2"].T
        return out
    nc1, nc2 = _builders()
    maps1 = host_inputs_l0(inputs, with_l1=True)
    res1 = run_bass_kernel_spmd(nc1, maps1, core_ids=list(range(8)))
    r1 = res1.results
    weff = np.concatenate([r1[c]["weffp"] for c in range(8)], axis=2)
    beff = np.concatenate([r1[c]["beffp"] for c in range(8)], axis=1)
    pw = np.asarray(inputs["L1_ffn_path_w"])
    pw_t = np.stack([np.stack([tile_proj(pw[l, j], DC, DC)
                               for j in range(2)]) for l in range(2)])
    maps2 = []
    for c in range(8):
        g = (c // 4) * 4
        maps2.append({
            "x1": r1[c]["x1"], "h0": r1[c]["h0"], "q1": r1[c]["q1"],
            "kvps": np.stack([r1[g + i]["kvp"] for i in range(4)]),
            "weff": weff, "beff": beff,
            "Wo1": tile_proj(C(np.asarray(inputs["L1_attn_Wo"])), DC, DC),
            "bo": np.asarray(inputs["L1_attn_bo"]),
            "pathw": pw_t,
            "pathb": np.asarray(inputs["L1_ffn_path_b"]),
            "phase": np.asarray(inputs["L1_ffn_phase"]),
            "ln3g": np.asarray(inputs["L1_ffn_ln3_g"]),
            "ln3b": np.asarray(inputs["L1_ffn_ln3_b"]),
            "ln4g": np.asarray(inputs["L1_ffn_ln4_g"]),
            "ln4b": np.asarray(inputs["L1_ffn_ln4_b"]),
        })
    res2 = run_bass_kernel_spmd(nc2, maps2, core_ids=list(range(8)))
    out = np.empty((2, 4096, D), np.float32)
    for c in range(8):
        b, t0 = c // 4, (c % 4) * TOK
        out[b, t0:t0 + TOK] = res2.results[c]["x2"].T
    return out
